# version 10
# speedup vs baseline: 1.3130x; 1.0057x over previous
import math
import numpy as np

import concourse.bass as bass
import concourse.mybir as mybir
from concourse.ap import AP
from concourse.bass_core_rust import IndirectOffsetOnAxis
from concourse.bass_utils import run_bass_kernel_spmd

F32 = mybir.dt.float32
BF16 = mybir.dt.bfloat16
AF = mybir.ActivationFunctionType
ALU = mybir.AluOpType
AXX = mybir.AxisListType.X

B, S, D = 2, 8192, 1024
NC = 8
CHUNK = 1024
TPC = 2 * CHUNK
NCT = D // 128

DIM_QK = 128
NUM_BUCKETS = 32
MAX_DISTANCE = 128
LAP_MU = math.sqrt(0.5)
LAP_STD = math.sqrt(0.25 * math.pi)
K1S = (1.0 / S) / (LAP_STD * math.sqrt(2.0))
KB1 = math.sqrt(DIM_QK) / (LAP_STD * math.sqrt(2.0))
KB0 = -LAP_MU / (LAP_STD * math.sqrt(2.0))
MASK_SENTINEL = -2000.0

NEXP = 8
HEXP = 512

C_REC = 8.0

HEADS = 8
DHEAD = 64
WINDOW = 512
INNER = 512
HID = 2730
HIDP = 2816
QK_SCALE = 8.0

ICH = 256


def chunks_of_core(c):
    if c <= 3:
        return [(0, c, 0), (1, 7 - c, CHUNK)]
    return [(1, 7 - c, 0), (0, c, CHUNK)]


def owner_of_chunk(b, k):
    core = k if b == 0 else 7 - k
    return (core, 0 if k <= 3 else CHUNK)



def _rel_bucket(d):
    d = np.asarray(d)
    me = NUM_BUCKETS // 2
    nf = np.maximum(d, 1).astype(np.float32)
    val = me + (np.log(nf / me) / math.log(MAX_DISTANCE / me)
                * (NUM_BUCKETS - me)).astype(np.int32)
    val = np.minimum(val, NUM_BUCKETS - 1)
    return np.where(d < me, d, val).astype(np.int64)


def _toeplitz_gather(rel_emb):
    out = []
    for D0 in (-128, 0, 128):
        pi = np.arange(128)[:, None]
        fi = np.arange(ICH)[None, :]
        d = fi - pi + D0
        t = np.where(d >= 0, rel_emb[_rel_bucket(np.maximum(d, 0)), 0],
                     np.float32(MASK_SENTINEL))
        out.append(t)
    return np.ascontiguousarray(np.stack(out), np.float32)


def _xpos_tables():
    half = np.arange(0, DHEAD, 2, dtype=np.float64)
    inv_freq = 1.0 / 10000.0 ** (half / DHEAD)
    t = np.arange(S, dtype=np.float64)
    freqs = t[:, None] * inv_freq[None, :]
    freqs = np.concatenate([freqs, freqs], -1)
    base = (half + 0.4 * DHEAD) / (1.4 * DHEAD)
    power = (t - S // 2) / float(WINDOW // 2)
    scale = base[None, :] ** power[:, None]
    scale = np.concatenate([scale, scale], -1)
    cos, sin = np.cos(freqs), np.sin(freqs)
    r = (cos * scale, sin * scale, cos / scale, sin / scale)
    return [np.ascontiguousarray(x, np.float32) for x in r]


def _l1ffn_masks():
    out = np.zeros((8, 128, WINDOW), np.float32)
    di = np.arange(128)[:, None]
    dj = np.arange(WINDOW)[None, :]
    for oi, o in enumerate((0, 128, 256, 384)):
        out[oi * 2 + 0] = np.where(dj >= o + di, 0.0, -1e9)
        out[oi * 2 + 1] = np.where(dj <= o + di, 0.0, -1e9)
    return out


def prep_inputs(inputs):
    x = np.asarray(inputs["L0_attn_x"], dtype=np.float32)
    cs_q, sn_q, cs_k, sn_k = _xpos_tables()
    f32 = lambda a: np.ascontiguousarray(a, np.float32)
    shared = {
        "Wqk": inputs["L0_attn_Wqk"],
        "bqk": np.reshape(inputs["L0_attn_bqk"], (128, 1)),
        "gamma": inputs["L0_attn_gamma"],
        "beta": inputs["L0_attn_beta"],
        "Wv": inputs["L0_attn_Wv"],
        "bv": np.reshape(inputs["L0_attn_bv"], (1, 1024)),
        "toe": _toeplitz_gather(np.asarray(inputs["L0_attn_rel_emb"],
                                           np.float32)),
        "gate_W": inputs["L0_ffn_gate_W"],
        "W1": inputs["L0_ffn_W1"],
        "b1": inputs["L0_ffn_b1"],
        "W2": inputs["L0_ffn_W2"],
        "b2": inputs["L0_ffn_b2"],
        "Ws1": inputs["L0_ffn_Ws1"],
        "bs1": np.reshape(inputs["L0_ffn_bs1"], (1, 512)),
        "Ws2": inputs["L0_ffn_Ws2"],
        "bs2": np.reshape(inputs["L0_ffn_bs2"], (1024, 1)),
        "route_scale": np.reshape(inputs["L0_ffn_route_scale"], (1, 1)),
        "WinT": np.asarray(inputs["L1_attn_W_in"]).T,
        "WigT": np.asarray(inputs["L1_attn_W_ig"]).T,
        "b_ig": np.reshape(inputs["L1_attn_b_ig"], (NCT, 128)).T,
        "WrgT": np.asarray(inputs["L1_attn_W_rg"]).T,
        "b_rg": np.reshape(inputs["L1_attn_b_rg"], (NCT, 128)).T,
        "lam": np.reshape(inputs["L1_attn_lam"], (NCT, 128)).T,
        "WoutT": np.asarray(inputs["L1_attn_W_out"]).T,
        "h0": np.reshape(inputs["L1_attn_h0"], (NCT, 128)).T,
        "norm_w": np.reshape(inputs["L1_attn_norm_w"], (NCT, 128)).T,
        "wqkv": inputs["L1_ffn_wqkv"],
        "qs_t": np.tile(np.asarray(inputs["L1_ffn_q_scale"]), (1, HEADS)
                        ).reshape(2, 1, INNER),
        "ks_t": np.tile(np.asarray(inputs["L1_ffn_k_scale"]), (1, HEADS)
                        ).reshape(2, 1, INNER),
        "wo": inputs["L1_ffn_wo"],
        "ln_g": np.transpose(np.reshape(inputs["L1_ffn_ln_g"],
                                        (2, NCT, 128)), (0, 2, 1)),
        "ln_b": np.transpose(np.reshape(inputs["L1_ffn_ln_b"],
                                        (2, NCT, 128)), (0, 2, 1)),
        "b2f": np.transpose(np.reshape(inputs["L1_ffn_b2"],
                                       (2, NCT, 128)), (0, 2, 1)),
        "masks": _l1ffn_masks(),
        "ident": np.eye(128, dtype=np.float32),
        "cs_q": cs_q, "sn_q": sn_q, "cs_k": cs_k, "sn_k": sn_k,
    }
    w1 = np.asarray(inputs["L1_ffn_w1"], np.float32)
    b1 = np.asarray(inputs["L1_ffn_b1"], np.float32)
    w1p = np.zeros((2, 1024, 2 * HIDP), np.float32)
    b1p = np.zeros((2, 2 * HIDP), np.float32)
    w1p[:, :, :HID] = w1[:, :, :HID]
    w1p[:, :, HIDP:HIDP + HID] = w1[:, :, HID:]
    b1p[:, :HID] = b1[:, :HID]
    b1p[:, HIDP:HIDP + HID] = b1[:, HID:]
    w2p = np.zeros((2, HIDP, 1024), np.float32)
    w2p[:, :HID] = np.asarray(inputs["L1_ffn_w2"], np.float32)
    shared["w1p"] = w1p
    shared["b1p"] = np.transpose(b1p.reshape(2, 2 * HIDP // 128, 128),
                                 (0, 2, 1))
    shared["w2p"] = w2p
    shared = {k: f32(v) for k, v in shared.items()}

    per_core = []
    for c in range(NC):
        cols = [x[b, k * CHUNK:(k + 1) * CHUNK, :].T
                for (b, k, _o) in chunks_of_core(c)]
        m = dict(shared)
        m["x_cm"] = f32(np.concatenate(cols, axis=1))
        per_core.append(m)
    return per_core



class Sem:
    __slots__ = ("h", "n")

    def __init__(self, h):
        self.h = h
        self.n = 0


CONSTS = [0.0, 1.0, -1.0, 0.5, -0.5, 1e-5, 1e-6, 1e-8, 1.0 / 1024.0,
          K1S, KB1, KB0, 1.000001, QK_SCALE, 1e-24, -1e9, 1e9, 2.0,
          1.0 / 512.0, -1e4]


class Prog:
    def __init__(self, partition_id=False):
        self.nc = bass.Bass("TRN2", target_bir_lowering=False, debug=False,
                            enable_partition_id=partition_id)
        self._sems = {}
        self._block_cm = None
        self.b = None
        self.rec = False
        self._treg = {}
        self.in_switch = False
        self._pid = {}

    def sem(self, name):
        if name not in self._sems:
            self._sems[name] = Sem(self.nc.alloc_semaphore(name))
        return self._sems[name]

    def open(self, want_pid=False):
        self._block_cm = self.nc.Block()
        self.b = self._block_cm.__enter__()
        cp = self.constpool
        cinit = self.sem("cinit")

        def _(v):
            for i, val in enumerate(CONSTS):
                ins = v.memset(cp[:, i:i + 1], val)
            ins.then_inc(cinit.h, 1)
        self.b.vector(_)
        cinit.n = 1
        for e in ("sync", "gpsimd", "scalar", "tensor"):
            self.wait(e, cinit)
        if want_pid:
            for e in ("sync", "gpsimd", "vector", "scalar", "tensor"):
                def f(eng, e=e):
                    self._pid[e] = eng.partition_id()
                getattr(self.b, e)(f)

    def close(self):
        self._block_cm.__exit__(None, None, None)
        return self.nc

    _ENG = {"sync": "sync", "gpsimd": "gpsimd", "vector": "vector",
            "scalar": "scalar", "tensor": "tensor"}

    def _emit(self, eng, f):
        if self.rec:
            return
        if self.in_switch:
            if eng == self._switch_eng:
                f(getattr(self.nc, eng))
        else:
            getattr(self.b, eng)(f)

    def t(self, f):
        self._emit("tensor", f)

    def v(self, f):
        self._emit("vector", f)

    def s(self, f):
        self._emit("scalar", f)

    def g(self, f):
        self._emit("gpsimd", f)

    def y(self, f):
        self._emit("sync", f)

    def wait(self, eng, sem, val=None):
        val = sem.n if val is None else val
        self._emit(eng, lambda e: e.wait_ge(sem.h, val))

    def lag_wait(self, eng, sem, lag):
        tgt = sem.n - lag
        if tgt > 0:
            self._emit(eng, lambda e: e.wait_ge(sem.h, tgt))

    def di(self, eng, sem, k=1):
        sem.n += k
        self._emit(eng, lambda e: e.drain().then_inc(sem.h, k))

    def tdma(self, eng, out, ten, dims, tabsl, sem, slow_ok=False):
        sem.n += 16

        def f(e, out=out, ten=ten, dims=dims, tabsl=tabsl, slow_ok=slow_ok,
              eng=eng):
            if eng not in self._treg:
                g = e.register("tdma_off_" + eng)
                self._treg[eng] = (g, g.__enter__())
            reg = self._treg[eng][1]
            e.reg_load(reg, tabsl)
            e.dma_start(out=out, in_=AP(ten, reg, [list(d) for d in dims]),
                        allow_slow_non_contiguous=slow_ok).then_inc(sem.h, 16)
        self._emit(eng, f)

    def gdma(self, out, in_dram, idx_col, sem):
        sem.n += 16
        self._emit("gpsimd",
                   lambda e: e.indirect_dma_start(
                       out, None, in_dram,
                       IndirectOffsetOnAxis(idx_col, 0)).then_inc(sem.h, 16))

    def dma(self, eng, out, in_, sem, slow_ok=False):
        sem.n += 16
        self._emit(eng,
                   lambda e: e.dma_start(out=out, in_=in_,
                                         allow_slow_non_contiguous=slow_ok
                                         ).then_inc(sem.h, 16))

    def barrier(self, engines=("sync", "gpsimd", "vector", "scalar",
                               "tensor")):
        s = self.sem("phb")
        s.n += len(engines)
        tgt = s.n
        for e in engines:
            def f(eng, s=s, tgt=tgt, e=e):
                if e in ("vector", "scalar"):
                    eng.drain().then_inc(s.h, 1)
                else:
                    eng.nop().then_inc(s.h, 1)
                eng.wait_ge(s.h, tgt)
            self._emit(e, f)

    def switch8(self, body_fn):
        assert not self.rec and not self.in_switch
        base = {k: s.n for k, s in self._sems.items()}

        def reset():
            for k, s in self._sems.items():
                s.n = base.get(k, 0)
            for k, s in self._sems.items():
                if k not in base:
                    s.n = 0

        deltas = []
        for core in range(NC):
            self.rec = True
            body_fn(core)
            self.rec = False
            deltas.append({k: s.n - base.get(k, 0)
                           for k, s in self._sems.items()})
            reset()
        quota = {}
        for d in deltas:
            for k, v in d.items():
                quota[k] = max(quota.get(k, 0), v)
        self.in_switch = True
        for e in ("sync", "gpsimd", "vector", "scalar", "tensor"):
            self._switch_eng = e
            eh = getattr(self.nc, e)
            for core in self.nc.Switch(engines=[eh], index=[self._pid[e]],
                                       n=NC):
                reset()
                body_fn(core)
                if e == "sync":
                    pads = {k: quota[k] - (self._sems[k].n - base.get(k, 0))
                            for k in quota}
                    for k, v in pads.items():
                        if v > 0:
                            eh.sem_inc(self._sems[k].h, v)
            reset()
        self.in_switch = False
        self._switch_eng = None
        for k, v in quota.items():
            self._sems[k].n = base.get(k, 0) + v


class Ctx:
    pass


def setup_common(p: Prog, io, big16_cols=16 * 1024, xn_cols=NCT * TPC, rows_cols=2560, scr_cols=3072, stc_cols=512, itab_cols=1040):
    nc = p.nc
    cx = Ctx()
    A = nc.alloc_sbuf_tensor
    p.constpool = A("constpool", [128, len(CONSTS)], F32)
    for i, val in enumerate(CONSTS):
        nc.const_aps.register(F32, float(val), p.constpool[:, i:i + 1])
    cx.xres = A("xres", [128, NCT * TPC], F32)
    cx.xn = A("xn", [128, xn_cols], BF16)
    cx.big16 = A("big16", [128, big16_cols], BF16)
    cx.wA = A("wA", [128, 12288], BF16)
    cx.wB = A("wB", [128, 4096], BF16)
    cx.scr = A("scr", [128, scr_cols], F32)
    cx.stc = A("stc", [128, stc_cols], F32)
    cx.rows = A("rows", [1, rows_cols], F32)
    cx.rrow = A("rrow", [1, 1024], F32)
    cx.ones_col = A("ones_col", [128, 1], F32)
    cx.ones_row = A("ones_row", [1, 128], F32)
    cx.ones_col_b = A("ones_col_b", [128, 1], BF16)
    cx.ones_row_b = A("ones_row_b", [1, 128], BF16)
    cx.identb = A("identb", [128, 128], BF16)
    cx.itab = A("itab", [1, itab_cols], mybir.dt.int32)
    cx.identf = A("identf", [128, 128], F32)
    cx.pA = nc.alloc_psum_tensor("pA", [128, 2048], F32)
    cx.pB = nc.alloc_psum_tensor("pB", [128, 1536], F32)
    cx.tp = nc.alloc_psum_tensor("tp", [128, 1024], BF16)
    cx.io = io
    return cx


def ph_init(p, cx):
    ld = p.sem("ld_init")
    p.dma("gpsimd", cx.identb[:, :], cx.io["ident"], ld)
    p.dma("sync", cx.identf[:, :], cx.io["ident"], ld)

    def _(v):
        v.memset(cx.ones_col[:, :], 1.0)
        v.memset(cx.ones_row[:, :], 1.0)
        v.memset(cx.ones_col_b[:, :], 1.0)
        v.memset(cx.ones_row_b[:, :], 1.0)
    p.v(_)
    p.wait("tensor", ld)
    p.barrier()


def ph_load_x(p, cx):
    ld = p.sem("ld_x")
    for ct in range(NCT):
        p.dma("sync", cx.xres[:, ct * TPC:(ct + 1) * TPC],
              cx.io["x_cm"][ct * 128:(ct + 1) * 128, :], ld)
    for e in ("vector", "scalar", "tensor"):
        p.wait(e, ld)
    p.barrier()


def ph_layernorm(p, cx, tag):
    mm = p.sem("mm_" + tag)
    ev = p.sem("ev_" + tag)
    for tch in range(4):
        t0 = tch * 512
        for ct in range(NCT):
            xt = cx.xres[:, ct * TPC + t0: ct * TPC + t0 + 512]
            xbt = cx.big16[:, ct * 512:(ct + 1) * 512]
            sqt = cx.wA[:, ct * 512:(ct + 1) * 512]
            last = ct == NCT - 1

            def fv(v, xt=xt, xbt=xbt):
                v.tensor_copy(xbt, xt)

            def fs(s_, xt=xt, sqt=sqt):
                s_.square(sqt, xt)
            p.v(fv)
            p.s(fs)
        p.di("vector", ev)
        p.di("scalar", ev)
        p.wait("tensor", ev)
        for ct in range(NCT):
            xbt = cx.big16[:, ct * 512:(ct + 1) * 512]
            sqt = cx.wA[:, ct * 512:(ct + 1) * 512]
            first, last = ct == 0, ct == NCT - 1

            def ft(t, xbt=xbt, sqt=sqt, first=first, last=last):
                t.matmul(cx.pB[0:1, 0:512], cx.ones_col_b[:, :], xbt,
                         start=first, stop=last)
                i = t.matmul(cx.pB[0:1, 512:1024], cx.ones_col_b[:, :], sqt,
                             start=first, stop=last)
                if last:
                    i.then_inc(mm.h, 1)
            p.t(ft)
        mm.n += 1
        p.wait("vector", mm)
        mu, msq, var = (cx.rows[:, 0:512], cx.rows[:, 512:1024],
                        cx.rows[:, 1024:1536])
        Arow, Brow = cx.rows[:, 1536:2048], cx.rows[:, 2048:2560]

        def f3(v):
            v.tensor_scalar_mul(mu, cx.pB[0:1, 0:512], 1.0 / 1024.0)
            v.tensor_mul(msq, mu, mu)
            v.tensor_scalar_mul(var, cx.pB[0:1, 512:1024], 1.0 / 1024.0)
            v.tensor_sub(var, var, msq)
        p.v(f3)
        p.di("vector", ev)
        p.wait("scalar", ev)

        def f4(s_):
            s_.activation(var, var, AF.Ln, bias=1e-5, scale=1.0)
        p.s(f4)
        p.di("scalar", ev)
        p.wait("scalar", ev)

        def f4b(s_):
            s_.activation(var, var, AF.Exp, bias=0.0, scale=-0.5)
        p.s(f4b)
        p.di("scalar", ev)
        p.wait("vector", ev)

        def f5(v):
            v.tensor_copy(Arow, var)
            v.tensor_mul(Brow, mu, var)
            v.tensor_scalar_mul(Brow, Brow, -1.0)
        p.v(f5)
        p.di("vector", ev)
        p.wait("tensor", ev)

        def f6(t):
            t.matmul(cx.pB[:, 0:512], cx.ones_row[:, :], Arow,
                     start=True, stop=True)
            t.matmul(cx.pB[:, 512:1024], cx.ones_row[:, :], Brow,
                     start=True, stop=True).then_inc(mm.h, 1)
        p.t(f6)
        mm.n += 1
        p.wait("vector", mm)
        for ct in range(NCT):
            xt = cx.xres[:, ct * TPC + t0: ct * TPC + t0 + 512]
            xnt = cx.xn[:, ct * TPC + t0: ct * TPC + t0 + 512]
            tmp = cx.scr[:, 0:512]
            last = ct == NCT - 1

            def f7(v, xt=xt, xnt=xnt, tmp=tmp):
                v.tensor_mul(tmp, xt, cx.pB[:, 0:512])
                v.tensor_add(xnt, tmp, cx.pB[:, 512:1024])
            p.v(f7)
        p.di("vector", ev)
        p.wait("tensor", ev)
    p.barrier()


def ph_l0_qkv(p, cx):
    nc = p.nc
    A = nc.alloc_sbuf_tensor
    if getattr(cx, "l0q", None) is None:
        cx.l0q = A("l0q", [128, TPC], BF16)
    if getattr(cx, "l0k", None) is None:
        cx.l0k = A("l0k", [128, TPC], BF16)
    cx.vtm = cx.big16
    ld = p.sem("ld_qkv")
    mm = p.sem("mm_qkv")
    ev = p.sem("ev_qkv")
    for kt in range(NCT):
        p.dma("gpsimd", cx.wB[:, kt * 128:(kt + 1) * 128],
              cx.io["Wqk"][kt * 128:(kt + 1) * 128, :], ld)
    bqk = cx.stc[:, 0:1]
    gam0, bet0 = cx.stc[:, 1:2], cx.stc[:, 2:3]
    gam1, bet1 = cx.stc[:, 3:4], cx.stc[:, 4:5]
    p.dma("sync", bqk, cx.io["bqk"], ld)
    p.dma("sync", gam0, cx.io["gamma"][0:1, :].rearrange("a b -> b a"), ld)
    p.dma("sync", gam1, cx.io["gamma"][1:2, :].rearrange("a b -> b a"), ld)
    p.dma("sync", bet0, cx.io["beta"][0:1, :].rearrange("a b -> b a"), ld)
    p.dma("sync", bet1, cx.io["beta"][1:2, :].rearrange("a b -> b a"), ld)
    for e in ("tensor", "scalar", "vector"):
        p.wait(e, ld)
    ev_si = p.sem("ev_qk_si")
    ev_af = p.sem("ev_qk_af")
    for tch in range(4):
        t0 = tch * 512
        dst = cx.pB[:, (tch % 2) * 512:(tch % 2) * 512 + 512]
        p.lag_wait("tensor", ev_si, 1)

        def ft(t, dst=dst, t0=t0):
            for kt in range(NCT):
                i = t.matmul(dst, cx.wB[:, kt * 128:(kt + 1) * 128],
                             cx.xn[:, kt * TPC + t0: kt * TPC + t0 + 512],
                             start=(kt == 0), stop=(kt == NCT - 1))
            i.then_inc(mm.h, 1)
        p.t(ft)
        mm.n += 1
        p.wait("scalar", mm)
        p.lag_wait("scalar", ev_af, 1)
        pre = cx.scr[:, (tch % 2) * 512:(tch % 2) * 512 + 512]

        def fs(s_, dst=dst, pre=pre):
            s_.activation(pre, dst, AF.Silu, bias=bqk, scale=1.0)
        p.s(fs)
        p.di("scalar", ev_si)
        p.wait("vector", ev_si)

        def fv(v, pre=pre, t0=t0):
            v.tensor_scalar(cx.l0q[:, t0:t0 + 512], pre, gam0, bet0,
                            ALU.mult, ALU.add)
            v.tensor_scalar(cx.l0k[:, t0:t0 + 512], pre, gam1, bet1,
                            ALU.mult, ALU.add)
        p.v(fv)
        p.di("vector", ev_af)
    p.barrier()

    for kt in range(NCT):
        p.dma("gpsimd", cx.wA[:, kt * 1024:(kt + 1) * 1024],
              cx.io["Wv"][kt * 128:(kt + 1) * 128, :], ld)
    bvrow = cx.rrow[:, 0:1024]
    p.dma("sync", bvrow, cx.io["bv"], ld)
    for e in ("tensor", "vector"):
        p.wait(e, ld)
    bvb = cx.scr[:, 1024:2048]

    def ftb(t):
        t.matmul(cx.pB[:, 0:512], cx.ones_row[:, :], bvrow[:, 0:512],
                 start=True, stop=True)
        t.matmul(cx.pB[:, 512:1024], cx.ones_row[:, :], bvrow[:, 512:1024],
                 start=True, stop=True).then_inc(mm.h, 1)
    p.t(ftb)
    mm.n += 1
    p.wait("vector", mm)

    def fvb(v):
        v.tensor_copy(bvb[:, 0:512], cx.pB[:, 0:512])
        v.tensor_copy(bvb[:, 512:1024], cx.pB[:, 512:1024])
    p.v(fvb)
    p.di("vector", ev)
    p.wait("tensor", ev)
    ev_ad = p.sem("ev_v_ad")
    ev_vs = p.sem("ev_v_si")
    for it in range(32):
        tt, nh = it // 2, it % 2
        dst = cx.pA[:, (it % 2) * 512:(it % 2) * 512 + 512]
        p.lag_wait("tensor", ev_ad, 1)

        def ft(t, dst=dst, tt=tt, nh=nh):
            for ct in range(NCT):
                i = t.matmul(
                    dst,
                    cx.xn[:, ct * TPC + tt * 128: ct * TPC + (tt + 1) * 128],
                    cx.wA[:, ct * 1024 + nh * 512: ct * 1024 + nh * 512 + 512],
                    start=(ct == 0), stop=(ct == NCT - 1))
            i.then_inc(mm.h, 1)
        p.t(ft)
        mm.n += 1
        p.wait("vector", mm)
        p.lag_wait("vector", ev_vs, 1)
        tmp = cx.scr[:, (it % 2) * 512:(it % 2) * 512 + 512]

        def fv(v, dst=dst, tmp=tmp, nh=nh):
            v.tensor_add(tmp, dst, bvb[:, nh * 512:nh * 512 + 512])
        p.v(fv)
        p.di("vector", ev_ad)
        p.wait("scalar", ev_ad)
        vdst = cx.vtm[:, tt * 1024 + nh * 512: tt * 1024 + nh * 512 + 512]

        def fs(s_, tmp=tmp, vdst=vdst):
            s_.activation(vdst, tmp, AF.Silu, bias=0.0, scale=1.0)
        p.s(fs)
        p.di("scalar", ev_vs)
    p.barrier()


def ph_l0_gates(p, cx):
    nc = p.nc
    if getattr(cx, "gcm", None) is None:
        cx.gcm = nc.alloc_sbuf_tensor("gcm", [8, TPC], BF16)
    ld = p.sem("ld_g")
    mm = p.sem("mm_g")
    ev = p.sem("ev_g")
    for ct in range(NCT):
        p.dma("gpsimd", cx.wB[:, 2048 + ct * 8: 2048 + ct * 8 + 8],
              cx.io["gate_W"][ct * 128:(ct + 1) * 128, :], ld)
    rs_row = cx.rrow[:, 0:1]
    p.dma("sync", rs_row, cx.io["route_scale"], ld)
    p.wait("tensor", ld)

    def ft(t):
        t.matmul(cx.pB[:, 1024:1025], cx.ones_row[:, :], rs_row,
                 start=True, stop=True).then_inc(mm.h, 1)
    p.t(ft)
    mm.n += 1
    p.wait("vector", mm)
    rs_col = cx.stc[:, 5:6]

    def fv0(v):
        v.tensor_copy(rs_col, cx.pB[:, 1024:1025])
    p.v(fv0)
    p.di("vector", ev)
    p.wait("tensor", ev)
    for tt in range(16):
        for ct in range(NCT):
            first, last = ct == 0, ct == NCT - 1

            def ft(t, tt=tt, ct=ct, first=first, last=last):
                i = t.matmul(
                    cx.pB[:, 1024 + tt * 8: 1024 + tt * 8 + 8],
                    cx.xn[:, ct * TPC + tt * 128: ct * TPC + (tt + 1) * 128],
                    cx.wB[:, 2048 + ct * 8: 2048 + ct * 8 + 8],
                    start=first, stop=last)
                if last and tt == 15:
                    i.then_inc(mm.h, 1)
            p.t(ft)
    mm.n += 1
    p.wait("scalar", mm)
    sgm = cx.stc[:, 16:144]

    def fs(s_):
        s_.activation(sgm, cx.pB[:, 1024:1152], AF.Sigmoid, bias=0.0,
                      scale=1.0)
    p.s(fs)
    p.di("scalar", ev)
    p.wait("vector", ev)
    m1 = cx.stc[:, 144:160]
    m2 = cx.stc[:, 160:176]
    rec = cx.stc[:, 176:192]
    den = cx.stc[:, 448:464]
    ind = cx.stc[:, 192:320]
    gates = cx.stc[:, 320:448]
    sg3 = sgm.rearrange("p (t e) -> p t e", e=8)
    ind3 = ind.rearrange("p (t e) -> p t e", e=8)
    g3 = gates.rearrange("p (t e) -> p t e", e=8)

    def fv2(v):
        v.tensor_scalar_mul(sgm, sgm, rs_col)
        v.drain()
        v.tensor_reduce(m1, sg3, AXX, ALU.max)
        v.drain()
        m1b = m1.rearrange("p (t o) -> p t o", o=1).broadcast_to([128, 16, 8])
        v.tensor_tensor(ind3, sg3, m1b, ALU.is_ge)
        v.drain()
        v.scalar_tensor_tensor(ind3, ind3, 1e9, sg3, ALU.mult, ALU.subtract)
        v.drain()
        v.tensor_reduce(m2, ind3, AXX, ALU.min, negate=True)
        v.drain()
        v.tensor_add(den, m1, m2)
        v.drain()
        v.reciprocal(rec, den)
        v.drain()
        m2b = m2.rearrange("p (t o) -> p t o", o=1).broadcast_to([128, 16, 8])
        v.tensor_tensor(ind3, sg3, m2b, ALU.is_ge)
        v.drain()
        v.tensor_mul(g3, sg3, ind3)
        v.drain()
        recb = rec.rearrange("p (t o) -> p t o", o=1).broadcast_to([128, 16, 8])
        v.tensor_tensor(g3, g3, recb, ALU.mult)
    p.v(fv2)
    p.di("vector", ev)
    p.wait("tensor", ev)
    for tt in range(16):
        def ft(t, tt=tt):
            t.transpose(cx.pB[0:8, 1280 + (tt % 2) * 128:
                              1280 + (tt % 2) * 128 + 128],
                        gates[:, tt * 8: tt * 8 + 8],
                        cx.identf[:, :]).then_inc(mm.h, 1)
        p.t(ft)
        mm.n += 1
        p.wait("vector", mm)

        def fv3(v, tt=tt):
            v.tensor_copy(cx.gcm[0:8, tt * 128:(tt + 1) * 128],
                          cx.pB[0:8, 1280 + (tt % 2) * 128:
                                1280 + (tt % 2) * 128 + 128])
        p.v(fv3)
        p.di("vector", ev)
        p.wait("tensor", ev)
    p.barrier()


def ph_l0_moe(p, cx):
    ld = p.sem("ld_moe")
    ldg = p.sem("ldg_moe")
    mm = p.sem("mm_moe")
    evh = p.sem("evh_moe")
    evy = p.sem("evy_moe")
    hsem = p.sem("h_moe")
    grow = cx.rrow[:, 0:512]
    for e in range(NEXP + 1):
        shared_e = e == NEXP
        if shared_e:
            w1src, w2src = cx.io["Ws1"], cx.io["Ws2"]
        else:
            w1src, w2src = cx.io["W1"][e], cx.io["W2"][e]
        p.barrier(("gpsimd", "tensor", "sync", "scalar"))
        for kt in range(NCT):
            p.dma("gpsimd", cx.wA[:, kt * 512:(kt + 1) * 512],
                  w1src[kt * 128:(kt + 1) * 128, :], ld)
        for kt in range(4):
            p.dma("gpsimd",
                  cx.wA[:, 4096 + kt * 1024: 4096 + (kt + 1) * 1024],
                  w2src[kt * 128:(kt + 1) * 128, :], ld)
        for m in range(4):
            src = (cx.io["bs1"][:, m * 128:(m + 1) * 128] if shared_e
                   else cx.io["b1"][e:e + 1, m * 128:(m + 1) * 128])
            p.dma("sync", cx.stc[:, 6 + m:7 + m],
                  src.rearrange("a b -> b a"), ld)
        if shared_e:
            for m in range(NEXP):
                p.dma("sync", cx.stc[:, 10 + m:11 + m],
                      cx.io["bs2"][m * 128:(m + 1) * 128, :], ld)
            p.dma("gpsimd", cx.wB[0:8, 3072:4096], cx.io["b2"], ld)
        p.wait("tensor", ld)
        p.wait("scalar", ld)
        for tc in range(4):
            t0 = tc * 512
            if not shared_e:
                p.wait("gpsimd", mm)
                p.dma("gpsimd", grow, cx.gcm[e:e + 1, t0:t0 + 512], ldg)
                p.wait("tensor", ldg)
                p.lag_wait("tensor", hsem, 0)

                def ftg(t):
                    t.matmul(cx.pB[:, 1024:1536], cx.ones_row[:, :], grow,
                             start=True, stop=True).then_inc(mm.h, 1)
                p.t(ftg)
                mm.n += 1
            for m in range(4):
                dsth = cx.pB[:, (m % 2) * 512:(m % 2) * 512 + 512]
                p.lag_wait("tensor", evh, 1)

                def ft(t, dsth=dsth, t0=t0, m=m):
                    for kt in range(NCT):
                        i = t.matmul(
                            dsth, cx.wA[:, kt * 512 + m * 128:
                                        kt * 512 + m * 128 + 128],
                            cx.xn[:, kt * TPC + t0: kt * TPC + t0 + 512],
                            start=(kt == 0), stop=(kt == NCT - 1))
                    i.then_inc(mm.h, 1)
                p.t(ft)
                mm.n += 1
                p.wait("scalar", mm)
                hdst = cx.big16[:, m * 512: m * 512 + 512]

                def fs(s_, dsth=dsth, hdst=hdst, m=m):
                    s_.activation(hdst, dsth, AF.Gelu,
                                  bias=cx.stc[:, 6 + m:7 + m],
                                  scale=1.0)
                p.s(fs)
                p.di("scalar", evh)
            if not shared_e:
                p.wait("vector", mm)
                p.wait("vector", evh)
                for m in range(4):
                    hdst = cx.big16[:, m * 512: m * 512 + 512]

                    def fvg(v, hdst=hdst):
                        v.tensor_tensor(hdst, hdst, cx.pB[:, 1024:1536],
                                        ALU.mult)
                    p.v(fvg)
                p.di("vector", hsem)
                p.wait("tensor", hsem)
            else:
                p.wait("tensor", evh)
            for m in range(NEXP):
                dsty = cx.pA[:, (m % 4) * 512:(m % 4) * 512 + 512]
                p.lag_wait("tensor", evy, 3)

                def ft2(t, dsty=dsty, m=m, tc=tc, shared_e=shared_e):
                    ops = []
                    for kt in range(4):
                        ops.append((cx.wA[:, 4096 + kt * 1024 + m * 128:
                                          4096 + kt * 1024 + m * 128 + 128],
                                    cx.big16[:, kt * 512: kt * 512 + 512]))
                    if shared_e:
                        ops.append((cx.wB[0:8, 3072 + m * 128:
                                          3072 + m * 128 + 128],
                                    cx.gcm[0:8, tc * 512:tc * 512 + 512]))
                    for j, (l_, r_) in enumerate(ops):
                        i = t.matmul(dsty, l_, r_, start=(j == 0),
                                     stop=(j == len(ops) - 1))
                    i.then_inc(mm.h, 1)
                p.t(ft2)
                mm.n += 1
                p.wait("vector", mm)
                xdst = cx.xres[:, m * TPC + tc * 512: m * TPC + tc * 512 + 512]

                def fv2(v, dsty=dsty, xdst=xdst, shared_e=shared_e, m=m):
                    if shared_e:
                        v.scalar_tensor_tensor(
                            xdst, dsty, cx.stc[:, 10 + m:11 + m], xdst,
                            ALU.add, ALU.add).then_inc(evy.h, 1)
                    else:
                        v.tensor_add(xdst, xdst, dsty).then_inc(evy.h, 1)
                p.v(fv2)
                evy.n += 1
            p.wait("tensor", evy)
    p.barrier()



def _declare_io(nc, arrays, outs):
    import ml_dtypes
    io = {}
    for name, arr in arrays.items():
        if arr.dtype == ml_dtypes.bfloat16:
            dt = BF16
        elif arr.dtype == np.int32:
            dt = mybir.dt.int32
        else:
            dt = F32
        io[name] = nc.dram_tensor(name, list(arr.shape), dt,
                                  kind="ExternalInput").ap()
    for name, shape, dt in outs:
        io[name] = nc.dram_tensor(name, list(shape), dt,
                                  kind="ExternalOutput").ap()
    return io


def st_store(p, cx, name_ap, src, sem):
    p.dma("sync", name_ap, src, sem)


def build_launch1(core_in):
    p = Prog()
    nc = p.nc
    outs = [("K_OWN", [128, TPC], BF16), ("Q_OWN", [128, TPC], BF16),
            ("V_OWN", [16 * 128, 1024], BF16), ("X1A", [1024, TPC], F32),
            ("XN0", [1024, TPC], BF16), ("GCM_D", [8, TPC], BF16),
            ("STC_D", [128, 512], F32)]
    io = _declare_io(nc, core_in, outs)
    cx = setup_common(p, io, itab_cols=8)
    p.open()
    ph_init(p, cx)
    ph_load_x(p, cx)
    ph_layernorm(p, cx, "ln0")
    ph_l0_qkv(p, cx)
    sts = p.sem("st1")
    p.dma("sync", io["K_OWN"], cx.l0k[:, :], sts)
    p.dma("sync", io["Q_OWN"], cx.l0q[:, :], sts)
    for tt in range(16):
        p.dma("sync", io["V_OWN"][tt * 128:(tt + 1) * 128, :],
              cx.vtm[:, tt * 1024:(tt + 1) * 1024], sts)
    for ct in range(NCT):
        p.dma("sync", io["XN0"][ct * 128:(ct + 1) * 128, :],
              cx.xn[:, ct * TPC:(ct + 1) * TPC], sts)
    p.wait("sync", sts)
    p.barrier()
    ph_l0_gates(p, cx)
    p.dma("sync", io["GCM_D"], cx.gcm[:, :], sts)
    p.dma("sync", io["STC_D"], cx.stc[:, :], sts)
    p.wait("sync", sts)
    ph_l0_moe(p, cx)
    for ct in range(NCT):
        p.dma("sync", io["X1A"][ct * 128:(ct + 1) * 128, :],
              cx.xres[:, ct * TPC:(ct + 1) * TPC], sts)
    p.wait("sync", sts)
    p.barrier()
    return p.close()


JT_U = [[8 * 3 + 2 * i + 2 for i in range(4)],
        [8 * 7 + 2 * i + 2 for i in range(4)]]
KCOLS = [4096, 8192]


def l0_gtab(core):
    g = np.zeros((128, 732), np.int64)
    pp = np.arange(128)
    col_v, col_b = 12, 372
    for s, (b, ktrue, coff) in enumerate(chunks_of_core(core)):
        base = 0 if s == 0 else 4
        for jc in range(KCOLS[s] // 1024):
            if jc <= ktrue:
                r, off = owner_of_chunk(b, jc)
                g[:, base + jc] = (r * 128 + pp) * 2 + (off // 1024)
            else:
                g[:, base + jc] = (1024 + pp) * 2
        for ich in range(4):
            jt_true = 8 * ktrue + 2 * ich + 2
            i0g = ktrue * 1024 + ich * 256
            for j in range(JT_U[s][ich]):
                if j < jt_true:
                    jc, tt = j // 8, j % 8
                    r, off = owner_of_chunk(b, jc)
                    g[:, col_v] = r * 2048 + off + tt * 128 + pp
                    D0 = i0g - j * 128
                    bi = (D0 // 128 + 1) if D0 in (-128, 0, 128) else 3
                else:
                    g[:, col_v] = 16384 + pp
                    bi = 3
                g[:, col_b] = bi * 128 + pp
                col_v += 1
                col_b += 1
    return g.astype(np.int32)


def ph_l0_attn_u(p, cx, kgx2_ap, vgx_ap, btiles_ap, gtab_ap):
    nc = p.nc
    mm = p.sem("mm_att")
    vld = p.sem("vld_att")
    bld = p.sem("bld_att")
    sread = p.sem("sr_att")
    erfd = p.sem("erf_att")
    attd = p.sem("ar_att")
    kld = p.sem("kld_att")
    evx = p.sem("evx_att")
    tld = p.sem("tld_att")
    gt = cx.gtab
    p.dma("sync", gt[:, :], gtab_ap, tld)
    for i in range(3):
        p.dma("sync", cx.scr[:, i * ICH:(i + 1) * ICH], cx.io["toe"][i], tld)
    p.wait("vector", tld)

    def fv0(v):
        for i in range(3):
            sl = cx.scr[:, i * ICH:(i + 1) * ICH]
            v.tensor_scalar(sl, sl, KB1, KB0, ALU.mult, ALU.add)
        v.drain()
        v.tensor_scalar(cx.stc[:, 20:21], cx.scr[:, 3 * ICH - 1: 3 * ICH],
                        1.0, None, ALU.mult)
        v.drain()
        v.tensor_scalar(cx.scr[:, 3 * ICH: 4 * ICH],
                        cx.scr[:, 0:ICH], 0.0, cx.stc[:, 20:21],
                        ALU.mult, ALU.add)
    p.v(fv0)
    p.di("vector", tld)
    p.wait("sync", tld)
    for i in range(4):
        p.dma("sync", btiles_ap[i * 128:(i + 1) * 128, :],
              cx.scr[:, i * ICH:(i + 1) * ICH], tld)
    p.wait("sync", tld)
    p.wait("gpsimd", tld)
    p.barrier()

    kbuf = cx.big16
    af32p = [cx.scr[:, 1024 + i * 512: 1024 + (i + 1) * 512] for i in range(2)]
    ef32p = [cx.scr[:, 2048 + i * 512: 2048 + (i + 1) * 512] for i in range(2)]
    attb = [cx.wB[:, i * ICH:(i + 1) * ICH] for i in range(4)]
    bring = getattr(cx, "attn_bring", None) or \
        [cx.abuf[:, i * ICH:(i + 1) * ICH] for i in range(4)]
    vring = [cx.wA[:, i * 1024:(i + 1) * 1024] for i in range(4)]
    attnv_mark = {}

    for s in range(2):
        coff = s * CHUNK
        p.wait("gpsimd", mm)
        for jc in range(KCOLS[s] // 1024):
            p.gdma(kbuf[:, jc * 1024:(jc + 1) * 1024], kgx2_ap,
                   gt[:, (0 if s == 0 else 4) + jc:
                      (0 if s == 0 else 4) + jc + 1], kld)
        p.wait("tensor", kld)
        for ich in range(4):
            jt_n = JT_U[s][ich]
            i0c = coff + ich * 256
            cv0 = 12 + sum(JT_U[ss][ii] for ss in range(2) for ii in range(4)
                           if (ss, ii) < (s, ich))
            cb0 = cv0 + 360
            p.wait("gpsimd", mm)
            nbp = len(bring) // 2
            for j0 in range(0, jt_n, 2):
                for dj in range(2):
                    j = j0 + dj
                    if j >= 4:
                        p.wait("gpsimd", mm, attnv_mark[j - 4])
                    p.gdma(vring[j % 4], vgx_ap, gt[:, cv0 + j: cv0 + j + 1],
                           vld)
                p.lag_wait("gpsimd", sread, (nbp - 1) if j0 > 0 else 0)
                for dj in range(2):
                    j = j0 + dj
                    p.gdma(bring[j % (2 * nbp)], btiles_ap,
                           gt[:, cb0 + j: cb0 + j + 1], bld)
                p.lag_wait("tensor", sread, 1 if j0 > 0 else 0)
                for dj in range(2):
                    j = j0 + dj
                    simbank = cx.pB[:, (j % 4) * 256:(j % 4) * 256 + 256]

                    def fts(t, simbank=simbank, j=j, i0c=i0c):
                        t.matmul(simbank, kbuf[:, j * 128:(j + 1) * 128],
                                 cx.l0q[:, i0c:i0c + 256],
                                 start=True, stop=True).then_inc(mm.h, 1)
                    p.t(fts)
                    mm.n += 1
                ip = (j0 // 2) % 2
                simpair = cx.pB[:, (j0 % 4) * 256:(j0 % 4) * 256 + 512]
                brpair = bring[j0 % (2 * nbp)]
                brpair2 = bring[(j0 + 1) % (2 * nbp)]
                p.wait("vector", mm)
                p.wait("vector", bld)
                p.lag_wait("vector", erfd, 1)

                def fvn(v, simpair=simpair, ip=ip, brpair=brpair,
                        brpair2=brpair2):
                    v.scalar_tensor_tensor(af32p[ip][:, 0:256],
                                           simpair[:, 0:256], K1S,
                                           brpair, ALU.mult, ALU.add)
                    v.scalar_tensor_tensor(af32p[ip][:, 256:512],
                                           simpair[:, 256:512], K1S,
                                           brpair2, ALU.mult, ALU.add)
                p.v(fvn)
                p.di("vector", sread)
                p.wait("scalar", sread)
                p.lag_wait("scalar", attd, 1)

                def fsn(s_, ip=ip):
                    s_.activation(ef32p[ip], af32p[ip], AF.Erf, bias=0.0,
                                  scale=1.0)
                p.s(fsn)
                p.di("scalar", erfd)
                p.wait("vector", erfd)
                if j0 >= 4:
                    p.wait("vector", mm, attnv_mark[j0 - 3])

                def fva(v, ip=ip, j0=j0):
                    v.tensor_scalar(
                        cx.wB[:, (j0 % 4) * 256:(j0 % 4) * 256 + 512],
                        ef32p[ip], 0.5, 0.5, ALU.mult, ALU.add)
                p.v(fva)
                p.di("vector", attd)
                p.wait("tensor", attd)
                p.wait("tensor", vld)
                for dj in range(2):
                    j = j0 + dj

                    def fta(t, j=j, jt_n=jt_n):
                        for et in range(8):
                            i = t.matmul(
                                cx.pA[:, et * 256:(et + 1) * 256],
                                vring[j % 4][:, et * 128:(et + 1) * 128],
                                attb[j % 4], start=(j == 0 and et % 2 == 0),
                                stop=(j == jt_n - 1))
                        i.then_inc(mm.h, 1)
                    p.t(fta)
                    mm.n += 1
                    attnv_mark[j] = mm.n
            p.wait("vector", mm)
            for et in range(8):
                def fve(v, et=et, i0c=i0c):
                    v.tensor_add(
                        cx.xres[:, et * TPC + i0c: et * TPC + i0c + 256],
                        cx.xres[:, et * TPC + i0c: et * TPC + i0c + 256],
                        cx.pA[:, et * 256:(et + 1) * 256])
                p.v(fve)
            p.di("vector", evx)
            p.wait("tensor", evx)
    p.barrier()


def ph_l1_proj_scan(p, cx, lbuf, pbuf, sumout):
    nc = p.nc
    ld = p.sem("ld_l1p")
    mm = p.sem("mm_l1p")
    evp = p.sem("evp_l1p")
    sst = p.sem("sst_l1p")
    p.dma("sync", cx.stc[:, 24:32], cx.io["lam"], ld)
    p.dma("sync", cx.stc[:, 32:40], cx.io["b_ig"], ld)
    p.dma("sync", cx.stc[:, 40:48], cx.io["b_rg"], ld)
    p.wait("scalar", ld)
    p.wait("vector", ld)
    lamc = cx.stc[:, 24:32]
    nbig = cx.stc[:, 32:40]
    nbrg = cx.stc[:, 40:48]

    def fs0(s_):
        s_.activation(cx.stc[:, 48:56], lamc, AF.Exp, bias=0.0, scale=-1.0)
    p.s(fs0)
    p.di("scalar", evp)
    p.wait("vector", evp)

    def fv0(v):
        v.tensor_scalar_add(cx.stc[:, 48:56], cx.stc[:, 48:56], 1.0)
        v.drain()
        v.reciprocal(cx.stc[:, 56:64], cx.stc[:, 48:56])
        v.drain()
        v.tensor_scalar_add(cx.stc[:, 56:64], cx.stc[:, 56:64], 1e-8)
        v.tensor_scalar_mul(nbig, nbig, -1.0)
        v.tensor_scalar_mul(nbrg, nbrg, -1.0)
    p.v(fv0)
    p.di("vector", evp)
    p.wait("scalar", evp)

    def fs1(s_):
        s_.activation(lamc, cx.stc[:, 56:64], AF.Ln, bias=0.0, scale=1.0)
    p.s(fs1)
    p.di("scalar", evp)
    p.wait("vector", evp)

    def fv1(v):
        v.tensor_scalar_mul(lamc, lamc, C_REC)
    p.v(fv1)
    p.di("vector", evp)
    p.wait("scalar", evp)
    p.wait("tensor", evp)

    zt = cx.scr[:, 2048:3072]

    def fvz(v):
        v.memset(zt, 0.0)
    p.v(fvz)
    p.di("vector", evp)

    for mt in range(NCT):
        p.barrier(("gpsimd", "tensor"))
        for wi, wname in enumerate(("WinT", "WigT", "WrgT")):
            for kt in range(NCT):
                p.dma("gpsimd",
                      cx.wB[:, wi * 1024 + kt * 128: wi * 1024 + kt * 128 + 128],
                      cx.io[wname][kt * 128:(kt + 1) * 128,
                                   mt * 128:(mt + 1) * 128], ld)
        p.wait("tensor", ld)
        for tch in range(4):
            t0 = tch * 512
            for wi in range(3):
                dst = cx.pB[:, wi * 512: wi * 512 + 512]

                def ftp(t, dst=dst, wi=wi, t0=t0):
                    for kt in range(NCT):
                        i = t.matmul(
                            dst,
                            cx.wB[:, wi * 1024 + kt * 128:
                                  wi * 1024 + kt * 128 + 128],
                            cx.xn[:, kt * TPC + t0: kt * TPC + t0 + 512],
                            start=(kt == 0), stop=(kt == NCT - 1))
                    i.then_inc(mm.h, 1)
                p.t(ftp)
                mm.n += 1
            p.wait("scalar", mm)
            p.wait("vector", mm)
            xp = cx.scr[:, 0:512]
            iw = cx.scr[:, 512:1024]
            aw = cx.scr[:, 1024:1536]
            sq = cx.scr[:, 1536:2048]
            mtc = cx.stc[:, 24 + mt:25 + mt]

            def fvx(v):
                v.tensor_copy(xp, cx.pB[:, 0:512])
            p.v(fvx)

            def fsi(s_):
                s_.activation(iw, cx.pB[:, 512:1024], AF.Exp,
                              bias=nbig[:, mt:mt + 1], scale=-1.0)
                s_.drain()
                s_.activation(aw, cx.pB[:, 1024:1536], AF.Exp,
                              bias=nbrg[:, mt:mt + 1], scale=-1.0)
            p.s(fsi)
            p.di("scalar", evp)
            p.wait("vector", evp)

            def fvi(v):
                v.tensor_scalar_add(iw, iw, 1.0)
                v.drain()
                v.reciprocal(iw, iw)
                v.drain()
                v.tensor_scalar_add(aw, aw, 1.0)
                v.drain()
                v.reciprocal(aw, aw)
            p.v(fvi)
            p.di("vector", evp)
            p.wait("scalar", evp)

            def fsa(s_):
                s_.activation(aw, aw, AF.Exp, bias=0.0,
                              scale=mtc)
                s_.drain()
                s_.activation(sq, aw, AF.Square, bias=0.0, scale=1.0)
            p.s(fsa)
            p.di("scalar", evp)
            p.wait("vector", evp)

            def fvu(v):
                v.tensor_scalar(sq, sq, -1.0, 1.000001, ALU.mult, ALU.add)
            p.v(fvu)
            p.di("vector", evp)
            p.wait("scalar", evp)

            def fsq(s_):
                s_.activation(sq, sq, AF.Ln, bias=0.0, scale=1.0)
                s_.drain()
                s_.activation(sq, sq, AF.Exp, bias=0.0, scale=0.5)
            p.s(fsq)
            p.di("scalar", evp)
            p.wait("vector", evp)

            def fvg(v, t0=t0):
                v.tensor_mul(iw, iw, xp)
                v.drain()
                v.tensor_mul(cx.gbuf[:, t0:t0 + 512], iw, sq)
                v.tensor_copy(cx.abuf[:, t0:t0 + 512], aw)
            p.v(fvg)
            p.di("vector", evp)
            p.wait("tensor", evp)
        for ch in range(2):
            c0 = ch * 1024
            Lout = cx.scr[:, 0:1024]
            Pout = cx.scr[:, 1024:2048]
            p.wait("vector", sst)

            def fsc(v, c0=c0, ch=ch, Lout=Lout, Pout=Pout):
                v.tensor_tensor_scan(Lout, cx.abuf[:, c0:c0 + 1024],
                                     cx.gbuf[:, c0:c0 + 1024], 0.0,
                                     ALU.mult, ALU.add)
                v.drain()
                v.tensor_tensor_scan(Pout, cx.abuf[:, c0:c0 + 1024],
                                     zt, 1.0, ALU.mult, ALU.add)
                v.drain()
                v.tensor_copy(cx.stc[:, 64 + ch * 2: 65 + ch * 2],
                              Pout[:, 1023:1024])
                v.tensor_copy(cx.stc[:, 65 + ch * 2: 66 + ch * 2],
                              Lout[:, 1023:1024])
            p.v(fsc)
            p.di("vector", evp)
            p.wait("sync", evp)
            p.dma("sync", lbuf[mt * 128:(mt + 1) * 128, c0:c0 + 1024],
                  Lout, sst)
            p.dma("sync", pbuf[mt * 128:(mt + 1) * 128, c0:c0 + 1024],
                  Pout, sst)
        p.dma("sync", sumout[:, mt * 4:(mt + 1) * 4],
              cx.stc[:, 64:68], sst)
    p.wait("sync", sst)
    p.barrier()


def build_launch2(core_in):
    p = Prog()
    nc = p.nc
    outs = [("X1", [1024, TPC], F32), ("XN1", [1024, TPC], BF16),
            ("LBUF", [1024, TPC], F32), ("PBUF", [1024, TPC], F32),
            ("SUM", [128, 32], F32)]
    io = _declare_io(nc, core_in, outs)
    cx = setup_common(p, io, big16_cols=8192, itab_cols=8)
    cx.abuf = nc.alloc_sbuf_tensor("abuf", [128, TPC], F32)
    cx.gbuf = nc.alloc_sbuf_tensor("gbuf", [128, TPC], F32)
    cx.l0q = nc.alloc_sbuf_tensor("l0q", [128, TPC], BF16)
    btiles = nc.dram_tensor("btiles", [4 * 128, ICH], F32).ap()
    cx.gtab = nc.alloc_sbuf_tensor("gtab", [128, 732], mybir.dt.int32)
    p.open()
    ph_init(p, cx)
    ld = p.sem("ld_x")
    for ct in range(NCT):
        p.dma("sync", cx.xres[:, ct * TPC:(ct + 1) * TPC],
              cx.io["X1A"][ct * 128:(ct + 1) * 128, :], ld)
    p.dma("sync", cx.l0q[:, :], cx.io["Q_OWN"], ld)
    for e in ("vector", "scalar", "tensor"):
        p.wait(e, ld)
    p.barrier()
    ph_l0_attn_u(p, cx, io["KGX2"], io["VGX"], btiles, io["l0g"])
    sts = p.sem("st2")
    for ct in range(NCT):
        p.dma("sync", io["X1"][ct * 128:(ct + 1) * 128, :],
              cx.xres[:, ct * TPC:(ct + 1) * TPC], sts)
    ph_layernorm(p, cx, "ln1")
    for ct in range(NCT):
        p.dma("sync", io["XN1"][ct * 128:(ct + 1) * 128, :],
              cx.xn[:, ct * TPC:(ct + 1) * TPC], sts)
    ph_l1_proj_scan(p, cx, io["LBUF"], io["PBUF"], io["SUM"])
    p.wait("sync", sts)
    p.barrier()
    return p.close()


def ph_l1_apply(p, cx, lbuf, pbuf, sumgx_ap, gt):
    ld = p.sem("ld_ap")
    mm = p.sem("mm_ap")
    ev = p.sem("ev_ap")
    gld = p.sem("gld_ap")
    p.dma("sync", cx.stc[:, 60:68], cx.io["h0"], ld)
    p.dma("sync", cx.stc[:, 68:76], cx.io["norm_w"], ld)
    p.wait("vector", ld)
    for ci in range(2):
        hin = cx.stc[:, 76 + ci * 8: 84 + ci * 8]
        h3 = hin.rearrange("p (ct o) -> p ct o", o=1)

        def fvh(v, hin=hin):
            v.tensor_copy(hin, cx.stc[:, 60:68])
        p.v(fvh)
        p.di("vector", ev)
        for j in range(7):
            p.wait("gpsimd", ev)
            p.gdma(cx.stc[:, 100:132], sumgx_ap,
                   gt[:, 48 + ci * 7 + j: 49 + ci * 7 + j], gld)
            p.wait("vector", gld)
            blk = cx.stc[:, 100:132].rearrange("p (ct c) -> p ct c", c=4)
            cslot = 0 if j <= 3 else 2

            def fv(v, h3=h3, blk=blk, cslot=cslot):
                v.tensor_tensor(h3, h3, blk[:, :, cslot:cslot + 1], ALU.mult)
                v.drain()
                v.tensor_tensor(h3, h3, blk[:, :, cslot + 1:cslot + 2],
                                ALU.add)
                v.drain()
            p.v(fv)
            p.di("vector", ev)
    p.wait("scalar", ev)
    p.wait("sync", ev)
    for ci in range(2):
        coff = ci * CHUNK
        for ct in range(NCT):
            Lt = cx.scr[:, 0:1024]
            Pt = cx.scr[:, 1024:2048]
            p.dma("sync", Lt, lbuf[ct * 128:(ct + 1) * 128,
                                   coff:coff + 1024], ld)
            p.dma("sync", Pt, pbuf[ct * 128:(ct + 1) * 128,
                                   coff:coff + 1024], ld)
            p.wait("vector", ld)
            hf = cx.pA[:, 1024:2048]
            hin = cx.stc[:, 76 + ci * 8 + ct: 77 + ci * 8 + ct]

            def fv1(v, Pt=Pt, Lt=Lt, hf=hf, hin=hin):
                v.scalar_tensor_tensor(hf, Pt, hin, Lt, ALU.mult, ALU.add)
            p.v(fv1)
            p.di("vector", ev)
            p.wait("scalar", ev)
            sqt = cx.wA[:, ct * 1024:(ct + 1) * 1024]

            def fs1(s_, hf=hf, sqt=sqt):
                s_.activation(sqt, hf, AF.Square, bias=0.0, scale=1.0)
            p.s(fs1)
            p.di("scalar", ev)
            p.wait("sync", ev)
        p.wait("tensor", ev)
        for ct in range(NCT):
            sqt = cx.wA[:, ct * 1024:(ct + 1) * 1024]
            first, last = ct == 0, ct == NCT - 1

            def ft1(t, sqt=sqt, first=first, last=last):
                t.matmul(cx.pB[0:1, 0:512], cx.ones_col_b[:, :],
                         sqt[:, 0:512], start=first, stop=last)
                i = t.matmul(cx.pB[0:1, 512:1024], cx.ones_col_b[:, :],
                             sqt[:, 512:1024], start=first, stop=last)
                if last:
                    i.then_inc(mm.h, 1)
            p.t(ft1)
        mm.n += 1
        p.wait("scalar", mm)
        rr = cx.rows[:, 0:1024]

        def fs2(s_, rr=rr):
            s_.activation(rr[:, 0:512], cx.pB[0:1, 0:512], AF.Ln,
                          bias=1e-6, scale=1.0 / 1024.0)
            s_.drain()
            s_.activation(rr[:, 512:1024], cx.pB[0:1, 512:1024], AF.Ln,
                          bias=1e-6, scale=1.0 / 1024.0)
            s_.drain()
            s_.activation(rr, rr, AF.Exp, bias=0.0, scale=-0.5)
        p.s(fs2)
        p.di("scalar", ev)
        p.wait("tensor", ev)

        def ft2(t, rr=rr):
            t.matmul(cx.pB[:, 0:512], cx.ones_row[:, :], rr[:, 0:512],
                     start=True, stop=True)
            t.matmul(cx.pB[:, 512:1024], cx.ones_row[:, :], rr[:, 512:1024],
                     start=True, stop=True).then_inc(mm.h, 1)
        p.t(ft2)
        mm.n += 1
        p.wait("vector", mm)
        rstd = cx.scr[:, 2048:3072]

        def fvr2(v, rstd=rstd):
            v.tensor_copy(rstd, cx.pB[:, 0:1024])
        p.v(fvr2)
        for ct in range(NCT):
            Lt = cx.scr[:, 0:1024]
            Pt = cx.scr[:, 1024:2048]
            p.dma("sync", Lt, lbuf[ct * 128:(ct + 1) * 128,
                                   coff:coff + 1024], ld)
            p.dma("sync", Pt, pbuf[ct * 128:(ct + 1) * 128,
                                   coff:coff + 1024], ld)
            p.wait("vector", ld)
            hf = cx.pA[:, 1024:2048]
            hin = cx.stc[:, 76 + ci * 8 + ct: 77 + ci * 8 + ct]
            hbt = cx.big16[:, ct * TPC + coff: ct * TPC + coff + 1024]
            nw = cx.stc[:, 68 + ct:69 + ct]

            def fv2(v, Pt=Pt, Lt=Lt, hf=hf, hin=hin, hbt=hbt, nw=nw,
                    rstd=rstd):
                v.scalar_tensor_tensor(hf, Pt, hin, Lt, ALU.mult, ALU.add)
                v.drain()
                v.scalar_tensor_tensor(hf[:, 0:512], hf[:, 0:512], nw,
                                       rstd[:, 0:512], ALU.mult, ALU.mult)
                v.scalar_tensor_tensor(hf[:, 512:1024], hf[:, 512:1024], nw,
                                       rstd[:, 512:1024], ALU.mult, ALU.mult)
                v.drain()
                v.tensor_copy(hbt, hf)
            p.v(fv2)
            p.di("vector", ev)
            p.wait("sync", ev)
        p.wait("tensor", ev)
    p.barrier()
    evw = p.sem("evw_ap")
    for mt in range(NCT):
        p.barrier(("gpsimd", "tensor"))
        for kt in range(NCT):
            p.dma("gpsimd", cx.wB[:, kt * 128:(kt + 1) * 128],
                  cx.io["WoutT"][kt * 128:(kt + 1) * 128,
                                 mt * 128:(mt + 1) * 128], ld)
        p.wait("tensor", ld)
        for tch in range(4):
            t0 = tch * 512
            dst = cx.pB[:, (tch % 2) * 512:(tch % 2) * 512 + 512]
            p.lag_wait("tensor", evw, 1)

            def ft3(t, dst=dst, t0=t0):
                for kt in range(NCT):
                    i = t.matmul(
                        dst, cx.wB[:, kt * 128:(kt + 1) * 128],
                        cx.big16[:, kt * TPC + t0: kt * TPC + t0 + 512],
                        start=(kt == 0), stop=(kt == NCT - 1))
                i.then_inc(mm.h, 1)
            p.t(ft3)
            mm.n += 1
            p.wait("vector", mm)
            xdst = cx.xres[:, mt * TPC + t0: mt * TPC + t0 + 512]

            def fv3(v, dst=dst, xdst=xdst):
                v.tensor_add(xdst, xdst, dst)
            p.v(fv3)
            p.di("vector", evw)
    p.wait("tensor", evw)
    p.barrier()


XMW = 1284
ZW = 2064

W_KT = 0
W_QT = 8192
W_SB2 = 12800
W_OT = 12800
W_VBQ = 17408
W_MASK = 21504
W_PN = 25600
W_PT = 26624
W_XLN = 0
W_HGL = 8192
W_GEL = 19456


def l1_gtab(core):
    g = np.zeros((128, 70), np.int64)
    pp = np.arange(128)
    for ci, (b, ktrue, coff) in enumerate(chunks_of_core(core)):
        t0g = ktrue * 1024
        for ct in range(NCT):
            for piece in range(3):
                tok = [t0g - 1025, t0g - 1024, t0g][piece]
                if tok < 0:
                    g[:, ci * 24 + ct * 3 + piece] = 16384 + pp
                else:
                    jc = tok // 1024
                    r, off = owner_of_chunk(b, jc)
                    g[:, ci * 24 + ct * 3 + piece] = (
                        (r * 1024 + ct * 128 + pp) * 2 + off // 1024)
        for j in range(7):
            if j < ktrue:
                r, off = owner_of_chunk(b, j)
                g[:, 48 + ci * 7 + j] = r * 128 + pp
            else:
                g[:, 48 + ci * 7 + j] = 1024 + pp
        for sl4 in range(4):
            pat = sl4 * 2
            if ktrue == 0:
                g[:, 62 + ci * 4 + sl4] = 1024 + pp
            else:
                g[:, 62 + ci * 4 + sl4] = pat * 128 + pp
    return g.astype(np.int32)


def l1_rots(core):
    cs_q, sn_q, cs_k, sn_k = _xpos_tables()
    tabs = (cs_q, sn_q, cs_k, sn_k)
    out = {}
    for ci, (b, ktrue, coff) in enumerate(chunks_of_core(core)):
        qs = ktrue * 1024 - 1024
        for ri in range(4):
            a = np.zeros((2048, 64), np.float32)
            lo = max(qs, 0)
            a[lo - qs:2048] = tabs[ri][lo:qs + 2048]
            out["rot%d_%d" % (ci, ri)] = a
    return out


def _l1ffn_maskx():
    m = _l1ffn_masks().reshape(1024, 512)
    return np.concatenate([m, np.full((128, 512), -1e9, np.float32)], 0)


def ph_l1ffn_layer(p, cx, l, zg2_ap, zl_ap, gt, vscr, maskx_ap,
                   yout=None, x2a=None, fout=None, ylout=None):
    ld = p.sem("ldf%d" % l)
    ldr = p.sem("ldrf%d" % l)
    ldw = p.sem("ldwf%d" % l)
    gz = p.sem("gzf%d" % l)
    mm = p.sem("mmf%d" % l)
    evm = p.sem("evmf%d" % l)
    evs = p.sem("evsf%d" % l)
    evz = p.sem("evzf%d" % l)
    ev = p.sem("evf%d" % l)
    evt = p.sem("evtf%d" % l)
    evo = p.sem("evof%d" % l)
    vst = p.sem("vstf%d" % l)
    yst = p.sem("ystf%d" % l)
    wk = cx.work16
    negm = cx.stc[:, 296:297]
    Zc = cx.stc[:, 297:298]
    rz = cx.stc[:, 298:299]
    p.dma("sync", cx.rrow[:, 0:512], cx.io["qs_t"][l], ld)
    p.dma("sync", cx.rrow[:, 512:1024], cx.io["ks_t"][l], ld)
    for mi in range(8):
        p.dma("gpsimd", wk[:, W_MASK + mi * 512: W_MASK + (mi + 1) * 512],
              maskx_ap[mi * 128:(mi + 1) * 128, :], ld)
    p.dma("sync", cx.stc[:, 200:208], cx.io["ln_g"][l], ld)
    p.dma("sync", cx.stc[:, 208:216], cx.io["ln_b"][l], ld)
    p.dma("sync", cx.stc[:, 216:224], cx.io["b2f"][l], ld)
    p.dma("sync", cx.stc[:, 224:268], cx.io["b1p"][l], ld)
    p.wait("tensor", ld)

    def ftq(t):
        t.matmul(cx.pB[:, 0:512], cx.ones_row[:, :], cx.rrow[:, 0:512],
                 start=True, stop=True)
        t.matmul(cx.pB[:, 512:1024], cx.ones_row[:, :], cx.rrow[:, 512:1024],
                 start=True, stop=True).then_inc(mm.h, 1)
    p.t(ftq)
    mm.n += 1
    p.wait("vector", mm)

    def fvq(v):
        v.tensor_scalar_mul(cx.qksb[:, 0:512], cx.pB[:, 0:512], QK_SCALE)
        v.tensor_copy(cx.qksb[:, 512:1024], cx.pB[:, 512:1024])
    p.v(fvq)
    p.di("vector", ev)
    p.wait("tensor", ev)

    for ci in range(2):
        coff = ci * CHUNK
        ntt = 16
        p.barrier()
        for ct in range(NCT):
            zrow = cx.big16[:, ct * ZW: ct * ZW + ZW]

            def fvz(v, zrow=zrow):
                v.memset(zrow[:, 0:1], 0.0)
            p.v(fvz)
            base = ci * 24 + ct * 3
            p.gdma(zrow[:, 1:2], zl_ap, gt[:, base:base + 1], gz)
            p.gdma(zrow[:, 2:1026], zg2_ap, gt[:, base + 1:base + 2], gz)
            p.gdma(zrow[:, 1026:2050], zg2_ap, gt[:, base + 2:base + 3], gz)
        p.di("vector", ev)
        p.wait("gpsimd", ev)
        p.wait("tensor", gz)
        p.wait("tensor", ev)
        for kt in range(NCT):
            p.dma("gpsimd", cx.wA[:, kt * 1024:(kt + 1) * 1024],
                  cx.io["wqkv"][l][kt * 128:(kt + 1) * 128, 0:1024], ldw)
            p.dma("gpsimd", cx.wA[:, 8192 + kt * 512: 8192 + (kt + 1) * 512],
                  cx.io["wqkv"][l][kt * 128:(kt + 1) * 128, 1024:1536], ldw)
        p.wait("tensor", ldw)
        for tt in range(ntt):
            zc0 = 2 + tt * 128
            rots = cx.scr[:, 2048:2304]
            p.wait("sync", evt)
            for ri in range(4):
                p.dma("sync", rots[:, ri * 64: ri * 64 + 64],
                      cx.io["rot%d_%d" % (ci, ri)][tt * 128:(tt + 1) * 128,
                                                   :], ldr)
            p.lag_wait("tensor", ev, 1)

            def ftk(t, zc0=zc0):
                for kt in range(NCT):
                    zrow = cx.big16[:, kt * ZW: kt * ZW + ZW]
                    sh = 0 if kt < 4 else -1
                    lhsT = zrow[:, zc0 + sh: zc0 + sh + 128]
                    t.matmul(cx.pA[:, 0:512], lhsT,
                             cx.wA[:, kt * 1024: kt * 1024 + 512],
                             start=(kt == 0), stop=(kt == NCT - 1))
                    t.matmul(cx.pA[:, 512:1024], lhsT,
                             cx.wA[:, kt * 1024 + 512: kt * 1024 + 1024],
                             start=(kt == 0), stop=(kt == NCT - 1))
                    i = t.matmul(
                        cx.pB[:, 0:512], lhsT,
                        cx.wA[:, 8192 + kt * 512: 8192 + (kt + 1) * 512],
                        start=(kt == 0), stop=(kt == NCT - 1))
                i.then_inc(mm.h, 1)
            p.t(ftk)
            mm.n += 1
            p.wait("vector", mm)
            p.wait("vector", ldr)
            vtile = wk[:, W_SB2:W_SB2 + 512]
            p.wait("vector", vst)

            def fvv(v, vtile=vtile):
                v.tensor_copy(vtile, cx.pB[:, 0:512])
            p.v(fvv)
            p.di("vector", ev)
            p.wait("sync", ev)
            p.dma("sync", vscr[tt * 128:(tt + 1) * 128, :], vtile, vst)
            have_q = (tt * 128 - 896) >= 0
            qis = [0, 1] if have_q else [1]
            fbufs = [cx.scr[:, 0:512], cx.scr[:, 512:1024]]
            sqbs = [cx.scr[:, 1024:1536], cx.scr[:, 1536:2048]]
            qbs = [wk[:, W_SB2 + 512:W_SB2 + 1024],
                   wk[:, W_SB2 + 1792:W_SB2 + 2304]]
            qrots = [wk[:, W_SB2 + 1024:W_SB2 + 1536],
                     wk[:, W_SB2 + 2304:W_SB2 + 2816]]
            tmps = [wk[:, W_SB2 + 1536:W_SB2 + 1792],
                    wk[:, W_SB2 + 2816:W_SB2 + 3072]]
            sss = [cx.stc[:, 280:288], cx.stc[:, 288:296]]

            def fvc(v):
                for qi in qis:
                    v.tensor_copy(fbufs[qi], cx.pA[:, qi * 512:
                                                   qi * 512 + 512])
            p.v(fvc)
            p.di("vector", ev)
            p.wait("scalar", ev)

            def fsq(s_):
                if len(qis) == 2:
                    s_.activation(cx.scr[:, 1024:2048], cx.scr[:, 0:1024],
                                  AF.Square, bias=0.0, scale=1.0)
                else:
                    s_.activation(sqbs[1], fbufs[1], AF.Square, bias=0.0,
                                  scale=1.0)
            p.s(fsq)
            p.di("scalar", ev)
            p.wait("vector", ev)

            def fvr(v):
                for qi in qis:
                    v.tensor_reduce(sss[qi], sqbs[qi].rearrange(
                        "p (h d) -> p h d", d=64), AXX, ALU.add)
                v.drain()
                for qi in qis:
                    v.tensor_scalar_max(sss[qi], sss[qi], 1e-24)
            p.v(fvr)
            p.di("vector", ev)
            p.wait("scalar", ev)

            def fsr(s_):
                lo = 280 if have_q else 288
                sl = cx.stc[:, lo:296]
                s_.activation(sl, sl, AF.Ln, bias=0.0, scale=1.0)
                s_.drain()
                s_.activation(sl, sl, AF.Exp, bias=0.0, scale=-0.5)
            p.s(fsr)
            p.di("scalar", ev)
            p.wait("vector", ev)

            def fvn(v):
                for qi in qis:
                    f3 = fbufs[qi].rearrange("p (h d) -> p h d", d=64)
                    ssb = sss[qi].rearrange("p (h o) -> p h o", o=1
                                            ).broadcast_to([128, 8, 64])
                    v.tensor_tensor(f3, f3, ssb, ALU.mult)
                v.drain()
                for qi in qis:
                    v.tensor_mul(fbufs[qi], fbufs[qi],
                                 cx.qksb[:, qi * 512: qi * 512 + 512])
                v.drain()
                for qi in qis:
                    v.tensor_copy(qbs[qi], fbufs[qi])
                v.drain()
            p.v(fvn)

            def fvrot(v):
                o3s, q3s, tm3s = {}, {}, {}
                for qi in qis:
                    q3s[qi] = qbs[qi].rearrange("p (h d) -> p h d", d=64)
                    o3s[qi] = qrots[qi].rearrange("p (h d) -> p h d", d=64)
                    tm3s[qi] = tmps[qi].rearrange("p (h d) -> p h d", d=32)
                for qi in qis:
                    cs = rots[:, (0 if qi == 0 else 2) * 64:
                              (0 if qi == 0 else 2) * 64 + 64]
                    csb = cs.rearrange("p (o d) -> p o d", o=1
                                       ).broadcast_to([128, 8, 64])
                    v.tensor_tensor(o3s[qi], q3s[qi], csb, ALU.mult)
                v.drain()
                for qi in qis:
                    sn = rots[:, (1 if qi == 0 else 3) * 64:
                              (1 if qi == 0 else 3) * 64 + 64]
                    snl = sn[:, 0:32].rearrange("p (o d) -> p o d", o=1
                                                ).broadcast_to([128, 8, 32])
                    v.tensor_tensor(tm3s[qi], q3s[qi][:, :, 32:64], snl,
                                    ALU.mult)
                v.drain()
                for qi in qis:
                    v.tensor_tensor(o3s[qi][:, :, 0:32], o3s[qi][:, :, 0:32],
                                    tm3s[qi], ALU.subtract)
                v.drain()
                for qi in qis:
                    sn = rots[:, (1 if qi == 0 else 3) * 64:
                              (1 if qi == 0 else 3) * 64 + 64]
                    snh = sn[:, 32:64].rearrange("p (o d) -> p o d", o=1
                                                 ).broadcast_to([128, 8, 32])
                    v.tensor_tensor(tm3s[qi], q3s[qi][:, :, 0:32], snh,
                                    ALU.mult)
                v.drain()
                for qi in qis:
                    v.tensor_tensor(o3s[qi][:, :, 32:64],
                                    o3s[qi][:, :, 32:64], tm3s[qi], ALU.add)
            p.v(fvrot)
            p.di("vector", ev)
            for qi in qis:
                qcol = (tt * 128 - 896) if qi == 0 else (tt * 128)
                stride = 1152 if qi == 0 else 2048
                base = W_QT if qi == 0 else W_KT
                p.wait("tensor", ev)
                p.wait("tensor", evt)

                def ftt(t, qrot=qrots[qi]):
                    for s in range(4):
                        i = t.transpose(cx.tp[:, s * 128:(s + 1) * 128],
                                        qrot[:, s * 128:(s + 1) * 128],
                                        cx.identb[:, :])
                    i.then_inc(mm.h, 1)
                p.t(ftt)
                mm.n += 1
                p.wait("vector", mm)

                def fvt(v, base=base, stride=stride, qcol=qcol):
                    for s in range(4):
                        v.tensor_copy(
                            wk[:, base + s * stride + qcol:
                               base + s * stride + qcol + 128],
                            cx.tp[:, s * 128:(s + 1) * 128])
                p.v(fvt)
                p.di("vector", evt)
        p.wait("tensor", evt)
        p.barrier()
        wlist = [(-128, 1, -512), (0, 4, 0), (512, 4, 512)]
        for wi, (qstart, nqt, w0) in enumerate(wlist):
            kstart = w0 - 512
            ko = kstart + 1024
            if wi == 1:
                p.wait("gpsimd", evm)
                for sl4 in range(4):
                    p.gdma(wk[:, W_MASK + sl4 * 2 * 512:
                              W_MASK + sl4 * 2 * 512 + 512], maskx_ap,
                           gt[:, 62 + ci * 4 + sl4: 63 + ci * 4 + sl4], ld)
                p.wait("vector", ld)
            if wi == 2:
                p.wait("gpsimd", evm)
                for sl4 in range(4):
                    p.dma("gpsimd",
                          wk[:, W_MASK + sl4 * 2 * 512:
                             W_MASK + sl4 * 2 * 512 + 512],
                          maskx_ap[sl4 * 2 * 128:(sl4 * 2 + 1) * 128, :], ld)
                p.wait("vector", ld)
            p.wait("sync", mm)
            for js in range(8):
                jrel = ko // 128 + js
                p.dma("sync", wk[:, W_VBQ + js * 512: W_VBQ + (js + 1) * 512],
                      vscr[jrel * 128:(jrel + 1) * 128, :], ld)
            p.wait("tensor", ld)
            xq = cx.xres.bitcast(BF16)
            simsFb = [cx.xres[:, 12800:13824], cx.xres[:, 13824:14848]]
            pexpb = [xq[:, 29696:30720], xq[:, 30720:31744]]
            pTb = [wk[:, W_PN:W_PN + 1024], wk[:, W_PT:W_PT + 1024]]

            def run_tiles(tiles):
                nT = len(tiles)
                ftscnt = [0] * nT
                evmcnt = [0] * nT
                scnt = [0] * nT
                zcnt = [0] * nT
                tpcnt = [0] * nT
                evtcnt = [0] * nT
                ftocnt = [0] * nT
                evocnt = [evo.n] * nT

                def em_fts(T, h):
                    qcol = tiles[T][0]
                    sl, hb = h // 2, (h % 2) * 64

                    def fts(t, sl=sl, hb=hb, ko=ko, qcol=qcol, T=T):
                        lhsT = wk[hb:hb + 64, W_QT + sl * 1152 + qcol:
                                  W_QT + sl * 1152 + qcol + 128]
                        for sd in range(2):
                            i = t.matmul(
                                cx.pA[:, T * 1024 + sd * 512:
                                      T * 1024 + sd * 512 + 512],
                                lhsT,
                                wk[hb:hb + 64,
                                   W_KT + sl * 2048 + ko + sd * 512:
                                   W_KT + sl * 2048 + ko + sd * 512 + 512],
                                start=True, stop=True)
                        i.then_inc(mm.h, 1)
                    p.wait("tensor", evm, val=evmcnt[T])
                    p.t(fts)
                    mm.n += 1
                    ftscnt[T] = mm.n

                for T in range(nT):
                    em_fts(T, 0)
                for h in range(HEADS):
                    sl, hb = h // 2, (h % 2) * 64
                    for T in range(nT):
                        moff = tiles[T][1]
                        simsF = simsFb[T]
                        ngm = cx.stc[:, 270 + T:271 + T]

                        def fvm(v, moff=moff, T=T, simsF=simsF, ngm=ngm):
                            for sd in range(2):
                                mi = moff + sd
                                v.scalar_tensor_tensor(
                                    simsF[:, sd * 512: sd * 512 + 512],
                                    cx.pA[:, T * 1024 + sd * 512:
                                          T * 1024 + sd * 512 + 512], 1.0,
                                    wk[:, W_MASK + mi * 512:
                                       W_MASK + (mi + 1) * 512],
                                    ALU.mult, ALU.add)
                            v.drain()
                            v.tensor_reduce(ngm, simsF, AXX, ALU.max,
                                            negate=True)
                        p.wait("vector", mm, val=ftscnt[T])
                        p.wait("vector", evs, val=scnt[T])
                        p.v(fvm)
                        p.di("vector", evm)
                        evmcnt[T] = evm.n
                    for T in range(nT):
                        if h + 1 < HEADS:
                            em_fts(T, h + 1)
                    for T in range(nT):
                        simsF = simsFb[T]
                        pexp = pexpb[T]
                        ngm = cx.stc[:, 270 + T:271 + T]
                        Zch = cx.stc[:, 272 + T:273 + T]

                        def fse(s_, simsF=simsF, pexp=pexp, ngm=ngm,
                                Zch=Zch):
                            s_.activation(pexp, simsF, AF.Exp, bias=ngm,
                                          scale=1.0, accum_out=Zch)
                        p.wait("scalar", evm, val=evmcnt[T])
                        p.wait("scalar", mm, val=tpcnt[T])
                        p.wait("scalar", evz, val=zcnt[T])
                        p.s(fse)
                        p.di("scalar", evs)
                        scnt[T] = evs.n
                    for T in range(nT):
                        pexp = pexpb[T]
                        Zch = cx.stc[:, 272 + T:273 + T]
                        rzh = cx.stc[:, 274 + T:275 + T]

                        def fvz2(v, pexp=pexp, Zch=Zch, rzh=rzh):
                            v.reciprocal(rzh, Zch)
                            v.drain()
                            v.tensor_scalar_mul(pexp, pexp, rzh)
                        p.wait("vector", evs, val=scnt[T])
                        p.v(fvz2)
                        p.di("vector", evz)
                        zcnt[T] = evz.n
                    for T in range(nT):
                        pexp = pexpb[T]
                        pT_ = pTb[T]

                        def ftp(t, pexp=pexp):
                            for s in range(8):
                                i = t.transpose(
                                    cx.tp[:, s * 128:(s + 1) * 128],
                                    pexp[:, s * 128:(s + 1) * 128],
                                    cx.identb[:, :])
                            i.then_inc(mm.h, 1)
                        p.wait("tensor", evz, val=zcnt[T])
                        p.wait("tensor", evt)
                        p.t(ftp)
                        mm.n += 1
                        tpcnt[T] = mm.n

                        def fvp(v, pT_=pT_):
                            v.tensor_copy(pT_, cx.tp[:, 0:1024])
                        p.wait("vector", mm, val=tpcnt[T])
                        p.v(fvp)
                        p.di("vector", evt)
                        evtcnt[T] = evt.n
                    for T in range(nT):
                        qcol = tiles[T][0]
                        pT_ = pTb[T]
                        obank = cx.pB[0:64, T * 512:T * 512 + 128]

                        def fto(t, obank=obank, h=h, pT_=pT_):
                            for js in range(8):
                                i = t.matmul(
                                    obank,
                                    wk[:, W_VBQ + js * 512 + h * 64:
                                       W_VBQ + js * 512 + h * 64 + 64],
                                    pT_[:, js * 128:(js + 1) * 128],
                                    start=(js == 0), stop=(js == 7))
                            i.then_inc(mm.h, 1)
                        p.wait("tensor", evt, val=evtcnt[T])
                        p.wait("tensor", evo, val=evocnt[T])
                        p.t(fto)
                        mm.n += 1
                        ftocnt[T] = mm.n

                        def fvo(v, obank=obank, sl=sl, hb=hb, qcol=qcol):
                            v.tensor_copy(
                                wk[hb:hb + 64, W_OT + sl * 1152 + qcol:
                                   W_OT + sl * 1152 + qcol + 128], obank)
                        p.wait("vector", mm, val=ftocnt[T])
                        p.v(fvo)
                        p.di("vector", evo)
                        evocnt[T] = evo.n

            qts = list(range(nqt))
            pairs = ([(qts[0], qts[1]), (qts[2], qts[3])] if nqt == 4
                     else [(qts[0],)])
            for pair in pairs:
                tiles = []
                for qt in pair:
                    qg = qstart + qt * 128
                    tiles.append((qg + 128, ((qg - w0) // 128) * 2))
                run_tiles(tiles)
        p.wait("tensor", evo)
        p.barrier(("sync", "gpsimd", "vector", "scalar", "tensor"))
        for kt in range(4):
            p.dma("gpsimd", cx.wB[:, kt * 1024:(kt + 1) * 1024],
                  cx.io["wo"][l][kt * 128:(kt + 1) * 128, :], ldw)
        p.wait("tensor", ldw)
        nranges = [(1, 513), (513, 1025), (1025, 1153)]
        evE = p.sem("evEf%d" % l)
        for (c0, c1) in nranges:
            Wn = c1 - c0
            for mt in range(NCT):
                dst = cx.pB[:, (mt % 2) * 512:(mt % 2) * 512 + Wn]
                p.lag_wait("tensor", evE, 1)

                def ftw(t, dst=dst, mt=mt, c0=c0, Wn=Wn):
                    for kt in range(4):
                        i = t.matmul(
                            dst,
                            cx.wB[:, kt * 1024 + mt * 128:
                                  kt * 1024 + mt * 128 + 128],
                            wk[:, W_OT + kt * 1152 + c0 - 1:
                               W_OT + kt * 1152 + c0 - 1 + Wn],
                            start=(kt == 0), stop=(kt == 3))
                    i.then_inc(mm.h, 1)
                p.t(ftw)
                mm.n += 1
                p.wait("vector", mm)
                zc = c0 + 897
                xm = cx.xres[:, mt * XMW + c0: mt * XMW + c1]

                def fvw(v, dst=dst, xm=xm, mt=mt, zc=zc, Wn=Wn):
                    v.tensor_tensor(xm, dst,
                                    cx.big16[:, mt * ZW + zc:
                                             mt * ZW + zc + Wn], ALU.add)
                p.v(fvw)
                p.di("vector", evE)
        p.wait("tensor", evE)
        p.barrier(("vector", "scalar", "tensor", "sync", "gpsimd"))
        mmL = p.sem("mmLf%d" % l)
        evL = p.sem("evLf%d" % l)
        for th in range(2):
            c0 = 129 + th * 512
            for ct in range(NCT):
                sh = 0 if ct < 4 else -1
                xs = cx.xres[:, ct * XMW + c0 + sh: ct * XMW + c0 + sh + 512]
                xbt = cx.wA[:, ct * 512:(ct + 1) * 512]
                sqt = cx.wA[:, 4096 + ct * 512: 4096 + (ct + 1) * 512]

                def fv4(v, xs=xs, xbt=xbt):
                    v.tensor_copy(xbt, xs)

                def fs4(s_, xs=xs, sqt=sqt):
                    s_.activation(sqt, xs, AF.Square, bias=0.0, scale=1.0)
                p.v(fv4)
                p.s(fs4)
            p.di("vector", evL)
            p.di("scalar", evL)
            p.wait("tensor", evL)
            for ct in range(NCT):
                xbt = cx.wA[:, ct * 512:(ct + 1) * 512]
                sqt = cx.wA[:, 4096 + ct * 512: 4096 + (ct + 1) * 512]
                first, last = ct == 0, ct == NCT - 1

                def ft4(t, xbt=xbt, sqt=sqt, first=first, last=last):
                    t.matmul(cx.pB[0:1, 0:512], cx.ones_col_b[:, :], xbt,
                             start=first, stop=last)
                    i = t.matmul(cx.pB[0:1, 512:1024], cx.ones_col_b[:, :],
                                 sqt, start=first, stop=last)
                    if last:
                        i.then_inc(mmL.h, 1)
                p.t(ft4)
            mmL.n += 1
            p.wait("vector", mmL)
            r0, r1, r2 = (cx.rows[:, 0:512], cx.rows[:, 512:1024],
                          cx.rrow[:, 0:512])

            def f5(v, r0=r0, r1=r1, r2=r2):
                v.tensor_scalar_mul(r0, cx.pB[0:1, 0:512], 1.0 / 1024.0)
                v.drain()
                v.tensor_mul(r2, r0, r0)
                v.tensor_scalar_mul(r1, cx.pB[0:1, 512:1024], 1.0 / 1024.0)
                v.drain()
                v.tensor_sub(r1, r1, r2)
            p.v(f5)
            p.di("vector", evL)
            p.wait("scalar", evL)

            def f6(s_, r1=r1):
                s_.activation(r1, r1, AF.Ln, bias=1e-5, scale=1.0)
                s_.drain()
                s_.activation(r1, r1, AF.Exp, bias=0.0, scale=-0.5)
            p.s(f6)
            p.di("scalar", evL)
            p.wait("vector", evL)

            def f7(v, r0=r0, r1=r1, r2=r2):
                v.tensor_mul(r2, r0, r1)
                v.drain()
                v.tensor_scalar_mul(r2, r2, -1.0)
            p.v(f7)
            p.di("vector", evL)
            p.wait("tensor", evL)

            def f8(t, r1=r1, r2=r2):
                t.matmul(cx.pB[:, 0:512], cx.ones_row[:, :], r1,
                         start=True, stop=True)
                t.matmul(cx.pB[:, 512:1024], cx.ones_row[:, :], r2,
                         start=True, stop=True).then_inc(mmL.h, 1)
            p.t(f8)
            mmL.n += 1
            p.wait("vector", mmL)
            for ct in range(NCT):
                sh = 0 if ct < 4 else -1
                xs = cx.xres[:, ct * XMW + c0 + sh: ct * XMW + c0 + sh + 512]
                tmp = cx.scr[:, 0:512]
                xlt = wk[:, W_XLN + ct * 1024 + th * 512:
                         W_XLN + ct * 1024 + th * 512 + 512]

                def f9(v, xs=xs, tmp=tmp, xlt=xlt, ct=ct):
                    v.tensor_mul(tmp, xs, cx.pB[:, 0:512])
                    v.drain()
                    v.tensor_add(tmp, tmp, cx.pB[:, 512:1024])
                    v.drain()
                    v.tensor_scalar(xlt, tmp, cx.stc[:, 200 + ct:201 + ct],
                                    cx.stc[:, 208 + ct:209 + ct],
                                    ALU.mult, ALU.add)
                p.v(f9)
            p.di("vector", evL)
            p.wait("tensor", evL)
        p.barrier(("vector", "scalar", "tensor", "sync", "gpsimd"))
        mmG = p.sem("mmGf%d" % l)
        evG = p.sem("evGf%d" % l)
        ldG = p.sem("ldGf%d" % l)
        ldG2 = p.sem("ldG2f%d" % l)
        for th in range(2):
            for m in range(44):
                qeng, qsem = "gpsimd", ldG
                p.lag_wait(qeng, mmG, 3)
                w1t = cx.io["w1p"].tensor
                src_ap = AP(w1t, l * 1024 * 5632 + m * 128,
                            [[5632, 128], [128 * 5632, 8], [1, 128]])
                p.dma(qeng,
                      cx.wB[:, (m % 4) * 1024:(m % 4) * 1024 + 1024].rearrange(
                          "p (kt c) -> p kt c", c=128),
                      src_ap, qsem)
                p.wait("tensor", qsem)
                dst = cx.pB[:, (m % 2) * 512:(m % 2) * 512 + 512]
                p.lag_wait("tensor", evG, 1)

                def fth(t, dst=dst, m=m, th=th):
                    for kt in range(NCT):
                        i = t.matmul(
                            dst,
                            cx.wB[:, (m % 4) * 1024 + kt * 128:
                                  (m % 4) * 1024 + kt * 128 + 128],
                            wk[:, W_XLN + kt * 1024 + th * 512:
                               W_XLN + kt * 1024 + th * 512 + 512],
                            start=(kt == 0), stop=(kt == NCT - 1))
                    i.then_inc(mmG.h, 1)
                p.t(fth)
                mmG.n += 1
                p.wait("scalar", mmG)
                b1c = cx.stc[:, 224 + m:225 + m]
                if m < 22:
                    hglt = wk[:, W_HGL + m * 512:W_HGL + (m + 1) * 512]

                    def fsh(s_, dst=dst, hglt=hglt, b1c=b1c):
                        s_.activation(hglt, dst, AF.Identity, bias=b1c,
                                      scale=1.0)
                    p.s(fsh)
                    p.di("scalar", evG)
                else:
                    gel = wk[:, W_GEL:W_GEL + 512]
                    hglt = wk[:, W_HGL + (m - 22) * 512:
                              W_HGL + (m - 21) * 512]

                    def fsg(s_, dst=dst, gel=gel, b1c=b1c):
                        s_.activation(gel, dst, AF.Gelu, bias=b1c, scale=1.0)
                    p.s(fsg)
                    p.di("scalar", evG)
                    p.wait("vector", evG)

                    def fvgl(v, hglt=hglt, gel=gel):
                        v.tensor_tensor(hglt, hglt, gel, ALU.mult)
                        v.drain()
                    p.v(fvgl)
                    p.di("vector", evG)
            p.wait("tensor", evG)
            for half in range(2):
                for kt in range(22):
                    qeng, qsem = "gpsimd", ldG
                    p.lag_wait(qeng, mmG, 3)
                    p.dma(qeng,
                          cx.wB[:, (kt % 4) * 1024:(kt % 4) * 1024 + 512],
                          cx.io["w2p"][l][kt * 128:(kt + 1) * 128,
                                          half * 512:(half + 1) * 512], qsem)
                    p.wait("tensor", qsem)

                    def ftw2(t, kt=kt, half=half, th=th):
                        for mtl in range(4):
                            i = t.matmul(
                                cx.pA[:, mtl * 512:(mtl + 1) * 512],
                                cx.wB[:, (kt % 4) * 1024 + mtl * 128:
                                      (kt % 4) * 1024 + mtl * 128 + 128],
                                wk[:, W_HGL + kt * 512:W_HGL + (kt + 1) * 512],
                                start=(kt == 0), stop=(kt == 21))
                        i.then_inc(mmG.h, 1)
                    p.t(ftw2)
                    mmG.n += 1
                p.wait("vector", mmG)
                for mtl in range(4):
                    mtg = half * 4 + mtl
                    tmp = cx.scr[:, mtl * 512: mtl * 512 + 512]
                    b2c = cx.stc[:, 216 + mtg:217 + mtg]
                    xm = cx.xres[:, mtg * XMW + 129 + th * 512:
                                 mtg * XMW + 129 + th * 512 + 512]

                    def fvy(v, tmp=tmp, mtl=mtl, b2c=b2c, xm=xm):
                        v.scalar_tensor_tensor(
                            tmp, cx.pA[:, mtl * 512:(mtl + 1) * 512], b2c,
                            xm, ALU.add, ALU.add)
                        v.drain()
                    p.v(fvy)
                p.di("vector", evG)
                p.wait("tensor", evG)
                p.wait("sync", evG)
                p.wait("gpsimd", evG)
                for mtl in range(4):
                    mtg = half * 4 + mtl
                    tmp = cx.scr[:, mtl * 512: mtl * 512 + 512]
                    if l == 0:
                        p.dma("gpsimd",
                              yout[mtg * 128:(mtg + 1) * 128,
                                   coff + th * 512: coff + th * 512 + 512],
                              tmp, yst)
                        if ylout is not None and th == 1:
                            p.dma("gpsimd",
                                  AP(ylout.tensor, mtg * 256 + ci,
                                     [[2, 128], [1, 1]]),
                                  tmp[:, 511:512], yst, slow_ok=True)
                    else:
                        for piece in range(2):
                            p.dma("sync", cx.scr[:, 2048:2304],
                                  x2a[mtg * 128:(mtg + 1) * 128,
                                      coff + th * 512 + piece * 256:
                                      coff + th * 512 + piece * 256 + 256],
                                  ldG)
                            p.wait("vector", ldG)

                            def fvf(v, tmp=tmp, piece=piece):
                                v.tensor_add(
                                    tmp[:, piece * 256:(piece + 1) * 256],
                                    tmp[:, piece * 256:(piece + 1) * 256],
                                    cx.scr[:, 2048:2304])
                                v.drain()
                            p.v(fvf)
                            p.di("vector", evG)
                            p.wait("sync", evG)
                            p.dma("sync",
                                  fout[mtg * 128:(mtg + 1) * 128,
                                       coff + th * 512 + piece * 256:
                                       coff + th * 512 + piece * 256 + 256],
                                  tmp[:, piece * 256:(piece + 1) * 256], yst)
                if l == 0:
                    p.wait("gpsimd", yst)
                else:
                    p.wait("sync", yst)
                p.barrier(("vector", "sync", "gpsimd", "tensor"))
    p.wait("sync", yst)
    p.wait("gpsimd", yst)
    p.barrier()


def build_launch3(core_in):
    p = Prog()
    nc = p.nc
    outs = [("X2A", [1024, TPC], F32), ("Y1", [1024, TPC], BF16)]
    io = _declare_io(nc, core_in, outs)
    cx = setup_common(p, io, big16_cols=NCT * ZW, xn_cols=1,
                      rows_cols=1024, scr_cols=3072, stc_cols=384,
                      itab_cols=8)
    cx.work16 = nc.alloc_sbuf_tensor("work16", [128, 27648], BF16)
    cx.qksb = nc.alloc_sbuf_tensor("qksb", [128, 1024], BF16)
    cx.gtab = nc.alloc_sbuf_tensor("gtab", [128, 70], mybir.dt.int32)
    vscr = nc.dram_tensor("vscr", [2048, 512], BF16).ap()
    p.open()
    ph_init(p, cx)
    ld = p.sem("ld_x")
    p.dma("sync", cx.gtab[:, :], io["l1g"], ld)
    for ct in range(NCT):
        p.dma("sync", cx.xres[:, ct * TPC:(ct + 1) * TPC],
              cx.io["X1"][ct * 128:(ct + 1) * 128, :], ld)
    for e in ("vector", "scalar", "tensor", "gpsimd"):
        p.wait(e, ld)
    p.barrier()
    ph_l1_apply(p, cx, io["LBUF"], io["PBUF"], io["SUMGX"], cx.gtab)
    sts = p.sem("st3")
    for ct in range(NCT):
        p.dma("sync", io["X2A"][ct * 128:(ct + 1) * 128, :],
              cx.xres[:, ct * TPC:(ct + 1) * TPC], sts)
    p.wait("sync", sts)
    p.barrier()
    ph_l1ffn_layer(p, cx, 0, io["XN1G2"], io["XN1L"], cx.gtab, vscr,
                   io["maskx"], yout=io["Y1"])
    p.barrier()
    return p.close()


def build_launch4(core_in):
    p = Prog()
    nc = p.nc
    outs = [("OUT", [1024, TPC], F32)]
    io = _declare_io(nc, core_in, outs)
    cx = setup_common(p, io, big16_cols=NCT * ZW, xn_cols=1,
                      rows_cols=1024, scr_cols=3072, stc_cols=384,
                      itab_cols=8)
    cx.work16 = nc.alloc_sbuf_tensor("work16", [128, 27648], BF16)
    cx.qksb = nc.alloc_sbuf_tensor("qksb", [128, 1024], BF16)
    cx.gtab = nc.alloc_sbuf_tensor("gtab", [128, 70], mybir.dt.int32)
    vscr = nc.dram_tensor("vscr", [2048, 512], BF16).ap()
    p.open()
    ph_init(p, cx)
    ld = p.sem("ld_g4")
    p.dma("sync", cx.gtab[:, :], io["l1g"], ld)
    for e in ("vector", "scalar", "tensor", "gpsimd"):
        p.wait(e, ld)
    p.barrier()
    ph_l1ffn_layer(p, cx, 1, io["Y1G2"], io["Y1L"], cx.gtab, vscr,
                   io["maskx"], x2a=io["X2A"], fout=io["OUT"])
    p.barrier()
    return p.close()



def _double_rows(zg):
    import ml_dtypes
    d = zg.reshape(16384, 1024)
    return np.ascontiguousarray(
        np.concatenate([d, np.zeros((128, 1024), d.dtype)], 0))


def _sum_gather(sums):
    ident = np.tile(np.array([1.0, 0.0], np.float32), 16)[None, :]
    return np.ascontiguousarray(np.concatenate(
        sums + [np.broadcast_to(ident, (128, 32))], 0).astype(np.float32))



def kernel(**inputs):
    return kernel_fused(**inputs)


def kernel_multilaunch(**inputs):
    import ml_dtypes
    per_core = prep_inputs(inputs)
    maskx = _l1ffn_maskx().astype(ml_dtypes.bfloat16)
    for c in range(NC):
        per_core[c]["l0g"] = l0_gtab(c)
        per_core[c]["l1g"] = l1_gtab(c)
        per_core[c]["maskx"] = maskx
        per_core[c].update(l1_rots(c))
    nc1 = build_launch1(per_core[0])
    r1 = run_bass_kernel_spmd(nc1, per_core, core_ids=list(range(NC))).results
    KG = np.concatenate([r1[c]["K_OWN"] for c in range(NC)], 0)
    VG = np.concatenate([r1[c]["V_OWN"] for c in range(NC)], 0)
    KGX = np.concatenate([KG, np.zeros((128, 2048), ml_dtypes.bfloat16)], 0)
    KGX2 = np.ascontiguousarray(KGX.reshape(2304, 1024))
    VGX = np.ascontiguousarray(np.concatenate(
        [VG, np.zeros((128, 1024), ml_dtypes.bfloat16)], 0))

    m2 = []
    for c in range(NC):
        m = dict(per_core[c])
        m["KGX2"] = KGX2
        m["VGX"] = VGX
        m["Q_OWN"] = r1[c]["Q_OWN"]
        m["X1A"] = r1[c]["X1A"]
        m2.append(m)
    nc2 = build_launch2(m2[0])
    r2 = run_bass_kernel_spmd(nc2, m2, core_ids=list(range(NC))).results
    XN1G2 = _double_rows(np.concatenate([r2[c]["XN1"] for c in range(NC)], 0))
    SUMGX = _sum_gather([r2[c]["SUM"] for c in range(NC)])

    m3 = []
    for c in range(NC):
        m = dict(per_core[c])
        m["X1"] = r2[c]["X1"]
        m["LBUF"] = r2[c]["LBUF"]
        m["PBUF"] = r2[c]["PBUF"]
        m["SUMGX"] = SUMGX
        m["XN1G2"] = XN1G2
        m["XN1L"] = np.ascontiguousarray(XN1G2[:, 1023:1024])
        m3.append(m)
    nc3 = build_launch3(m3[0])
    r3 = run_bass_kernel_spmd(nc3, m3, core_ids=list(range(NC))).results
    Y1G2 = _double_rows(np.concatenate([r3[c]["Y1"] for c in range(NC)], 0))

    m4 = []
    for c in range(NC):
        m = dict(per_core[c])
        m["Y1G2"] = Y1G2
        m["Y1L"] = np.ascontiguousarray(Y1G2[:, 1023:1024])
        m["X2A"] = r3[c]["X2A"]
        m4.append(m)
    nc4 = build_launch4(m4[0])
    r4 = run_bass_kernel_spmd(nc4, m4, core_ids=list(range(NC))).results
    full = np.zeros((B, S, D), np.float32)
    for c in range(NC):
        out = r4[c]["OUT"]
        for (b, k, off) in chunks_of_core(c):
            full[b, k * CHUNK:(k + 1) * CHUNK] = out[:, off:off + CHUNK].T
    return full



def setup_fused(p: Prog, io):
    nc = p.nc
    cx = Ctx()
    A = nc.alloc_sbuf_tensor
    p.constpool = A("constpool", [128, len(CONSTS)], F32)
    for i, val in enumerate(CONSTS):
        nc.const_aps.register(F32, float(val), p.constpool[:, i:i + 1])
    cx.xres = A("xres", [128, NCT * TPC], F32)
    cx.wA = A("wA", [128, 12288], BF16)
    cx.wB = A("wB", [128, 4096], BF16)
    cx.scr = A("scr", [128, 3072], F32)
    cx.stc = A("stc", [128, 512], F32)
    rows = A("rows", [1, 2560], F32)
    cx.rows = rows
    cx.rrow = rows[:, 1024:2048]
    cx.ones_col = A("ones_col", [128, 1], F32)
    cx.ones_row = A("ones_row", [1, 128], F32)
    cx.ones_col_b = A("ones_col_b", [128, 1], BF16)
    cx.ones_row_b = A("ones_row_b", [1, 128], BF16)
    cx.identb = A("identb", [128, 128], BF16)
    cx.gtab_l1 = A("gtab_l1", [128, 70], mybir.dt.int32)
    ov = A("overlay", [128, 22080], F32)
    cx.identf = ov[:, 20188:20316]
    ovb = ov.bitcast(BF16)
    ovi = ov.bitcast(mybir.dt.int32)
    cx.xn = ovb[:, 0:16384]
    cx.big16_l0 = ovb[:, 16384:32768]
    cx.l0q = ovb[:, 32768:34816]
    cx.l0k = ovb[:, 34816:36864]
    cx.gcm = ovb[:, 36864:38912]
    cx.gtab_l0 = ovi[:, 19456:20188]
    cx.abuf = ov[:, 8192:10240]
    cx.attn_bring = [ov[:, 12288 + i * ICH: 12288 + (i + 1) * ICH]
                     for i in range(4)]
    cx.gbuf = ov[:, 10240:12288]
    cx.work16 = ovb[:, 0:27648]
    cx.big16_z = ovb[:, 27648:44160]
    cx.qksb = cx.xres.bitcast(BF16)[:, 24576:25600]
    cx.pA = nc.alloc_psum_tensor("pA", [128, 2048], F32)
    cx.pB = nc.alloc_psum_tensor("pB", [128, 1536], F32)
    cx.tp = nc.alloc_psum_tensor("tp", [128, 1024], BF16)
    cx.io = io
    return cx


def build_fused(core_in):
    p = Prog()
    nc = p.nc
    outs = [("OUT", [1024, TPC], F32)]
    io = _declare_io(nc, core_in, outs)
    cx = setup_fused(p, io)
    D = nc.dram_tensor
    kin = D("kin", [256, 1024], BF16).ap()
    vin = D("vin", [16 * 128, 1024], BF16).ap()
    kgx2 = D("KGX2i", [2304, 1024], BF16).ap()
    vgx = D("VGXi", [16512, 1024], BF16).ap()
    btiles = D("btiles", [4 * 128, ICH], F32).ap()
    xn1_in = D("xn1_in", [2048, 1024], BF16).ap()
    xn1l_in = D("xn1l_in", [2048, 1], BF16).ap()
    xn1g2 = D("XN1G2i", [16512, 1024], BF16).ap()
    xn1lw = D("XN1Li", [16512, 1], BF16).ap()
    sum_in = D("sum_in", [128, 32], F32).ap()
    sumgx = D("SUMGXi", [1152, 32], F32).ap()
    lbuf = D("lbufi", [1024, TPC], F32).ap()
    pbuf = D("pbufi", [1024, TPC], F32).ap()
    x2a_d = D("x2a_d", [1024, TPC], F32).ap()
    y1_in = D("y1_in", [2048, 1024], BF16).ap()
    y1l_in = D("y1l_in", [2048, 1], BF16).ap()
    y1g2 = D("Y1G2i", [16512, 1024], BF16).ap()
    y1lw = D("Y1Li", [16512, 1], BF16).ap()
    vscr = D("vscr", [2048, 512], BF16).ap()
    p.open()
    cx.big16 = cx.big16_l0
    cx.gtab = cx.gtab_l0
    ph_init(p, cx)
    zp = p.sem("zp")

    def fz(v):
        v.memset(cx.xn[:, 0:1024], 0.0)
        sv = cx.scr[:, 0:32].rearrange("p (a b) -> p a b", b=2)
        v.memset(sv[:, :, 0:1], 1.0)
        v.memset(sv[:, :, 1:2], 0.0)
    p.v(fz)
    p.di("vector", zp)
    p.wait("sync", zp)
    p.dma("sync", kgx2[2048:2176, :], cx.xn[:, 0:1024], zp)
    p.dma("sync", kgx2[2176:2304, :], cx.xn[:, 0:1024], zp)
    p.dma("sync", vgx[16384:16512, :], cx.xn[:, 0:1024], zp)
    p.dma("sync", xn1g2[16384:16512, :], cx.xn[:, 0:1024], zp)
    p.dma("sync", y1g2[16384:16512, :], cx.xn[:, 0:1024], zp)
    p.dma("sync", xn1lw[16384:16512, :], cx.xn[:, 0:1], zp, slow_ok=True)
    p.dma("sync", y1lw[16384:16512, :], cx.xn[:, 0:1], zp, slow_ok=True)
    p.dma("sync", sumgx[1024:1152, :], cx.scr[:, 0:32], zp)
    p.wait("sync", zp)
    p.barrier()
    ph_load_x(p, cx)
    ph_layernorm(p, cx, "ln0")
    ph_l0_qkv(p, cx)
    sts = p.sem("stf")
    kv = AP(kin.tensor, 0, [[2048, 128], [1, 2048]])
    p.dma("sync", kv, cx.l0k[:, :], sts)
    for tt in range(16):
        p.dma("sync", vin[tt * 128:(tt + 1) * 128, :],
              cx.vtm[:, tt * 1024:(tt + 1) * 1024], sts)
    p.wait("sync", sts)
    p.barrier()
    cc1 = p.sem("cc1")

    def fc1(g):
        g.collective_compute("AllGather", ALU.bypass,
                             replica_groups=[list(range(NC))],
                             ins=[kin.opt()], outs=[kgx2[0:2048, :].opt()]
                             ).then_inc(cc1.h, 1)
        g.collective_compute("AllGather", ALU.bypass,
                             replica_groups=[list(range(NC))],
                             ins=[vin.opt()], outs=[vgx[0:16384, :].opt()]
                             ).then_inc(cc1.h, 1)
    p.g(fc1)
    cc1.n += 2
    ph_l0_gates(p, cx)
    ph_l0_moe(p, cx)
    p.wait("gpsimd", cc1)
    p.barrier()
    ph_l0_attn_u(p, cx, kgx2, vgx, btiles, io["l0g"])
    ph_layernorm(p, cx, "ln1")
    xv = AP(xn1_in.tensor, 0, [[2048, 1024], [1, 2048]])
    for ct in range(NCT):
        p.dma("sync", xv[ct * 128:(ct + 1) * 128, :],
              cx.xn[:, ct * TPC:(ct + 1) * TPC], sts)
    for ct in range(NCT):
        for sl in range(2):
            p.dma("sync",
                  AP(xn1l_in.tensor, ct * 256 + sl, [[2, 128], [1, 1]]),
                  cx.xn[:, ct * TPC + sl * 1024 + 1023:
                        ct * TPC + sl * 1024 + 1024], sts, slow_ok=True)
    p.wait("sync", sts)
    p.wait("gpsimd", sts)
    cc2 = p.sem("cc2")

    def fc2a(g):
        g.collective_compute("AllGather", ALU.bypass,
                             replica_groups=[list(range(NC))],
                             ins=[xn1_in.opt()],
                             outs=[xn1g2[0:16384, :].opt()]).then_inc(cc2.h, 1)
        g.collective_compute("AllGather", ALU.bypass,
                             replica_groups=[list(range(NC))],
                             ins=[xn1l_in.opt()],
                             outs=[xn1lw[0:16384, :].opt()]).then_inc(cc2.h, 1)
    p.g(fc2a)
    cc2.n += 2
    ph_l1_proj_scan(p, cx, lbuf, pbuf, sum_in)

    def fc2b(g):
        g.collective_compute("AllGather", ALU.bypass,
                             replica_groups=[list(range(NC))],
                             ins=[sum_in.opt()],
                             outs=[sumgx[0:1024, :].opt()]).then_inc(cc2.h, 1)
    p.g(fc2b)
    cc2.n += 1
    p.wait("gpsimd", cc2)
    p.barrier()
    cx.big16 = cx.big16_z
    cx.gtab = cx.gtab_l1
    gl = p.sem("gl1")
    p.dma("sync", cx.gtab[:, :], io["l1g"], gl)
    for e in ("vector", "scalar", "tensor", "gpsimd"):
        p.wait(e, gl)
    ph_l1_apply(p, cx, lbuf, pbuf, sumgx, cx.gtab)
    for ct in range(NCT):
        p.dma("sync", x2a_d[ct * 128:(ct + 1) * 128, :],
              cx.xres[:, ct * TPC:(ct + 1) * TPC], sts)
    p.wait("sync", sts)
    p.barrier()
    yv = AP(y1_in.tensor, 0, [[2048, 1024], [1, 2048]])
    ph_l1ffn_layer(p, cx, 0, xn1g2, xn1lw, cx.gtab, vscr, io["maskx"],
                   yout=yv, ylout=y1l_in)
    cc3 = p.sem("cc3")

    def fc3(g):
        g.collective_compute("AllGather", ALU.bypass,
                             replica_groups=[list(range(NC))],
                             ins=[y1_in.opt()],
                             outs=[y1g2[0:16384, :].opt()]).then_inc(cc3.h, 1)
        g.collective_compute("AllGather", ALU.bypass,
                             replica_groups=[list(range(NC))],
                             ins=[y1l_in.opt()],
                             outs=[y1lw[0:16384, :].opt()]).then_inc(cc3.h, 1)
    p.g(fc3)
    cc3.n += 2
    p.wait("gpsimd", cc3)
    p.barrier()
    ph_l1ffn_layer(p, cx, 1, y1g2, y1lw, cx.gtab, vscr, io["maskx"],
                   x2a=x2a_d, fout=io["OUT"])
    p.barrier()
    return p.close()


def kernel_fused(**inputs):
    import ml_dtypes
    per_core = prep_inputs(inputs)
    maskx = _l1ffn_maskx().astype(ml_dtypes.bfloat16)
    for c in range(NC):
        per_core[c]["l0g"] = l0_gtab(c)
        per_core[c]["l1g"] = l1_gtab(c)
        per_core[c]["maskx"] = maskx
        per_core[c].update(l1_rots(c))
    ncf = build_fused(per_core[0])
    rr = run_bass_kernel_spmd(ncf, per_core, core_ids=list(range(NC))).results
    full = np.zeros((B, S, D), np.float32)
    for c in range(NC):
        out = rr[c]["OUT"]
        for (b, k, off) in chunks_of_core(c):
            full[b, k * CHUNK:(k + 1) * CHUNK] = out[:, off:off + CHUNK].T
    return full


# version 11 (final)
# speedup vs baseline: 1.3151x; 1.0016x over previous
import math
import numpy as np

import concourse.bass as bass
import concourse.mybir as mybir
from concourse.ap import AP
from concourse.bass_core_rust import IndirectOffsetOnAxis
from concourse.bass_utils import run_bass_kernel_spmd

F32 = mybir.dt.float32
BF16 = mybir.dt.bfloat16
AF = mybir.ActivationFunctionType
ALU = mybir.AluOpType
AXX = mybir.AxisListType.X

B, S, D = 2, 8192, 1024
NC = 8
CHUNK = 1024
TPC = 2 * CHUNK
NCT = D // 128

DIM_QK = 128
NUM_BUCKETS = 32
MAX_DISTANCE = 128
LAP_MU = math.sqrt(0.5)
LAP_STD = math.sqrt(0.25 * math.pi)
K1S = (1.0 / S) / (LAP_STD * math.sqrt(2.0))
KB1 = math.sqrt(DIM_QK) / (LAP_STD * math.sqrt(2.0))
KB0 = -LAP_MU / (LAP_STD * math.sqrt(2.0))
MASK_SENTINEL = -2000.0

NEXP = 8
HEXP = 512

C_REC = 8.0

HEADS = 8
DHEAD = 64
WINDOW = 512
INNER = 512
HID = 2730
HIDP = 2816
QK_SCALE = 8.0

ICH = 256


def chunks_of_core(c):
    if c <= 3:
        return [(0, c, 0), (1, 7 - c, CHUNK)]
    return [(1, 7 - c, 0), (0, c, CHUNK)]


def owner_of_chunk(b, k):
    core = k if b == 0 else 7 - k
    return (core, 0 if k <= 3 else CHUNK)



def _rel_bucket(d):
    d = np.asarray(d)
    me = NUM_BUCKETS // 2
    nf = np.maximum(d, 1).astype(np.float32)
    val = me + (np.log(nf / me) / math.log(MAX_DISTANCE / me)
                * (NUM_BUCKETS - me)).astype(np.int32)
    val = np.minimum(val, NUM_BUCKETS - 1)
    return np.where(d < me, d, val).astype(np.int64)


def _toeplitz_gather(rel_emb):
    out = []
    for D0 in (-128, 0, 128):
        pi = np.arange(128)[:, None]
        fi = np.arange(ICH)[None, :]
        d = fi - pi + D0
        t = np.where(d >= 0, rel_emb[_rel_bucket(np.maximum(d, 0)), 0],
                     np.float32(MASK_SENTINEL))
        out.append(t)
    return np.ascontiguousarray(np.stack(out), np.float32)


def _xpos_tables():
    half = np.arange(0, DHEAD, 2, dtype=np.float64)
    inv_freq = 1.0 / 10000.0 ** (half / DHEAD)
    t = np.arange(S, dtype=np.float64)
    freqs = t[:, None] * inv_freq[None, :]
    freqs = np.concatenate([freqs, freqs], -1)
    base = (half + 0.4 * DHEAD) / (1.4 * DHEAD)
    power = (t - S // 2) / float(WINDOW // 2)
    scale = base[None, :] ** power[:, None]
    scale = np.concatenate([scale, scale], -1)
    cos, sin = np.cos(freqs), np.sin(freqs)
    r = (cos * scale, sin * scale, cos / scale, sin / scale)
    return [np.ascontiguousarray(x, np.float32) for x in r]


def _l1ffn_masks():
    out = np.zeros((8, 128, WINDOW), np.float32)
    di = np.arange(128)[:, None]
    dj = np.arange(WINDOW)[None, :]
    for oi, o in enumerate((0, 128, 256, 384)):
        out[oi * 2 + 0] = np.where(dj >= o + di, 0.0, -1e9)
        out[oi * 2 + 1] = np.where(dj <= o + di, 0.0, -1e9)
    return out


def prep_inputs(inputs):
    x = np.asarray(inputs["L0_attn_x"], dtype=np.float32)
    cs_q, sn_q, cs_k, sn_k = _xpos_tables()
    f32 = lambda a: np.ascontiguousarray(a, np.float32)
    shared = {
        "Wqk": inputs["L0_attn_Wqk"],
        "bqk": np.reshape(inputs["L0_attn_bqk"], (128, 1)),
        "gamma": inputs["L0_attn_gamma"],
        "beta": inputs["L0_attn_beta"],
        "Wv": inputs["L0_attn_Wv"],
        "bv": np.reshape(inputs["L0_attn_bv"], (1, 1024)),
        "toe": _toeplitz_gather(np.asarray(inputs["L0_attn_rel_emb"],
                                           np.float32)),
        "gate_W": inputs["L0_ffn_gate_W"],
        "W1": inputs["L0_ffn_W1"],
        "b1": inputs["L0_ffn_b1"],
        "W2": inputs["L0_ffn_W2"],
        "b2": inputs["L0_ffn_b2"],
        "Ws1": inputs["L0_ffn_Ws1"],
        "bs1": np.reshape(inputs["L0_ffn_bs1"], (1, 512)),
        "Ws2": inputs["L0_ffn_Ws2"],
        "bs2": np.reshape(inputs["L0_ffn_bs2"], (1024, 1)),
        "route_scale": np.reshape(inputs["L0_ffn_route_scale"], (1, 1)),
        "WinT": np.asarray(inputs["L1_attn_W_in"]).T,
        "WigT": np.asarray(inputs["L1_attn_W_ig"]).T,
        "b_ig": np.reshape(inputs["L1_attn_b_ig"], (NCT, 128)).T,
        "WrgT": np.asarray(inputs["L1_attn_W_rg"]).T,
        "b_rg": np.reshape(inputs["L1_attn_b_rg"], (NCT, 128)).T,
        "lam": np.reshape(inputs["L1_attn_lam"], (NCT, 128)).T,
        "WoutT": np.asarray(inputs["L1_attn_W_out"]).T,
        "h0": np.reshape(inputs["L1_attn_h0"], (NCT, 128)).T,
        "norm_w": np.reshape(inputs["L1_attn_norm_w"], (NCT, 128)).T,
        "wqkv": inputs["L1_ffn_wqkv"],
        "qs_t": np.tile(np.asarray(inputs["L1_ffn_q_scale"]), (1, HEADS)
                        ).reshape(2, 1, INNER),
        "ks_t": np.tile(np.asarray(inputs["L1_ffn_k_scale"]), (1, HEADS)
                        ).reshape(2, 1, INNER),
        "wo": inputs["L1_ffn_wo"],
        "ln_g": np.transpose(np.reshape(inputs["L1_ffn_ln_g"],
                                        (2, NCT, 128)), (0, 2, 1)),
        "ln_b": np.transpose(np.reshape(inputs["L1_ffn_ln_b"],
                                        (2, NCT, 128)), (0, 2, 1)),
        "b2f": np.transpose(np.reshape(inputs["L1_ffn_b2"],
                                       (2, NCT, 128)), (0, 2, 1)),
        "masks": _l1ffn_masks(),
        "ident": np.eye(128, dtype=np.float32),
        "cs_q": cs_q, "sn_q": sn_q, "cs_k": cs_k, "sn_k": sn_k,
    }
    w1 = np.asarray(inputs["L1_ffn_w1"], np.float32)
    b1 = np.asarray(inputs["L1_ffn_b1"], np.float32)
    w1p = np.zeros((2, 1024, 2 * HIDP), np.float32)
    b1p = np.zeros((2, 2 * HIDP), np.float32)
    w1p[:, :, :HID] = w1[:, :, :HID]
    w1p[:, :, HIDP:HIDP + HID] = w1[:, :, HID:]
    b1p[:, :HID] = b1[:, :HID]
    b1p[:, HIDP:HIDP + HID] = b1[:, HID:]
    w2p = np.zeros((2, HIDP, 1024), np.float32)
    w2p[:, :HID] = np.asarray(inputs["L1_ffn_w2"], np.float32)
    shared["w1p"] = w1p
    shared["b1p"] = np.transpose(b1p.reshape(2, 2 * HIDP // 128, 128),
                                 (0, 2, 1))
    shared["w2p"] = w2p
    shared = {k: f32(v) for k, v in shared.items()}

    per_core = []
    for c in range(NC):
        cols = [x[b, k * CHUNK:(k + 1) * CHUNK, :].T
                for (b, k, _o) in chunks_of_core(c)]
        m = dict(shared)
        m["x_cm"] = f32(np.concatenate(cols, axis=1))
        per_core.append(m)
    return per_core



class Sem:
    __slots__ = ("h", "n")

    def __init__(self, h):
        self.h = h
        self.n = 0


CONSTS = [0.0, 1.0, -1.0, 0.5, -0.5, 1e-5, 1e-6, 1e-8, 1.0 / 1024.0,
          K1S, KB1, KB0, 1.000001, QK_SCALE, 1e-24, -1e9, 1e9, 2.0,
          1.0 / 512.0, -1e4]


class Prog:
    def __init__(self, partition_id=False):
        self.nc = bass.Bass("TRN2", target_bir_lowering=False, debug=False,
                            enable_partition_id=partition_id)
        self._sems = {}
        self._block_cm = None
        self.b = None
        self.rec = False
        self._treg = {}
        self.in_switch = False
        self._pid = {}

    def sem(self, name):
        if name not in self._sems:
            self._sems[name] = Sem(self.nc.alloc_semaphore(name))
        return self._sems[name]

    def open(self, want_pid=False):
        self._block_cm = self.nc.Block()
        self.b = self._block_cm.__enter__()
        cp = self.constpool
        cinit = self.sem("cinit")

        def _(v):
            for i, val in enumerate(CONSTS):
                ins = v.memset(cp[:, i:i + 1], val)
            ins.then_inc(cinit.h, 1)
        self.b.vector(_)
        cinit.n = 1
        for e in ("sync", "gpsimd", "scalar", "tensor"):
            self.wait(e, cinit)
        if want_pid:
            for e in ("sync", "gpsimd", "vector", "scalar", "tensor"):
                def f(eng, e=e):
                    self._pid[e] = eng.partition_id()
                getattr(self.b, e)(f)

    def close(self):
        self._block_cm.__exit__(None, None, None)
        return self.nc

    _ENG = {"sync": "sync", "gpsimd": "gpsimd", "vector": "vector",
            "scalar": "scalar", "tensor": "tensor"}

    def _emit(self, eng, f):
        if self.rec:
            return
        if self.in_switch:
            if eng == self._switch_eng:
                f(getattr(self.nc, eng))
        else:
            getattr(self.b, eng)(f)

    def t(self, f):
        self._emit("tensor", f)

    def v(self, f):
        self._emit("vector", f)

    def s(self, f):
        self._emit("scalar", f)

    def g(self, f):
        self._emit("gpsimd", f)

    def y(self, f):
        self._emit("sync", f)

    def wait(self, eng, sem, val=None):
        val = sem.n if val is None else val
        self._emit(eng, lambda e: e.wait_ge(sem.h, val))

    def lag_wait(self, eng, sem, lag):
        tgt = sem.n - lag
        if tgt > 0:
            self._emit(eng, lambda e: e.wait_ge(sem.h, tgt))

    def di(self, eng, sem, k=1):
        sem.n += k
        self._emit(eng, lambda e: e.drain().then_inc(sem.h, k))

    def tdma(self, eng, out, ten, dims, tabsl, sem, slow_ok=False):
        sem.n += 16

        def f(e, out=out, ten=ten, dims=dims, tabsl=tabsl, slow_ok=slow_ok,
              eng=eng):
            if eng not in self._treg:
                g = e.register("tdma_off_" + eng)
                self._treg[eng] = (g, g.__enter__())
            reg = self._treg[eng][1]
            e.reg_load(reg, tabsl)
            e.dma_start(out=out, in_=AP(ten, reg, [list(d) for d in dims]),
                        allow_slow_non_contiguous=slow_ok).then_inc(sem.h, 16)
        self._emit(eng, f)

    def gdma(self, out, in_dram, idx_col, sem):
        sem.n += 16
        self._emit("gpsimd",
                   lambda e: e.indirect_dma_start(
                       out, None, in_dram,
                       IndirectOffsetOnAxis(idx_col, 0)).then_inc(sem.h, 16))

    def dma(self, eng, out, in_, sem, slow_ok=False):
        sem.n += 16
        self._emit(eng,
                   lambda e: e.dma_start(out=out, in_=in_,
                                         allow_slow_non_contiguous=slow_ok
                                         ).then_inc(sem.h, 16))

    def barrier(self, engines=("sync", "gpsimd", "vector", "scalar",
                               "tensor")):
        s = self.sem("phb")
        s.n += len(engines)
        tgt = s.n
        for e in engines:
            def f(eng, s=s, tgt=tgt, e=e):
                if e in ("vector", "scalar"):
                    eng.drain().then_inc(s.h, 1)
                else:
                    eng.nop().then_inc(s.h, 1)
                eng.wait_ge(s.h, tgt)
            self._emit(e, f)

    def switch8(self, body_fn):
        assert not self.rec and not self.in_switch
        base = {k: s.n for k, s in self._sems.items()}

        def reset():
            for k, s in self._sems.items():
                s.n = base.get(k, 0)
            for k, s in self._sems.items():
                if k not in base:
                    s.n = 0

        deltas = []
        for core in range(NC):
            self.rec = True
            body_fn(core)
            self.rec = False
            deltas.append({k: s.n - base.get(k, 0)
                           for k, s in self._sems.items()})
            reset()
        quota = {}
        for d in deltas:
            for k, v in d.items():
                quota[k] = max(quota.get(k, 0), v)
        self.in_switch = True
        for e in ("sync", "gpsimd", "vector", "scalar", "tensor"):
            self._switch_eng = e
            eh = getattr(self.nc, e)
            for core in self.nc.Switch(engines=[eh], index=[self._pid[e]],
                                       n=NC):
                reset()
                body_fn(core)
                if e == "sync":
                    pads = {k: quota[k] - (self._sems[k].n - base.get(k, 0))
                            for k in quota}
                    for k, v in pads.items():
                        if v > 0:
                            eh.sem_inc(self._sems[k].h, v)
            reset()
        self.in_switch = False
        self._switch_eng = None
        for k, v in quota.items():
            self._sems[k].n = base.get(k, 0) + v


class Ctx:
    pass


def setup_common(p: Prog, io, big16_cols=16 * 1024, xn_cols=NCT * TPC, rows_cols=2560, scr_cols=3072, stc_cols=512, itab_cols=1040):
    nc = p.nc
    cx = Ctx()
    A = nc.alloc_sbuf_tensor
    p.constpool = A("constpool", [128, len(CONSTS)], F32)
    for i, val in enumerate(CONSTS):
        nc.const_aps.register(F32, float(val), p.constpool[:, i:i + 1])
    cx.xres = A("xres", [128, NCT * TPC], F32)
    cx.xn = A("xn", [128, xn_cols], BF16)
    cx.big16 = A("big16", [128, big16_cols], BF16)
    cx.wA = A("wA", [128, 12288], BF16)
    cx.wB = A("wB", [128, 4096], BF16)
    cx.scr = A("scr", [128, scr_cols], F32)
    cx.stc = A("stc", [128, stc_cols], F32)
    cx.rows = A("rows", [1, rows_cols], F32)
    cx.rrow = A("rrow", [1, 1024], F32)
    cx.ones_col = A("ones_col", [128, 1], F32)
    cx.ones_row = A("ones_row", [1, 128], F32)
    cx.ones_col_b = A("ones_col_b", [128, 1], BF16)
    cx.ones_row_b = A("ones_row_b", [1, 128], BF16)
    cx.identb = A("identb", [128, 128], BF16)
    cx.itab = A("itab", [1, itab_cols], mybir.dt.int32)
    cx.identf = A("identf", [128, 128], F32)
    cx.pA = nc.alloc_psum_tensor("pA", [128, 2048], F32)
    cx.pB = nc.alloc_psum_tensor("pB", [128, 1536], F32)
    cx.tp = nc.alloc_psum_tensor("tp", [128, 1024], BF16)
    cx.io = io
    return cx


def ph_init(p, cx):
    ld = p.sem("ld_init")
    p.dma("gpsimd", cx.identb[:, :], cx.io["ident"], ld)
    p.dma("sync", cx.identf[:, :], cx.io["ident"], ld)

    def _(v):
        v.memset(cx.ones_col[:, :], 1.0)
        v.memset(cx.ones_row[:, :], 1.0)
        v.memset(cx.ones_col_b[:, :], 1.0)
        v.memset(cx.ones_row_b[:, :], 1.0)
    p.v(_)
    p.wait("tensor", ld)
    p.barrier()


def ph_load_x(p, cx):
    ld = p.sem("ld_x")
    for ct in range(NCT):
        p.dma("sync", cx.xres[:, ct * TPC:(ct + 1) * TPC],
              cx.io["x_cm"][ct * 128:(ct + 1) * 128, :], ld)
    for e in ("vector", "scalar", "tensor"):
        p.wait(e, ld)
    p.barrier()


def ph_layernorm(p, cx, tag):
    mm = p.sem("mm_" + tag)
    ev = p.sem("ev_" + tag)
    for tch in range(4):
        t0 = tch * 512
        for ct in range(NCT):
            xt = cx.xres[:, ct * TPC + t0: ct * TPC + t0 + 512]
            xbt = cx.big16[:, ct * 512:(ct + 1) * 512]
            sqt = cx.wA[:, ct * 512:(ct + 1) * 512]
            last = ct == NCT - 1

            def fv(v, xt=xt, xbt=xbt):
                v.tensor_copy(xbt, xt)

            def fs(s_, xt=xt, sqt=sqt):
                s_.square(sqt, xt)
            p.v(fv)
            p.s(fs)
        p.di("vector", ev)
        p.di("scalar", ev)
        p.wait("tensor", ev)
        for ct in range(NCT):
            xbt = cx.big16[:, ct * 512:(ct + 1) * 512]
            sqt = cx.wA[:, ct * 512:(ct + 1) * 512]
            first, last = ct == 0, ct == NCT - 1

            def ft(t, xbt=xbt, sqt=sqt, first=first, last=last):
                t.matmul(cx.pB[0:1, 0:512], cx.ones_col_b[:, :], xbt,
                         start=first, stop=last)
                i = t.matmul(cx.pB[0:1, 512:1024], cx.ones_col_b[:, :], sqt,
                             start=first, stop=last)
                if last:
                    i.then_inc(mm.h, 1)
            p.t(ft)
        mm.n += 1
        p.wait("vector", mm)
        mu, msq, var = (cx.rows[:, 0:512], cx.rows[:, 512:1024],
                        cx.rows[:, 1024:1536])
        Arow, Brow = cx.rows[:, 1536:2048], cx.rows[:, 2048:2560]

        def f3(v):
            v.tensor_scalar_mul(mu, cx.pB[0:1, 0:512], 1.0 / 1024.0)
            v.tensor_mul(msq, mu, mu)
            v.tensor_scalar_mul(var, cx.pB[0:1, 512:1024], 1.0 / 1024.0)
            v.tensor_sub(var, var, msq)
        p.v(f3)
        p.di("vector", ev)
        p.wait("scalar", ev)

        def f4(s_):
            s_.activation(var, var, AF.Ln, bias=1e-5, scale=1.0)
        p.s(f4)
        p.di("scalar", ev)
        p.wait("scalar", ev)

        def f4b(s_):
            s_.activation(var, var, AF.Exp, bias=0.0, scale=-0.5)
        p.s(f4b)
        p.di("scalar", ev)
        p.wait("vector", ev)

        def f5(v):
            v.tensor_copy(Arow, var)
            v.tensor_mul(Brow, mu, var)
            v.tensor_scalar_mul(Brow, Brow, -1.0)
        p.v(f5)
        p.di("vector", ev)
        p.wait("tensor", ev)

        def f6(t):
            t.matmul(cx.pB[:, 0:512], cx.ones_row[:, :], Arow,
                     start=True, stop=True)
            t.matmul(cx.pB[:, 512:1024], cx.ones_row[:, :], Brow,
                     start=True, stop=True).then_inc(mm.h, 1)
        p.t(f6)
        mm.n += 1
        p.wait("vector", mm)
        for ct in range(NCT):
            xt = cx.xres[:, ct * TPC + t0: ct * TPC + t0 + 512]
            xnt = cx.xn[:, ct * TPC + t0: ct * TPC + t0 + 512]
            tmp = cx.scr[:, 0:512]
            last = ct == NCT - 1

            def f7(v, xt=xt, xnt=xnt, tmp=tmp):
                v.tensor_mul(tmp, xt, cx.pB[:, 0:512])
                v.tensor_add(xnt, tmp, cx.pB[:, 512:1024])
            p.v(f7)
        p.di("vector", ev)
        p.wait("tensor", ev)
    p.barrier()


def ph_l0_qkv(p, cx):
    nc = p.nc
    A = nc.alloc_sbuf_tensor
    if getattr(cx, "l0q", None) is None:
        cx.l0q = A("l0q", [128, TPC], BF16)
    if getattr(cx, "l0k", None) is None:
        cx.l0k = A("l0k", [128, TPC], BF16)
    cx.vtm = cx.big16
    ld = p.sem("ld_qkv")
    mm = p.sem("mm_qkv")
    ev = p.sem("ev_qkv")
    for kt in range(NCT):
        p.dma("gpsimd", cx.wB[:, kt * 128:(kt + 1) * 128],
              cx.io["Wqk"][kt * 128:(kt + 1) * 128, :], ld)
    bqk = cx.stc[:, 0:1]
    gam0, bet0 = cx.stc[:, 1:2], cx.stc[:, 2:3]
    gam1, bet1 = cx.stc[:, 3:4], cx.stc[:, 4:5]
    p.dma("sync", bqk, cx.io["bqk"], ld)
    p.dma("sync", gam0, cx.io["gamma"][0:1, :].rearrange("a b -> b a"), ld)
    p.dma("sync", gam1, cx.io["gamma"][1:2, :].rearrange("a b -> b a"), ld)
    p.dma("sync", bet0, cx.io["beta"][0:1, :].rearrange("a b -> b a"), ld)
    p.dma("sync", bet1, cx.io["beta"][1:2, :].rearrange("a b -> b a"), ld)
    for e in ("tensor", "scalar", "vector"):
        p.wait(e, ld)
    ev_si = p.sem("ev_qk_si")
    ev_af = p.sem("ev_qk_af")
    for tch in range(4):
        t0 = tch * 512
        dst = cx.pB[:, (tch % 2) * 512:(tch % 2) * 512 + 512]
        p.lag_wait("tensor", ev_si, 1)

        def ft(t, dst=dst, t0=t0):
            for kt in range(NCT):
                i = t.matmul(dst, cx.wB[:, kt * 128:(kt + 1) * 128],
                             cx.xn[:, kt * TPC + t0: kt * TPC + t0 + 512],
                             start=(kt == 0), stop=(kt == NCT - 1))
            i.then_inc(mm.h, 1)
        p.t(ft)
        mm.n += 1
        p.wait("scalar", mm)
        p.lag_wait("scalar", ev_af, 1)
        pre = cx.scr[:, (tch % 2) * 512:(tch % 2) * 512 + 512]

        def fs(s_, dst=dst, pre=pre):
            s_.activation(pre, dst, AF.Silu, bias=bqk, scale=1.0)
        p.s(fs)
        p.di("scalar", ev_si)
        p.wait("vector", ev_si)

        def fv(v, pre=pre, t0=t0):
            v.tensor_scalar(cx.l0q[:, t0:t0 + 512], pre, gam0, bet0,
                            ALU.mult, ALU.add)
            v.tensor_scalar(cx.l0k[:, t0:t0 + 512], pre, gam1, bet1,
                            ALU.mult, ALU.add)
        p.v(fv)
        p.di("vector", ev_af)
    p.barrier()

    for kt in range(NCT):
        p.dma("gpsimd", cx.wA[:, kt * 1024:(kt + 1) * 1024],
              cx.io["Wv"][kt * 128:(kt + 1) * 128, :], ld)
    bvrow = cx.rrow[:, 0:1024]
    p.dma("sync", bvrow, cx.io["bv"], ld)
    for e in ("tensor", "vector"):
        p.wait(e, ld)
    bvb = cx.scr[:, 1024:2048]

    def ftb(t):
        t.matmul(cx.pB[:, 0:512], cx.ones_row[:, :], bvrow[:, 0:512],
                 start=True, stop=True)
        t.matmul(cx.pB[:, 512:1024], cx.ones_row[:, :], bvrow[:, 512:1024],
                 start=True, stop=True).then_inc(mm.h, 1)
    p.t(ftb)
    mm.n += 1
    p.wait("vector", mm)

    def fvb(v):
        v.tensor_copy(bvb[:, 0:512], cx.pB[:, 0:512])
        v.tensor_copy(bvb[:, 512:1024], cx.pB[:, 512:1024])
    p.v(fvb)
    p.di("vector", ev)
    p.wait("tensor", ev)
    ev_ad = p.sem("ev_v_ad")
    ev_vs = p.sem("ev_v_si")
    for it in range(32):
        tt, nh = it // 2, it % 2
        dst = cx.pA[:, (it % 2) * 512:(it % 2) * 512 + 512]
        p.lag_wait("tensor", ev_ad, 1)

        def ft(t, dst=dst, tt=tt, nh=nh):
            for ct in range(NCT):
                i = t.matmul(
                    dst,
                    cx.xn[:, ct * TPC + tt * 128: ct * TPC + (tt + 1) * 128],
                    cx.wA[:, ct * 1024 + nh * 512: ct * 1024 + nh * 512 + 512],
                    start=(ct == 0), stop=(ct == NCT - 1))
            i.then_inc(mm.h, 1)
        p.t(ft)
        mm.n += 1
        p.wait("vector", mm)
        p.lag_wait("vector", ev_vs, 1)
        tmp = cx.scr[:, (it % 2) * 512:(it % 2) * 512 + 512]

        def fv(v, dst=dst, tmp=tmp, nh=nh):
            v.tensor_add(tmp, dst, bvb[:, nh * 512:nh * 512 + 512])
        p.v(fv)
        p.di("vector", ev_ad)
        p.wait("scalar", ev_ad)
        vdst = cx.vtm[:, tt * 1024 + nh * 512: tt * 1024 + nh * 512 + 512]

        def fs(s_, tmp=tmp, vdst=vdst):
            s_.activation(vdst, tmp, AF.Silu, bias=0.0, scale=1.0)
        p.s(fs)
        p.di("scalar", ev_vs)
    p.barrier()


def ph_l0_gates(p, cx):
    nc = p.nc
    if getattr(cx, "gcm", None) is None:
        cx.gcm = nc.alloc_sbuf_tensor("gcm", [8, TPC], BF16)
    ld = p.sem("ld_g")
    mm = p.sem("mm_g")
    ev = p.sem("ev_g")
    for ct in range(NCT):
        p.dma("gpsimd", cx.wB[:, 2048 + ct * 8: 2048 + ct * 8 + 8],
              cx.io["gate_W"][ct * 128:(ct + 1) * 128, :], ld)
    rs_row = cx.rrow[:, 0:1]
    p.dma("sync", rs_row, cx.io["route_scale"], ld)
    p.wait("tensor", ld)

    def ft(t):
        t.matmul(cx.pB[:, 1024:1025], cx.ones_row[:, :], rs_row,
                 start=True, stop=True).then_inc(mm.h, 1)
    p.t(ft)
    mm.n += 1
    p.wait("vector", mm)
    rs_col = cx.stc[:, 5:6]

    def fv0(v):
        v.tensor_copy(rs_col, cx.pB[:, 1024:1025])
    p.v(fv0)
    p.di("vector", ev)
    p.wait("tensor", ev)
    for tt in range(16):
        for ct in range(NCT):
            first, last = ct == 0, ct == NCT - 1

            def ft(t, tt=tt, ct=ct, first=first, last=last):
                i = t.matmul(
                    cx.pB[:, 1024 + tt * 8: 1024 + tt * 8 + 8],
                    cx.xn[:, ct * TPC + tt * 128: ct * TPC + (tt + 1) * 128],
                    cx.wB[:, 2048 + ct * 8: 2048 + ct * 8 + 8],
                    start=first, stop=last)
                if last and tt == 15:
                    i.then_inc(mm.h, 1)
            p.t(ft)
    mm.n += 1
    p.wait("scalar", mm)
    sgm = cx.stc[:, 16:144]

    def fs(s_):
        s_.activation(sgm, cx.pB[:, 1024:1152], AF.Sigmoid, bias=0.0,
                      scale=1.0)
    p.s(fs)
    p.di("scalar", ev)
    p.wait("vector", ev)
    m1 = cx.stc[:, 144:160]
    m2 = cx.stc[:, 160:176]
    rec = cx.stc[:, 176:192]
    den = cx.stc[:, 448:464]
    ind = cx.stc[:, 192:320]
    gates = cx.stc[:, 320:448]
    sg3 = sgm.rearrange("p (t e) -> p t e", e=8)
    ind3 = ind.rearrange("p (t e) -> p t e", e=8)
    g3 = gates.rearrange("p (t e) -> p t e", e=8)

    def fv2(v):
        v.tensor_scalar_mul(sgm, sgm, rs_col)
        v.drain()
        v.tensor_reduce(m1, sg3, AXX, ALU.max)
        v.drain()
        m1b = m1.rearrange("p (t o) -> p t o", o=1).broadcast_to([128, 16, 8])
        v.tensor_tensor(ind3, sg3, m1b, ALU.is_ge)
        v.drain()
        v.scalar_tensor_tensor(ind3, ind3, 1e9, sg3, ALU.mult, ALU.subtract)
        v.drain()
        v.tensor_reduce(m2, ind3, AXX, ALU.min, negate=True)
        v.drain()
        v.tensor_add(den, m1, m2)
        v.drain()
        v.reciprocal(rec, den)
        v.drain()
        m2b = m2.rearrange("p (t o) -> p t o", o=1).broadcast_to([128, 16, 8])
        v.tensor_tensor(ind3, sg3, m2b, ALU.is_ge)
        v.drain()
        v.tensor_mul(g3, sg3, ind3)
        v.drain()
        recb = rec.rearrange("p (t o) -> p t o", o=1).broadcast_to([128, 16, 8])
        v.tensor_tensor(g3, g3, recb, ALU.mult)
    p.v(fv2)
    p.di("vector", ev)
    p.wait("tensor", ev)
    for tt in range(16):
        def ft(t, tt=tt):
            t.transpose(cx.pB[0:8, 1280 + (tt % 2) * 128:
                              1280 + (tt % 2) * 128 + 128],
                        gates[:, tt * 8: tt * 8 + 8],
                        cx.identf[:, :]).then_inc(mm.h, 1)
        p.t(ft)
        mm.n += 1
        p.wait("vector", mm)

        def fv3(v, tt=tt):
            v.tensor_copy(cx.gcm[0:8, tt * 128:(tt + 1) * 128],
                          cx.pB[0:8, 1280 + (tt % 2) * 128:
                                1280 + (tt % 2) * 128 + 128])
        p.v(fv3)
        p.di("vector", ev)
        p.wait("tensor", ev)
    p.barrier()


def ph_l0_moe(p, cx):
    ld = p.sem("ld_moe")
    ldg = p.sem("ldg_moe")
    mm = p.sem("mm_moe")
    evh = p.sem("evh_moe")
    evy = p.sem("evy_moe")
    hsem = p.sem("h_moe")
    grow = cx.rrow[:, 0:512]
    for e in range(NEXP + 1):
        shared_e = e == NEXP
        if shared_e:
            w1src, w2src = cx.io["Ws1"], cx.io["Ws2"]
        else:
            w1src, w2src = cx.io["W1"][e], cx.io["W2"][e]
        p.barrier(("gpsimd", "tensor", "sync", "scalar"))
        for kt in range(NCT):
            p.dma("gpsimd", cx.wA[:, kt * 512:(kt + 1) * 512],
                  w1src[kt * 128:(kt + 1) * 128, :], ld)
        for kt in range(4):
            p.dma("gpsimd",
                  cx.wA[:, 4096 + kt * 1024: 4096 + (kt + 1) * 1024],
                  w2src[kt * 128:(kt + 1) * 128, :], ld)
        for m in range(4):
            src = (cx.io["bs1"][:, m * 128:(m + 1) * 128] if shared_e
                   else cx.io["b1"][e:e + 1, m * 128:(m + 1) * 128])
            p.dma("sync", cx.stc[:, 6 + m:7 + m],
                  src.rearrange("a b -> b a"), ld)
        if shared_e:
            for m in range(NEXP):
                p.dma("sync", cx.stc[:, 10 + m:11 + m],
                      cx.io["bs2"][m * 128:(m + 1) * 128, :], ld)
            p.dma("gpsimd", cx.wB[0:8, 3072:4096], cx.io["b2"], ld)
        p.wait("tensor", ld)
        p.wait("scalar", ld)
        for tc in range(4):
            t0 = tc * 512
            if not shared_e:
                p.wait("gpsimd", mm)
                p.dma("gpsimd", grow, cx.gcm[e:e + 1, t0:t0 + 512], ldg)
                p.wait("tensor", ldg)
                p.lag_wait("tensor", hsem, 0)

                def ftg(t):
                    t.matmul(cx.pB[:, 1024:1536], cx.ones_row[:, :], grow,
                             start=True, stop=True).then_inc(mm.h, 1)
                p.t(ftg)
                mm.n += 1
            for m in range(4):
                dsth = cx.pB[:, (m % 2) * 512:(m % 2) * 512 + 512]
                p.lag_wait("tensor", evh, 1)

                def ft(t, dsth=dsth, t0=t0, m=m):
                    for kt in range(NCT):
                        i = t.matmul(
                            dsth, cx.wA[:, kt * 512 + m * 128:
                                        kt * 512 + m * 128 + 128],
                            cx.xn[:, kt * TPC + t0: kt * TPC + t0 + 512],
                            start=(kt == 0), stop=(kt == NCT - 1))
                    i.then_inc(mm.h, 1)
                p.t(ft)
                mm.n += 1
                p.wait("scalar", mm)
                hdst = cx.big16[:, m * 512: m * 512 + 512]

                def fs(s_, dsth=dsth, hdst=hdst, m=m):
                    s_.activation(hdst, dsth, AF.Gelu,
                                  bias=cx.stc[:, 6 + m:7 + m],
                                  scale=1.0)
                p.s(fs)
                p.di("scalar", evh)
            if not shared_e:
                p.wait("vector", mm)
                p.wait("vector", evh)
                for m in range(4):
                    hdst = cx.big16[:, m * 512: m * 512 + 512]

                    def fvg(v, hdst=hdst):
                        v.tensor_tensor(hdst, hdst, cx.pB[:, 1024:1536],
                                        ALU.mult)
                    p.v(fvg)
                p.di("vector", hsem)
                p.wait("tensor", hsem)
            else:
                p.wait("tensor", evh)
            for m in range(NEXP):
                dsty = cx.pA[:, (m % 4) * 512:(m % 4) * 512 + 512]
                p.lag_wait("tensor", evy, 3)

                def ft2(t, dsty=dsty, m=m, tc=tc, shared_e=shared_e):
                    ops = []
                    for kt in range(4):
                        ops.append((cx.wA[:, 4096 + kt * 1024 + m * 128:
                                          4096 + kt * 1024 + m * 128 + 128],
                                    cx.big16[:, kt * 512: kt * 512 + 512]))
                    if shared_e:
                        ops.append((cx.wB[0:8, 3072 + m * 128:
                                          3072 + m * 128 + 128],
                                    cx.gcm[0:8, tc * 512:tc * 512 + 512]))
                    for j, (l_, r_) in enumerate(ops):
                        i = t.matmul(dsty, l_, r_, start=(j == 0),
                                     stop=(j == len(ops) - 1))
                    i.then_inc(mm.h, 1)
                p.t(ft2)
                mm.n += 1
                p.wait("vector", mm)
                xdst = cx.xres[:, m * TPC + tc * 512: m * TPC + tc * 512 + 512]

                def fv2(v, dsty=dsty, xdst=xdst, shared_e=shared_e, m=m):
                    if shared_e:
                        v.scalar_tensor_tensor(
                            xdst, dsty, cx.stc[:, 10 + m:11 + m], xdst,
                            ALU.add, ALU.add).then_inc(evy.h, 1)
                    else:
                        v.tensor_add(xdst, xdst, dsty).then_inc(evy.h, 1)
                p.v(fv2)
                evy.n += 1
            p.wait("tensor", evy)
    p.barrier()



def _declare_io(nc, arrays, outs):
    import ml_dtypes
    io = {}
    for name, arr in arrays.items():
        if arr.dtype == ml_dtypes.bfloat16:
            dt = BF16
        elif arr.dtype == np.int32:
            dt = mybir.dt.int32
        else:
            dt = F32
        io[name] = nc.dram_tensor(name, list(arr.shape), dt,
                                  kind="ExternalInput").ap()
    for name, shape, dt in outs:
        io[name] = nc.dram_tensor(name, list(shape), dt,
                                  kind="ExternalOutput").ap()
    return io


def st_store(p, cx, name_ap, src, sem):
    p.dma("sync", name_ap, src, sem)


def build_launch1(core_in):
    p = Prog()
    nc = p.nc
    outs = [("K_OWN", [128, TPC], BF16), ("Q_OWN", [128, TPC], BF16),
            ("V_OWN", [16 * 128, 1024], BF16), ("X1A", [1024, TPC], F32),
            ("XN0", [1024, TPC], BF16), ("GCM_D", [8, TPC], BF16),
            ("STC_D", [128, 512], F32)]
    io = _declare_io(nc, core_in, outs)
    cx = setup_common(p, io, itab_cols=8)
    p.open()
    ph_init(p, cx)
    ph_load_x(p, cx)
    ph_layernorm(p, cx, "ln0")
    ph_l0_qkv(p, cx)
    sts = p.sem("st1")
    p.dma("sync", io["K_OWN"], cx.l0k[:, :], sts)
    p.dma("sync", io["Q_OWN"], cx.l0q[:, :], sts)
    for tt in range(16):
        p.dma("sync", io["V_OWN"][tt * 128:(tt + 1) * 128, :],
              cx.vtm[:, tt * 1024:(tt + 1) * 1024], sts)
    for ct in range(NCT):
        p.dma("sync", io["XN0"][ct * 128:(ct + 1) * 128, :],
              cx.xn[:, ct * TPC:(ct + 1) * TPC], sts)
    p.wait("sync", sts)
    p.barrier()
    ph_l0_gates(p, cx)
    p.dma("sync", io["GCM_D"], cx.gcm[:, :], sts)
    p.dma("sync", io["STC_D"], cx.stc[:, :], sts)
    p.wait("sync", sts)
    ph_l0_moe(p, cx)
    for ct in range(NCT):
        p.dma("sync", io["X1A"][ct * 128:(ct + 1) * 128, :],
              cx.xres[:, ct * TPC:(ct + 1) * TPC], sts)
    p.wait("sync", sts)
    p.barrier()
    return p.close()


JT_U = [[8 * 3 + 2 * i + 2 for i in range(4)],
        [8 * 7 + 2 * i + 2 for i in range(4)]]
KCOLS = [4096, 8192]


def l0_gtab(core):
    g = np.zeros((128, 732), np.int64)
    pp = np.arange(128)
    col_v, col_b = 12, 372
    for s, (b, ktrue, coff) in enumerate(chunks_of_core(core)):
        base = 0 if s == 0 else 4
        for jc in range(KCOLS[s] // 1024):
            if jc <= ktrue:
                r, off = owner_of_chunk(b, jc)
                g[:, base + jc] = (r * 128 + pp) * 2 + (off // 1024)
            else:
                g[:, base + jc] = (1024 + pp) * 2
        for ich in range(4):
            jt_true = 8 * ktrue + 2 * ich + 2
            i0g = ktrue * 1024 + ich * 256
            for j in range(JT_U[s][ich]):
                if j < jt_true:
                    jc, tt = j // 8, j % 8
                    r, off = owner_of_chunk(b, jc)
                    g[:, col_v] = r * 2048 + off + tt * 128 + pp
                    D0 = i0g - j * 128
                    bi = (D0 // 128 + 1) if D0 in (-128, 0, 128) else 3
                else:
                    g[:, col_v] = 16384 + pp
                    bi = 3
                g[:, col_b] = bi * 128 + pp
                col_v += 1
                col_b += 1
    return g.astype(np.int32)


def ph_l0_attn_u(p, cx, kgx2_ap, vgx_ap, btiles_ap, gtab_ap):
    nc = p.nc
    mm = p.sem("mm_att")
    vld = p.sem("vld_att")
    bld = p.sem("bld_att")
    sread = p.sem("sr_att")
    erfd = p.sem("erf_att")
    attd = p.sem("ar_att")
    kld = p.sem("kld_att")
    evx = p.sem("evx_att")
    tld = p.sem("tld_att")
    gt = cx.gtab
    p.dma("sync", gt[:, :], gtab_ap, tld)
    for i in range(3):
        p.dma("sync", cx.scr[:, i * ICH:(i + 1) * ICH], cx.io["toe"][i], tld)
    p.wait("vector", tld)

    def fv0(v):
        for i in range(3):
            sl = cx.scr[:, i * ICH:(i + 1) * ICH]
            v.tensor_scalar(sl, sl, KB1, KB0, ALU.mult, ALU.add)
        v.drain()
        v.tensor_scalar(cx.stc[:, 20:21], cx.scr[:, 3 * ICH - 1: 3 * ICH],
                        1.0, None, ALU.mult)
        v.drain()
        v.tensor_scalar(cx.scr[:, 3 * ICH: 4 * ICH],
                        cx.scr[:, 0:ICH], 0.0, cx.stc[:, 20:21],
                        ALU.mult, ALU.add)
    p.v(fv0)
    p.di("vector", tld)
    p.wait("sync", tld)
    for i in range(4):
        p.dma("sync", btiles_ap[i * 128:(i + 1) * 128, :],
              cx.scr[:, i * ICH:(i + 1) * ICH], tld)
    p.wait("sync", tld)
    p.wait("gpsimd", tld)
    p.barrier()

    kbuf = cx.big16
    af32p = [cx.scr[:, 1024 + i * 512: 1024 + (i + 1) * 512] for i in range(2)]
    ef32p = [cx.scr[:, 2048 + i * 512: 2048 + (i + 1) * 512] for i in range(2)]
    attb = [cx.wB[:, i * ICH:(i + 1) * ICH] for i in range(4)]
    bring = getattr(cx, "attn_bring", None) or \
        [cx.abuf[:, i * ICH:(i + 1) * ICH] for i in range(4)]
    vring = [cx.wA[:, i * 1024:(i + 1) * 1024] for i in range(4)]
    attnv_mark = {}

    for s in range(2):
        coff = s * CHUNK
        p.wait("gpsimd", mm)
        for jc in range(KCOLS[s] // 1024):
            p.gdma(kbuf[:, jc * 1024:(jc + 1) * 1024], kgx2_ap,
                   gt[:, (0 if s == 0 else 4) + jc:
                      (0 if s == 0 else 4) + jc + 1], kld)
        p.wait("tensor", kld)
        for ich in range(4):
            jt_n = JT_U[s][ich]
            i0c = coff + ich * 256
            cv0 = 12 + sum(JT_U[ss][ii] for ss in range(2) for ii in range(4)
                           if (ss, ii) < (s, ich))
            cb0 = cv0 + 360
            p.wait("gpsimd", mm)
            nbp = len(bring) // 2
            for j0 in range(0, jt_n, 2):
                for dj in range(2):
                    j = j0 + dj
                    if j >= 4:
                        p.wait("gpsimd", mm, attnv_mark[j - 4])
                    p.gdma(vring[j % 4], vgx_ap, gt[:, cv0 + j: cv0 + j + 1],
                           vld)
                p.lag_wait("gpsimd", sread, (nbp - 1) if j0 > 0 else 0)
                for dj in range(2):
                    j = j0 + dj
                    p.gdma(bring[j % (2 * nbp)], btiles_ap,
                           gt[:, cb0 + j: cb0 + j + 1], bld)
                p.lag_wait("tensor", sread, 1 if j0 > 0 else 0)
                for dj in range(2):
                    j = j0 + dj
                    simbank = cx.pB[:, (j % 4) * 256:(j % 4) * 256 + 256]

                    def fts(t, simbank=simbank, j=j, i0c=i0c):
                        t.matmul(simbank, kbuf[:, j * 128:(j + 1) * 128],
                                 cx.l0q[:, i0c:i0c + 256],
                                 start=True, stop=True).then_inc(mm.h, 1)
                    p.t(fts)
                    mm.n += 1
                ip = (j0 // 2) % 2
                simpair = cx.pB[:, (j0 % 4) * 256:(j0 % 4) * 256 + 512]
                brpair = bring[j0 % (2 * nbp)]
                brpair2 = bring[(j0 + 1) % (2 * nbp)]
                p.wait("vector", mm)
                p.wait("vector", bld)
                p.lag_wait("vector", erfd, 1)

                def fvn(v, simpair=simpair, ip=ip, brpair=brpair,
                        brpair2=brpair2):
                    v.scalar_tensor_tensor(af32p[ip][:, 0:256],
                                           simpair[:, 0:256], K1S,
                                           brpair, ALU.mult, ALU.add)
                    v.scalar_tensor_tensor(af32p[ip][:, 256:512],
                                           simpair[:, 256:512], K1S,
                                           brpair2, ALU.mult, ALU.add)
                p.v(fvn)
                p.di("vector", sread)
                p.wait("scalar", sread)
                p.lag_wait("scalar", attd, 1)

                def fsn(s_, ip=ip):
                    s_.activation(ef32p[ip], af32p[ip], AF.Erf, bias=0.0,
                                  scale=1.0)
                p.s(fsn)
                p.di("scalar", erfd)
                p.wait("vector", erfd)
                if j0 >= 4:
                    p.wait("vector", mm, attnv_mark[j0 - 3])

                def fva(v, ip=ip, j0=j0):
                    v.tensor_scalar(
                        cx.wB[:, (j0 % 4) * 256:(j0 % 4) * 256 + 512],
                        ef32p[ip], 0.5, 0.5, ALU.mult, ALU.add)
                p.v(fva)
                p.di("vector", attd)
                p.wait("tensor", attd)
                p.wait("tensor", vld)
                for dj in range(2):
                    j = j0 + dj

                    def fta(t, j=j, jt_n=jt_n):
                        for et in range(8):
                            i = t.matmul(
                                cx.pA[:, et * 256:(et + 1) * 256],
                                vring[j % 4][:, et * 128:(et + 1) * 128],
                                attb[j % 4], start=(j == 0 and et % 2 == 0),
                                stop=(j == jt_n - 1))
                        i.then_inc(mm.h, 1)
                    p.t(fta)
                    mm.n += 1
                    attnv_mark[j] = mm.n
            p.wait("vector", mm)
            for et in range(8):
                def fve(v, et=et, i0c=i0c):
                    v.tensor_add(
                        cx.xres[:, et * TPC + i0c: et * TPC + i0c + 256],
                        cx.xres[:, et * TPC + i0c: et * TPC + i0c + 256],
                        cx.pA[:, et * 256:(et + 1) * 256])
                p.v(fve)
            p.di("vector", evx)
            p.wait("tensor", evx)
    p.barrier()


def ph_l1_proj_scan(p, cx, lbuf, pbuf, sumout):
    nc = p.nc
    ld = p.sem("ld_l1p")
    mm = p.sem("mm_l1p")
    evp = p.sem("evp_l1p")
    sst = p.sem("sst_l1p")
    p.dma("sync", cx.stc[:, 24:32], cx.io["lam"], ld)
    p.dma("sync", cx.stc[:, 32:40], cx.io["b_ig"], ld)
    p.dma("sync", cx.stc[:, 40:48], cx.io["b_rg"], ld)
    p.wait("scalar", ld)
    p.wait("vector", ld)
    lamc = cx.stc[:, 24:32]
    nbig = cx.stc[:, 32:40]
    nbrg = cx.stc[:, 40:48]

    def fs0(s_):
        s_.activation(cx.stc[:, 48:56], lamc, AF.Exp, bias=0.0, scale=-1.0)
    p.s(fs0)
    p.di("scalar", evp)
    p.wait("vector", evp)

    def fv0(v):
        v.tensor_scalar_add(cx.stc[:, 48:56], cx.stc[:, 48:56], 1.0)
        v.drain()
        v.reciprocal(cx.stc[:, 56:64], cx.stc[:, 48:56])
        v.drain()
        v.tensor_scalar_add(cx.stc[:, 56:64], cx.stc[:, 56:64], 1e-8)
        v.tensor_scalar_mul(nbig, nbig, -1.0)
        v.tensor_scalar_mul(nbrg, nbrg, -1.0)
    p.v(fv0)
    p.di("vector", evp)
    p.wait("scalar", evp)

    def fs1(s_):
        s_.activation(lamc, cx.stc[:, 56:64], AF.Ln, bias=0.0, scale=1.0)
    p.s(fs1)
    p.di("scalar", evp)
    p.wait("vector", evp)

    def fv1(v):
        v.tensor_scalar_mul(lamc, lamc, C_REC)
    p.v(fv1)
    p.di("vector", evp)
    p.wait("scalar", evp)
    p.wait("tensor", evp)

    zt = cx.scr[:, 2048:3072]

    def fvz(v):
        v.memset(zt, 0.0)
    p.v(fvz)
    p.di("vector", evp)

    for mt in range(NCT):
        p.barrier(("gpsimd", "tensor"))
        for wi, wname in enumerate(("WinT", "WigT", "WrgT")):
            for kt in range(NCT):
                p.dma("gpsimd",
                      cx.wB[:, wi * 1024 + kt * 128: wi * 1024 + kt * 128 + 128],
                      cx.io[wname][kt * 128:(kt + 1) * 128,
                                   mt * 128:(mt + 1) * 128], ld)
        p.wait("tensor", ld)
        for tch in range(4):
            t0 = tch * 512
            for wi in range(3):
                dst = cx.pB[:, wi * 512: wi * 512 + 512]

                def ftp(t, dst=dst, wi=wi, t0=t0):
                    for kt in range(NCT):
                        i = t.matmul(
                            dst,
                            cx.wB[:, wi * 1024 + kt * 128:
                                  wi * 1024 + kt * 128 + 128],
                            cx.xn[:, kt * TPC + t0: kt * TPC + t0 + 512],
                            start=(kt == 0), stop=(kt == NCT - 1))
                    i.then_inc(mm.h, 1)
                p.t(ftp)
                mm.n += 1
            p.wait("scalar", mm)
            p.wait("vector", mm)
            xp = cx.scr[:, 0:512]
            iw = cx.scr[:, 512:1024]
            aw = cx.scr[:, 1024:1536]
            sq = cx.scr[:, 1536:2048]
            mtc = cx.stc[:, 24 + mt:25 + mt]

            def fvx(v):
                v.tensor_copy(xp, cx.pB[:, 0:512])
            p.v(fvx)

            def fsi(s_):
                s_.activation(iw, cx.pB[:, 512:1024], AF.Exp,
                              bias=nbig[:, mt:mt + 1], scale=-1.0)
                s_.drain()
                s_.activation(aw, cx.pB[:, 1024:1536], AF.Exp,
                              bias=nbrg[:, mt:mt + 1], scale=-1.0)
            p.s(fsi)
            p.di("scalar", evp)
            p.wait("vector", evp)

            def fvi(v):
                v.tensor_scalar_add(iw, iw, 1.0)
                v.drain()
                v.reciprocal(iw, iw)
                v.drain()
                v.tensor_scalar_add(aw, aw, 1.0)
                v.drain()
                v.reciprocal(aw, aw)
            p.v(fvi)
            p.di("vector", evp)
            p.wait("scalar", evp)

            def fsa(s_):
                s_.activation(aw, aw, AF.Exp, bias=0.0,
                              scale=mtc)
                s_.drain()
                s_.activation(sq, aw, AF.Square, bias=0.0, scale=1.0)
            p.s(fsa)
            p.di("scalar", evp)
            p.wait("vector", evp)

            def fvu(v):
                v.tensor_scalar(sq, sq, -1.0, 1.000001, ALU.mult, ALU.add)
            p.v(fvu)
            p.di("vector", evp)
            p.wait("scalar", evp)

            def fsq(s_):
                s_.activation(sq, sq, AF.Ln, bias=0.0, scale=1.0)
                s_.drain()
                s_.activation(sq, sq, AF.Exp, bias=0.0, scale=0.5)
            p.s(fsq)
            p.di("scalar", evp)
            p.wait("vector", evp)

            def fvg(v, t0=t0):
                v.tensor_mul(iw, iw, xp)
                v.drain()
                v.tensor_mul(cx.gbuf[:, t0:t0 + 512], iw, sq)
                v.tensor_copy(cx.abuf[:, t0:t0 + 512], aw)
            p.v(fvg)
            p.di("vector", evp)
            p.wait("tensor", evp)
        for ch in range(2):
            c0 = ch * 1024
            Lout = cx.scr[:, 0:1024]
            Pout = cx.scr[:, 1024:2048]
            p.wait("vector", sst)

            def fsc(v, c0=c0, ch=ch, Lout=Lout, Pout=Pout):
                v.tensor_tensor_scan(Lout, cx.abuf[:, c0:c0 + 1024],
                                     cx.gbuf[:, c0:c0 + 1024], 0.0,
                                     ALU.mult, ALU.add)
                v.drain()
                v.tensor_tensor_scan(Pout, cx.abuf[:, c0:c0 + 1024],
                                     zt, 1.0, ALU.mult, ALU.add)
                v.drain()
                v.tensor_copy(cx.stc[:, 64 + ch * 2: 65 + ch * 2],
                              Pout[:, 1023:1024])
                v.tensor_copy(cx.stc[:, 65 + ch * 2: 66 + ch * 2],
                              Lout[:, 1023:1024])
            p.v(fsc)
            p.di("vector", evp)
            p.wait("sync", evp)
            p.dma("sync", lbuf[mt * 128:(mt + 1) * 128, c0:c0 + 1024],
                  Lout, sst)
            p.dma("sync", pbuf[mt * 128:(mt + 1) * 128, c0:c0 + 1024],
                  Pout, sst)
        p.dma("sync", sumout[:, mt * 4:(mt + 1) * 4],
              cx.stc[:, 64:68], sst)
    p.wait("sync", sst)
    p.barrier()


def build_launch2(core_in):
    p = Prog()
    nc = p.nc
    outs = [("X1", [1024, TPC], F32), ("XN1", [1024, TPC], BF16),
            ("LBUF", [1024, TPC], F32), ("PBUF", [1024, TPC], F32),
            ("SUM", [128, 32], F32)]
    io = _declare_io(nc, core_in, outs)
    cx = setup_common(p, io, big16_cols=8192, itab_cols=8)
    cx.abuf = nc.alloc_sbuf_tensor("abuf", [128, TPC], F32)
    cx.gbuf = nc.alloc_sbuf_tensor("gbuf", [128, TPC], F32)
    cx.l0q = nc.alloc_sbuf_tensor("l0q", [128, TPC], BF16)
    btiles = nc.dram_tensor("btiles", [4 * 128, ICH], F32).ap()
    cx.gtab = nc.alloc_sbuf_tensor("gtab", [128, 732], mybir.dt.int32)
    p.open()
    ph_init(p, cx)
    ld = p.sem("ld_x")
    for ct in range(NCT):
        p.dma("sync", cx.xres[:, ct * TPC:(ct + 1) * TPC],
              cx.io["X1A"][ct * 128:(ct + 1) * 128, :], ld)
    p.dma("sync", cx.l0q[:, :], cx.io["Q_OWN"], ld)
    for e in ("vector", "scalar", "tensor"):
        p.wait(e, ld)
    p.barrier()
    ph_l0_attn_u(p, cx, io["KGX2"], io["VGX"], btiles, io["l0g"])
    sts = p.sem("st2")
    for ct in range(NCT):
        p.dma("sync", io["X1"][ct * 128:(ct + 1) * 128, :],
              cx.xres[:, ct * TPC:(ct + 1) * TPC], sts)
    ph_layernorm(p, cx, "ln1")
    for ct in range(NCT):
        p.dma("sync", io["XN1"][ct * 128:(ct + 1) * 128, :],
              cx.xn[:, ct * TPC:(ct + 1) * TPC], sts)
    ph_l1_proj_scan(p, cx, io["LBUF"], io["PBUF"], io["SUM"])
    p.wait("sync", sts)
    p.barrier()
    return p.close()


def ph_l1_apply(p, cx, lbuf, pbuf, sumgx_ap, gt):
    ld = p.sem("ld_ap")
    mm = p.sem("mm_ap")
    ev = p.sem("ev_ap")
    gld = p.sem("gld_ap")
    p.dma("sync", cx.stc[:, 60:68], cx.io["h0"], ld)
    p.dma("sync", cx.stc[:, 68:76], cx.io["norm_w"], ld)
    p.wait("vector", ld)
    for ci in range(2):
        hin = cx.stc[:, 76 + ci * 8: 84 + ci * 8]
        h3 = hin.rearrange("p (ct o) -> p ct o", o=1)

        def fvh(v, hin=hin):
            v.tensor_copy(hin, cx.stc[:, 60:68])
        p.v(fvh)
        p.di("vector", ev)
        for j in range(7):
            p.wait("gpsimd", ev)
            p.gdma(cx.stc[:, 100:132], sumgx_ap,
                   gt[:, 48 + ci * 7 + j: 49 + ci * 7 + j], gld)
            p.wait("vector", gld)
            blk = cx.stc[:, 100:132].rearrange("p (ct c) -> p ct c", c=4)
            cslot = 0 if j <= 3 else 2

            def fv(v, h3=h3, blk=blk, cslot=cslot):
                v.tensor_tensor(h3, h3, blk[:, :, cslot:cslot + 1], ALU.mult)
                v.drain()
                v.tensor_tensor(h3, h3, blk[:, :, cslot + 1:cslot + 2],
                                ALU.add)
                v.drain()
            p.v(fv)
            p.di("vector", ev)
    p.wait("scalar", ev)
    p.wait("sync", ev)
    for ci in range(2):
        coff = ci * CHUNK
        for ct in range(NCT):
            Lt = cx.scr[:, 0:1024]
            Pt = cx.scr[:, 1024:2048]
            p.dma("sync", Lt, lbuf[ct * 128:(ct + 1) * 128,
                                   coff:coff + 1024], ld)
            p.dma("sync", Pt, pbuf[ct * 128:(ct + 1) * 128,
                                   coff:coff + 1024], ld)
            p.wait("vector", ld)
            hf = cx.pA[:, 1024:2048]
            hin = cx.stc[:, 76 + ci * 8 + ct: 77 + ci * 8 + ct]

            def fv1(v, Pt=Pt, Lt=Lt, hf=hf, hin=hin):
                v.scalar_tensor_tensor(hf, Pt, hin, Lt, ALU.mult, ALU.add)
            p.v(fv1)
            p.di("vector", ev)
            p.wait("scalar", ev)
            sqt = cx.wA[:, ct * 1024:(ct + 1) * 1024]

            def fs1(s_, hf=hf, sqt=sqt):
                s_.activation(sqt, hf, AF.Square, bias=0.0, scale=1.0)
            p.s(fs1)
            p.di("scalar", ev)
            p.wait("sync", ev)
        p.wait("tensor", ev)
        for ct in range(NCT):
            sqt = cx.wA[:, ct * 1024:(ct + 1) * 1024]
            first, last = ct == 0, ct == NCT - 1

            def ft1(t, sqt=sqt, first=first, last=last):
                t.matmul(cx.pB[0:1, 0:512], cx.ones_col_b[:, :],
                         sqt[:, 0:512], start=first, stop=last)
                i = t.matmul(cx.pB[0:1, 512:1024], cx.ones_col_b[:, :],
                             sqt[:, 512:1024], start=first, stop=last)
                if last:
                    i.then_inc(mm.h, 1)
            p.t(ft1)
        mm.n += 1
        p.wait("scalar", mm)
        rr = cx.rows[:, 0:1024]

        def fs2(s_, rr=rr):
            s_.activation(rr[:, 0:512], cx.pB[0:1, 0:512], AF.Ln,
                          bias=1e-6, scale=1.0 / 1024.0)
            s_.drain()
            s_.activation(rr[:, 512:1024], cx.pB[0:1, 512:1024], AF.Ln,
                          bias=1e-6, scale=1.0 / 1024.0)
            s_.drain()
            s_.activation(rr, rr, AF.Exp, bias=0.0, scale=-0.5)
        p.s(fs2)
        p.di("scalar", ev)
        p.wait("tensor", ev)

        def ft2(t, rr=rr):
            t.matmul(cx.pB[:, 0:512], cx.ones_row[:, :], rr[:, 0:512],
                     start=True, stop=True)
            t.matmul(cx.pB[:, 512:1024], cx.ones_row[:, :], rr[:, 512:1024],
                     start=True, stop=True).then_inc(mm.h, 1)
        p.t(ft2)
        mm.n += 1
        p.wait("vector", mm)
        rstd = cx.scr[:, 2048:3072]

        def fvr2(v, rstd=rstd):
            v.tensor_copy(rstd, cx.pB[:, 0:1024])
        p.v(fvr2)
        for ct in range(NCT):
            Lt = cx.scr[:, 0:1024]
            Pt = cx.scr[:, 1024:2048]
            p.dma("sync", Lt, lbuf[ct * 128:(ct + 1) * 128,
                                   coff:coff + 1024], ld)
            p.dma("sync", Pt, pbuf[ct * 128:(ct + 1) * 128,
                                   coff:coff + 1024], ld)
            p.wait("vector", ld)
            hf = cx.pA[:, 1024:2048]
            hin = cx.stc[:, 76 + ci * 8 + ct: 77 + ci * 8 + ct]
            hbt = cx.big16[:, ct * TPC + coff: ct * TPC + coff + 1024]
            nw = cx.stc[:, 68 + ct:69 + ct]

            def fv2(v, Pt=Pt, Lt=Lt, hf=hf, hin=hin, hbt=hbt, nw=nw,
                    rstd=rstd):
                v.scalar_tensor_tensor(hf, Pt, hin, Lt, ALU.mult, ALU.add)
                v.drain()
                v.scalar_tensor_tensor(hf[:, 0:512], hf[:, 0:512], nw,
                                       rstd[:, 0:512], ALU.mult, ALU.mult)
                v.scalar_tensor_tensor(hf[:, 512:1024], hf[:, 512:1024], nw,
                                       rstd[:, 512:1024], ALU.mult, ALU.mult)
                v.drain()
                v.tensor_copy(hbt, hf)
            p.v(fv2)
            p.di("vector", ev)
            p.wait("sync", ev)
        p.wait("tensor", ev)
    p.barrier()
    evw = p.sem("evw_ap")
    for mt in range(NCT):
        p.barrier(("gpsimd", "tensor"))
        for kt in range(NCT):
            p.dma("gpsimd", cx.wB[:, kt * 128:(kt + 1) * 128],
                  cx.io["WoutT"][kt * 128:(kt + 1) * 128,
                                 mt * 128:(mt + 1) * 128], ld)
        p.wait("tensor", ld)
        for tch in range(4):
            t0 = tch * 512
            dst = cx.pB[:, (tch % 2) * 512:(tch % 2) * 512 + 512]
            p.lag_wait("tensor", evw, 1)

            def ft3(t, dst=dst, t0=t0):
                for kt in range(NCT):
                    i = t.matmul(
                        dst, cx.wB[:, kt * 128:(kt + 1) * 128],
                        cx.big16[:, kt * TPC + t0: kt * TPC + t0 + 512],
                        start=(kt == 0), stop=(kt == NCT - 1))
                i.then_inc(mm.h, 1)
            p.t(ft3)
            mm.n += 1
            p.wait("vector", mm)
            xdst = cx.xres[:, mt * TPC + t0: mt * TPC + t0 + 512]

            def fv3(v, dst=dst, xdst=xdst):
                v.tensor_add(xdst, xdst, dst)
            p.v(fv3)
            p.di("vector", evw)
    p.wait("tensor", evw)
    p.barrier()


XMW = 1284
ZW = 2064

W_KT = 0
W_QT = 8192
W_SB2 = 12800
W_OT = 12800
W_VBQ = 17408
W_MASK = 21504
W_PN = 25600
W_PT = 26624
W_XLN = 0
W_HGL = 8192
W_GEL = 19456


def l1_gtab(core):
    g = np.zeros((128, 70), np.int64)
    pp = np.arange(128)
    for ci, (b, ktrue, coff) in enumerate(chunks_of_core(core)):
        t0g = ktrue * 1024
        for ct in range(NCT):
            for piece in range(3):
                tok = [t0g - 1025, t0g - 1024, t0g][piece]
                if tok < 0:
                    g[:, ci * 24 + ct * 3 + piece] = 16384 + pp
                else:
                    jc = tok // 1024
                    r, off = owner_of_chunk(b, jc)
                    g[:, ci * 24 + ct * 3 + piece] = (
                        (r * 1024 + ct * 128 + pp) * 2 + off // 1024)
        for j in range(7):
            if j < ktrue:
                r, off = owner_of_chunk(b, j)
                g[:, 48 + ci * 7 + j] = r * 128 + pp
            else:
                g[:, 48 + ci * 7 + j] = 1024 + pp
        for sl4 in range(4):
            pat = sl4 * 2
            if ktrue == 0:
                g[:, 62 + ci * 4 + sl4] = 1024 + pp
            else:
                g[:, 62 + ci * 4 + sl4] = pat * 128 + pp
    return g.astype(np.int32)


def l1_rots(core):
    cs_q, sn_q, cs_k, sn_k = _xpos_tables()
    tabs = (cs_q, sn_q, cs_k, sn_k)
    out = {}
    for ci, (b, ktrue, coff) in enumerate(chunks_of_core(core)):
        qs = ktrue * 1024 - 1024
        for ri in range(4):
            a = np.zeros((2048, 64), np.float32)
            lo = max(qs, 0)
            a[lo - qs:2048] = tabs[ri][lo:qs + 2048]
            out["rot%d_%d" % (ci, ri)] = a
    return out


def _l1ffn_maskx():
    m = _l1ffn_masks().reshape(1024, 512)
    return np.concatenate([m, np.full((128, 512), -1e9, np.float32)], 0)


def ph_l1ffn_layer(p, cx, l, zg2_ap, zl_ap, gt, vscr, maskx_ap,
                   yout=None, x2a=None, fout=None, ylout=None):
    ld = p.sem("ldf%d" % l)
    ldr = p.sem("ldrf%d" % l)
    ldw = p.sem("ldwf%d" % l)
    gz = p.sem("gzf%d" % l)
    mm = p.sem("mmf%d" % l)
    evm = p.sem("evmf%d" % l)
    evs = p.sem("evsf%d" % l)
    evz = p.sem("evzf%d" % l)
    ev = p.sem("evf%d" % l)
    evt = p.sem("evtf%d" % l)
    evo = p.sem("evof%d" % l)
    vst = p.sem("vstf%d" % l)
    yst = p.sem("ystf%d" % l)
    wk = cx.work16
    negm = cx.stc[:, 296:297]
    Zc = cx.stc[:, 297:298]
    rz = cx.stc[:, 298:299]
    p.dma("sync", cx.rrow[:, 0:512], cx.io["qs_t"][l], ld)
    p.dma("sync", cx.rrow[:, 512:1024], cx.io["ks_t"][l], ld)
    for mi in range(8):
        p.dma("gpsimd", wk[:, W_MASK + mi * 512: W_MASK + (mi + 1) * 512],
              maskx_ap[mi * 128:(mi + 1) * 128, :], ld)
    p.dma("sync", cx.stc[:, 200:208], cx.io["ln_g"][l], ld)
    p.dma("sync", cx.stc[:, 208:216], cx.io["ln_b"][l], ld)
    p.dma("sync", cx.stc[:, 216:224], cx.io["b2f"][l], ld)
    p.dma("sync", cx.stc[:, 224:268], cx.io["b1p"][l], ld)
    p.wait("tensor", ld)

    def ftq(t):
        t.matmul(cx.pB[:, 0:512], cx.ones_row[:, :], cx.rrow[:, 0:512],
                 start=True, stop=True)
        t.matmul(cx.pB[:, 512:1024], cx.ones_row[:, :], cx.rrow[:, 512:1024],
                 start=True, stop=True).then_inc(mm.h, 1)
    p.t(ftq)
    mm.n += 1
    p.wait("vector", mm)

    def fvq(v):
        v.tensor_scalar_mul(cx.qksb[:, 0:512], cx.pB[:, 0:512], QK_SCALE)
        v.tensor_copy(cx.qksb[:, 512:1024], cx.pB[:, 512:1024])
    p.v(fvq)
    p.di("vector", ev)
    p.wait("tensor", ev)

    for ci in range(2):
        coff = ci * CHUNK
        ntt = 16
        p.barrier()
        for ct in range(NCT):
            zrow = cx.big16[:, ct * ZW: ct * ZW + ZW]

            def fvz(v, zrow=zrow):
                v.memset(zrow[:, 0:1], 0.0)
            p.v(fvz)
            base = ci * 24 + ct * 3
            p.gdma(zrow[:, 1:2], zl_ap, gt[:, base:base + 1], gz)
            p.gdma(zrow[:, 2:1026], zg2_ap, gt[:, base + 1:base + 2], gz)
            p.gdma(zrow[:, 1026:2050], zg2_ap, gt[:, base + 2:base + 3], gz)
        p.di("vector", ev)
        p.wait("gpsimd", ev)
        p.wait("tensor", gz)
        p.wait("tensor", ev)
        for kt in range(NCT):
            p.dma("gpsimd", cx.wA[:, kt * 1024:(kt + 1) * 1024],
                  cx.io["wqkv"][l][kt * 128:(kt + 1) * 128, 0:1024], ldw)
            p.dma("gpsimd", cx.wA[:, 8192 + kt * 512: 8192 + (kt + 1) * 512],
                  cx.io["wqkv"][l][kt * 128:(kt + 1) * 128, 1024:1536], ldw)
        p.wait("tensor", ldw)
        for tt in range(ntt):
            zc0 = 2 + tt * 128
            rots = cx.scr[:, 2048:2304]
            p.wait("sync", evt)
            for ri in range(4):
                p.dma("sync", rots[:, ri * 64: ri * 64 + 64],
                      cx.io["rot%d_%d" % (ci, ri)][tt * 128:(tt + 1) * 128,
                                                   :], ldr)
            p.lag_wait("tensor", ev, 1)

            def ftk(t, zc0=zc0):
                for kt in range(NCT):
                    zrow = cx.big16[:, kt * ZW: kt * ZW + ZW]
                    sh = 0 if kt < 4 else -1
                    lhsT = zrow[:, zc0 + sh: zc0 + sh + 128]
                    t.matmul(cx.pA[:, 0:512], lhsT,
                             cx.wA[:, kt * 1024: kt * 1024 + 512],
                             start=(kt == 0), stop=(kt == NCT - 1))
                    t.matmul(cx.pA[:, 512:1024], lhsT,
                             cx.wA[:, kt * 1024 + 512: kt * 1024 + 1024],
                             start=(kt == 0), stop=(kt == NCT - 1))
                    i = t.matmul(
                        cx.pB[:, 0:512], lhsT,
                        cx.wA[:, 8192 + kt * 512: 8192 + (kt + 1) * 512],
                        start=(kt == 0), stop=(kt == NCT - 1))
                i.then_inc(mm.h, 1)
            p.t(ftk)
            mm.n += 1
            p.wait("vector", mm)
            p.wait("vector", ldr)
            vtile = wk[:, W_SB2:W_SB2 + 512]
            p.wait("vector", vst)

            def fvv(v, vtile=vtile):
                v.tensor_copy(vtile, cx.pB[:, 0:512])
            p.v(fvv)
            p.di("vector", ev)
            p.wait("sync", ev)
            p.dma("sync", vscr[tt * 128:(tt + 1) * 128, :], vtile, vst)
            have_q = (tt * 128 - 896) >= 0
            qis = [0, 1] if have_q else [1]
            fbufs = [cx.scr[:, 0:512], cx.scr[:, 512:1024]]
            sqbs = [cx.scr[:, 1024:1536], cx.scr[:, 1536:2048]]
            qbs = [wk[:, W_SB2 + 512:W_SB2 + 1024],
                   wk[:, W_SB2 + 1792:W_SB2 + 2304]]
            qrots = [wk[:, W_SB2 + 1024:W_SB2 + 1536],
                     wk[:, W_SB2 + 2304:W_SB2 + 2816]]
            tmps = [wk[:, W_SB2 + 1536:W_SB2 + 1792],
                    wk[:, W_SB2 + 2816:W_SB2 + 3072]]
            sss = [cx.stc[:, 280:288], cx.stc[:, 288:296]]

            def fvc(v):
                for qi in qis:
                    v.tensor_copy(fbufs[qi], cx.pA[:, qi * 512:
                                                   qi * 512 + 512])
            p.v(fvc)
            p.di("vector", ev)
            p.wait("scalar", ev)

            def fsq(s_):
                if len(qis) == 2:
                    s_.activation(cx.scr[:, 1024:2048], cx.scr[:, 0:1024],
                                  AF.Square, bias=0.0, scale=1.0)
                else:
                    s_.activation(sqbs[1], fbufs[1], AF.Square, bias=0.0,
                                  scale=1.0)
            p.s(fsq)
            p.di("scalar", ev)
            p.wait("vector", ev)

            def fvr(v):
                for qi in qis:
                    v.tensor_reduce(sss[qi], sqbs[qi].rearrange(
                        "p (h d) -> p h d", d=64), AXX, ALU.add)
                v.drain()
                for qi in qis:
                    v.tensor_scalar_max(sss[qi], sss[qi], 1e-24)
            p.v(fvr)
            p.di("vector", ev)
            p.wait("scalar", ev)

            def fsr(s_):
                lo = 280 if have_q else 288
                sl = cx.stc[:, lo:296]
                s_.activation(sl, sl, AF.Ln, bias=0.0, scale=1.0)
                s_.drain()
                s_.activation(sl, sl, AF.Exp, bias=0.0, scale=-0.5)
            p.s(fsr)
            p.di("scalar", ev)
            p.wait("vector", ev)

            def fvn(v):
                for qi in qis:
                    f3 = fbufs[qi].rearrange("p (h d) -> p h d", d=64)
                    ssb = sss[qi].rearrange("p (h o) -> p h o", o=1
                                            ).broadcast_to([128, 8, 64])
                    v.tensor_tensor(f3, f3, ssb, ALU.mult)
                v.drain()
                for qi in qis:
                    v.tensor_mul(fbufs[qi], fbufs[qi],
                                 cx.qksb[:, qi * 512: qi * 512 + 512])
                v.drain()
                for qi in qis:
                    v.tensor_copy(qbs[qi], fbufs[qi])
                v.drain()
            p.v(fvn)

            def fvrot(v):
                o3s, q3s, tm3s = {}, {}, {}
                for qi in qis:
                    q3s[qi] = qbs[qi].rearrange("p (h d) -> p h d", d=64)
                    o3s[qi] = qrots[qi].rearrange("p (h d) -> p h d", d=64)
                    tm3s[qi] = tmps[qi].rearrange("p (h d) -> p h d", d=32)
                for qi in qis:
                    cs = rots[:, (0 if qi == 0 else 2) * 64:
                              (0 if qi == 0 else 2) * 64 + 64]
                    csb = cs.rearrange("p (o d) -> p o d", o=1
                                       ).broadcast_to([128, 8, 64])
                    v.tensor_tensor(o3s[qi], q3s[qi], csb, ALU.mult)
                v.drain()
                for qi in qis:
                    sn = rots[:, (1 if qi == 0 else 3) * 64:
                              (1 if qi == 0 else 3) * 64 + 64]
                    snl = sn[:, 0:32].rearrange("p (o d) -> p o d", o=1
                                                ).broadcast_to([128, 8, 32])
                    v.tensor_tensor(tm3s[qi], q3s[qi][:, :, 32:64], snl,
                                    ALU.mult)
                v.drain()
                for qi in qis:
                    v.tensor_tensor(o3s[qi][:, :, 0:32], o3s[qi][:, :, 0:32],
                                    tm3s[qi], ALU.subtract)
                v.drain()
                for qi in qis:
                    sn = rots[:, (1 if qi == 0 else 3) * 64:
                              (1 if qi == 0 else 3) * 64 + 64]
                    snh = sn[:, 32:64].rearrange("p (o d) -> p o d", o=1
                                                 ).broadcast_to([128, 8, 32])
                    v.tensor_tensor(tm3s[qi], q3s[qi][:, :, 0:32], snh,
                                    ALU.mult)
                v.drain()
                for qi in qis:
                    v.tensor_tensor(o3s[qi][:, :, 32:64],
                                    o3s[qi][:, :, 32:64], tm3s[qi], ALU.add)
            p.v(fvrot)
            p.di("vector", ev)
            for qi in qis:
                qcol = (tt * 128 - 896) if qi == 0 else (tt * 128)
                stride = 1152 if qi == 0 else 2048
                base = W_QT if qi == 0 else W_KT
                p.wait("tensor", ev)
                p.wait("tensor", evt)

                def ftt(t, qrot=qrots[qi]):
                    for s in range(4):
                        i = t.transpose(cx.tp[:, s * 128:(s + 1) * 128],
                                        qrot[:, s * 128:(s + 1) * 128],
                                        cx.identb[:, :])
                    i.then_inc(mm.h, 1)
                p.t(ftt)
                mm.n += 1
                p.wait("vector", mm)

                def fvt(v, base=base, stride=stride, qcol=qcol):
                    for s in range(4):
                        v.tensor_copy(
                            wk[:, base + s * stride + qcol:
                               base + s * stride + qcol + 128],
                            cx.tp[:, s * 128:(s + 1) * 128])
                p.v(fvt)
                p.di("vector", evt)
        p.wait("tensor", evt)
        p.barrier()
        wlist = [(-128, 1, -512), (0, 4, 0), (512, 4, 512)]
        for wi, (qstart, nqt, w0) in enumerate(wlist):
            kstart = w0 - 512
            ko = kstart + 1024
            if wi == 1:
                p.wait("gpsimd", evm)
                for sl4 in range(4):
                    p.gdma(wk[:, W_MASK + sl4 * 2 * 512:
                              W_MASK + sl4 * 2 * 512 + 512], maskx_ap,
                           gt[:, 62 + ci * 4 + sl4: 63 + ci * 4 + sl4], ld)
                p.wait("vector", ld)
            if wi == 2:
                p.wait("gpsimd", evm)
                for sl4 in range(4):
                    p.dma("gpsimd",
                          wk[:, W_MASK + sl4 * 2 * 512:
                             W_MASK + sl4 * 2 * 512 + 512],
                          maskx_ap[sl4 * 2 * 128:(sl4 * 2 + 1) * 128, :], ld)
                p.wait("vector", ld)
            p.wait("sync", mm)
            for js in range(8):
                jrel = ko // 128 + js
                p.dma("sync", wk[:, W_VBQ + js * 512: W_VBQ + (js + 1) * 512],
                      vscr[jrel * 128:(jrel + 1) * 128, :], ld)
            p.wait("tensor", ld)
            xq = cx.xres.bitcast(BF16)
            simsFb = [cx.xres[:, 12800:13824], cx.xres[:, 13824:14848]]
            pexpb = [xq[:, 29696:30720], xq[:, 30720:31744]]
            pTb = [wk[:, W_PN:W_PN + 1024], wk[:, W_PT:W_PT + 1024]]

            def run_tiles(tiles):
                nT = len(tiles)
                ftscnt = [0] * nT
                evmcnt = [0] * nT
                scnt = [0] * nT
                zcnt = [0] * nT
                tpcnt = [0] * nT
                evtcnt = [0] * nT
                ftocnt = [0] * nT
                evocnt = [evo.n] * nT

                def em_fts(T, h):
                    qcol = tiles[T][0]
                    sl, hb = h // 2, (h % 2) * 64

                    def fts(t, sl=sl, hb=hb, ko=ko, qcol=qcol, T=T):
                        lhsT = wk[hb:hb + 64, W_QT + sl * 1152 + qcol:
                                  W_QT + sl * 1152 + qcol + 128]
                        for sd in range(2):
                            i = t.matmul(
                                cx.pA[:, T * 1024 + sd * 512:
                                      T * 1024 + sd * 512 + 512],
                                lhsT,
                                wk[hb:hb + 64,
                                   W_KT + sl * 2048 + ko + sd * 512:
                                   W_KT + sl * 2048 + ko + sd * 512 + 512],
                                start=True, stop=True)
                        i.then_inc(mm.h, 1)
                    p.wait("tensor", evm, val=evmcnt[T])
                    p.t(fts)
                    mm.n += 1
                    ftscnt[T] = mm.n

                for T in range(nT):
                    em_fts(T, 0)
                for h in range(HEADS):
                    sl, hb = h // 2, (h % 2) * 64
                    for T in range(nT):
                        moff = tiles[T][1]
                        simsF = simsFb[T]
                        ngm = cx.stc[:, 270 + T:271 + T]

                        def fvm(v, moff=moff, T=T, simsF=simsF, ngm=ngm):
                            for sd in range(2):
                                mi = moff + sd
                                v.scalar_tensor_tensor(
                                    simsF[:, sd * 512: sd * 512 + 512],
                                    cx.pA[:, T * 1024 + sd * 512:
                                          T * 1024 + sd * 512 + 512], 1.0,
                                    wk[:, W_MASK + mi * 512:
                                       W_MASK + (mi + 1) * 512],
                                    ALU.mult, ALU.add)
                            v.drain()
                            v.tensor_reduce(ngm, simsF, AXX, ALU.max,
                                            negate=True)
                        p.wait("vector", mm, val=ftscnt[T])
                        p.wait("vector", evs, val=scnt[T])
                        p.v(fvm)
                        p.di("vector", evm)
                        evmcnt[T] = evm.n
                    for T in range(nT):
                        if h + 1 < HEADS:
                            em_fts(T, h + 1)
                    for T in range(nT):
                        simsF = simsFb[T]
                        pexp = pexpb[T]
                        ngm = cx.stc[:, 270 + T:271 + T]
                        Zch = cx.stc[:, 272 + T:273 + T]

                        def fse(s_, simsF=simsF, pexp=pexp, ngm=ngm,
                                Zch=Zch):
                            s_.activation(pexp, simsF, AF.Exp, bias=ngm,
                                          scale=1.0, accum_out=Zch)
                        p.wait("scalar", evm, val=evmcnt[T])
                        p.wait("scalar", mm, val=tpcnt[T])
                        p.wait("scalar", evz, val=zcnt[T])
                        p.s(fse)
                        p.di("scalar", evs)
                        scnt[T] = evs.n
                    for T in range(nT):
                        pexp = pexpb[T]
                        Zch = cx.stc[:, 272 + T:273 + T]
                        rzh = cx.stc[:, 274 + T:275 + T]

                        def fvz2(v, pexp=pexp, Zch=Zch, rzh=rzh):
                            v.reciprocal(rzh, Zch)
                            v.drain()
                            v.tensor_scalar_mul(pexp, pexp, rzh)
                        p.wait("vector", evs, val=scnt[T])
                        p.v(fvz2)
                        p.di("vector", evz)
                        zcnt[T] = evz.n
                    for T in range(nT):
                        pexp = pexpb[T]
                        pT_ = pTb[T]

                        def ftp(t, pexp=pexp):
                            for s in range(8):
                                i = t.transpose(
                                    cx.tp[:, s * 128:(s + 1) * 128],
                                    pexp[:, s * 128:(s + 1) * 128],
                                    cx.identb[:, :])
                            i.then_inc(mm.h, 1)
                        p.wait("tensor", evz, val=zcnt[T])
                        p.wait("tensor", evt)
                        p.t(ftp)
                        mm.n += 1
                        tpcnt[T] = mm.n

                        def fvp(v, pT_=pT_):
                            v.tensor_copy(pT_, cx.tp[:, 0:1024])
                        p.wait("vector", mm, val=tpcnt[T])
                        p.v(fvp)
                        p.di("vector", evt)
                        evtcnt[T] = evt.n
                    for T in range(nT):
                        qcol = tiles[T][0]
                        pT_ = pTb[T]
                        obank = cx.pB[0:64, T * 512:T * 512 + 128]

                        def fto(t, obank=obank, h=h, pT_=pT_):
                            for js in range(8):
                                i = t.matmul(
                                    obank,
                                    wk[:, W_VBQ + js * 512 + h * 64:
                                       W_VBQ + js * 512 + h * 64 + 64],
                                    pT_[:, js * 128:(js + 1) * 128],
                                    start=(js == 0), stop=(js == 7))
                            i.then_inc(mm.h, 1)
                        p.wait("tensor", evt, val=evtcnt[T])
                        p.wait("tensor", evo, val=evocnt[T])
                        p.t(fto)
                        mm.n += 1
                        ftocnt[T] = mm.n

                        def fvo(v, obank=obank, sl=sl, hb=hb, qcol=qcol):
                            v.tensor_copy(
                                wk[hb:hb + 64, W_OT + sl * 1152 + qcol:
                                   W_OT + sl * 1152 + qcol + 128], obank)
                        p.wait("vector", mm, val=ftocnt[T])
                        p.v(fvo)
                        p.di("vector", evo)
                        evocnt[T] = evo.n

            qts = list(range(nqt))
            pairs = ([(qts[0], qts[1]), (qts[2], qts[3])] if nqt == 4
                     else [(qts[0],)])
            for pair in pairs:
                tiles = []
                for qt in pair:
                    qg = qstart + qt * 128
                    tiles.append((qg + 128, ((qg - w0) // 128) * 2))
                run_tiles(tiles)
        p.wait("tensor", evo)
        p.barrier(("sync", "gpsimd", "vector", "scalar", "tensor"))
        for kt in range(4):
            p.dma("gpsimd", cx.wB[:, kt * 1024:(kt + 1) * 1024],
                  cx.io["wo"][l][kt * 128:(kt + 1) * 128, :], ldw)
        p.wait("tensor", ldw)
        nranges = [(1, 513), (513, 1025), (1025, 1153)]
        evE = p.sem("evEf%d" % l)
        for (c0, c1) in nranges:
            Wn = c1 - c0
            for mt in range(NCT):
                dst = cx.pB[:, (mt % 2) * 512:(mt % 2) * 512 + Wn]
                p.lag_wait("tensor", evE, 1)

                def ftw(t, dst=dst, mt=mt, c0=c0, Wn=Wn):
                    for kt in range(4):
                        i = t.matmul(
                            dst,
                            cx.wB[:, kt * 1024 + mt * 128:
                                  kt * 1024 + mt * 128 + 128],
                            wk[:, W_OT + kt * 1152 + c0 - 1:
                               W_OT + kt * 1152 + c0 - 1 + Wn],
                            start=(kt == 0), stop=(kt == 3))
                    i.then_inc(mm.h, 1)
                p.t(ftw)
                mm.n += 1
                p.wait("vector", mm)
                zc = c0 + 897
                xm = cx.xres[:, mt * XMW + c0: mt * XMW + c1]

                def fvw(v, dst=dst, xm=xm, mt=mt, zc=zc, Wn=Wn):
                    v.tensor_tensor(xm, dst,
                                    cx.big16[:, mt * ZW + zc:
                                             mt * ZW + zc + Wn], ALU.add)
                p.v(fvw)
                p.di("vector", evE)
        p.wait("tensor", evE)
        p.barrier(("vector", "scalar", "tensor", "sync", "gpsimd"))
        mmL = p.sem("mmLf%d" % l)
        evL = p.sem("evLf%d" % l)
        for th in range(2):
            c0 = 129 + th * 512
            for ct in range(NCT):
                sh = 0 if ct < 4 else -1
                xs = cx.xres[:, ct * XMW + c0 + sh: ct * XMW + c0 + sh + 512]
                xbt = cx.wA[:, ct * 512:(ct + 1) * 512]
                sqt = cx.wA[:, 4096 + ct * 512: 4096 + (ct + 1) * 512]

                def fv4(v, xs=xs, xbt=xbt):
                    v.tensor_copy(xbt, xs)

                def fs4(s_, xs=xs, sqt=sqt):
                    s_.activation(sqt, xs, AF.Square, bias=0.0, scale=1.0)
                p.v(fv4)
                p.s(fs4)
            p.di("vector", evL)
            p.di("scalar", evL)
            p.wait("tensor", evL)
            for ct in range(NCT):
                xbt = cx.wA[:, ct * 512:(ct + 1) * 512]
                sqt = cx.wA[:, 4096 + ct * 512: 4096 + (ct + 1) * 512]
                first, last = ct == 0, ct == NCT - 1

                def ft4(t, xbt=xbt, sqt=sqt, first=first, last=last):
                    t.matmul(cx.pB[0:1, 0:512], cx.ones_col_b[:, :], xbt,
                             start=first, stop=last)
                    i = t.matmul(cx.pB[0:1, 512:1024], cx.ones_col_b[:, :],
                                 sqt, start=first, stop=last)
                    if last:
                        i.then_inc(mmL.h, 1)
                p.t(ft4)
            mmL.n += 1
            p.wait("vector", mmL)
            r0, r1, r2 = (cx.rows[:, 0:512], cx.rows[:, 512:1024],
                          cx.rrow[:, 0:512])

            def f5(v, r0=r0, r1=r1, r2=r2):
                v.tensor_scalar_mul(r0, cx.pB[0:1, 0:512], 1.0 / 1024.0)
                v.drain()
                v.tensor_mul(r2, r0, r0)
                v.tensor_scalar_mul(r1, cx.pB[0:1, 512:1024], 1.0 / 1024.0)
                v.drain()
                v.tensor_sub(r1, r1, r2)
            p.v(f5)
            p.di("vector", evL)
            p.wait("scalar", evL)

            def f6(s_, r1=r1):
                s_.activation(r1, r1, AF.Ln, bias=1e-5, scale=1.0)
                s_.drain()
                s_.activation(r1, r1, AF.Exp, bias=0.0, scale=-0.5)
            p.s(f6)
            p.di("scalar", evL)
            p.wait("vector", evL)

            def f7(v, r0=r0, r1=r1, r2=r2):
                v.tensor_mul(r2, r0, r1)
                v.drain()
                v.tensor_scalar_mul(r2, r2, -1.0)
            p.v(f7)
            p.di("vector", evL)
            p.wait("tensor", evL)

            def f8(t, r1=r1, r2=r2):
                t.matmul(cx.pB[:, 0:512], cx.ones_row[:, :], r1,
                         start=True, stop=True)
                t.matmul(cx.pB[:, 512:1024], cx.ones_row[:, :], r2,
                         start=True, stop=True).then_inc(mmL.h, 1)
            p.t(f8)
            mmL.n += 1
            p.wait("vector", mmL)
            for ct in range(NCT):
                sh = 0 if ct < 4 else -1
                xs = cx.xres[:, ct * XMW + c0 + sh: ct * XMW + c0 + sh + 512]
                tmp = cx.scr[:, 0:512]
                xlt = wk[:, W_XLN + ct * 1024 + th * 512:
                         W_XLN + ct * 1024 + th * 512 + 512]

                def f9(v, xs=xs, tmp=tmp, xlt=xlt, ct=ct):
                    v.tensor_mul(tmp, xs, cx.pB[:, 0:512])
                    v.drain()
                    v.tensor_add(tmp, tmp, cx.pB[:, 512:1024])
                    v.drain()
                    v.tensor_scalar(xlt, tmp, cx.stc[:, 200 + ct:201 + ct],
                                    cx.stc[:, 208 + ct:209 + ct],
                                    ALU.mult, ALU.add)
                p.v(f9)
            p.di("vector", evL)
            p.wait("tensor", evL)
        p.barrier(("vector", "scalar", "tensor", "sync", "gpsimd"))
        mmG = p.sem("mmGf%d" % l)
        evG = p.sem("evGf%d" % l)
        ldG = p.sem("ldGf%d" % l)
        ldG2 = p.sem("ldG2f%d" % l)
        for th in range(2):
            for m in range(44):
                qeng, qsem = "gpsimd", ldG
                p.lag_wait(qeng, mmG, 3)
                w1t = cx.io["w1p"].tensor
                src_ap = AP(w1t, l * 1024 * 5632 + m * 128,
                            [[5632, 128], [128 * 5632, 8], [1, 128]])
                p.dma(qeng,
                      cx.wB[:, (m % 4) * 1024:(m % 4) * 1024 + 1024].rearrange(
                          "p (kt c) -> p kt c", c=128),
                      src_ap, qsem)
                p.wait("tensor", qsem)
                dst = cx.pB[:, (m % 2) * 512:(m % 2) * 512 + 512]
                p.lag_wait("tensor", evG, 1)

                def fth(t, dst=dst, m=m, th=th):
                    for kt in range(NCT):
                        i = t.matmul(
                            dst,
                            cx.wB[:, (m % 4) * 1024 + kt * 128:
                                  (m % 4) * 1024 + kt * 128 + 128],
                            wk[:, W_XLN + kt * 1024 + th * 512:
                               W_XLN + kt * 1024 + th * 512 + 512],
                            start=(kt == 0), stop=(kt == NCT - 1))
                    i.then_inc(mmG.h, 1)
                p.t(fth)
                mmG.n += 1
                p.wait("scalar", mmG)
                b1c = cx.stc[:, 224 + m:225 + m]
                if m < 22:
                    hglt = wk[:, W_HGL + m * 512:W_HGL + (m + 1) * 512]

                    def fsh(s_, dst=dst, hglt=hglt, b1c=b1c):
                        s_.activation(hglt, dst, AF.Identity, bias=b1c,
                                      scale=1.0)
                    p.s(fsh)
                    p.di("scalar", evG)
                else:
                    gel = wk[:, W_GEL:W_GEL + 512]
                    hglt = wk[:, W_HGL + (m - 22) * 512:
                              W_HGL + (m - 21) * 512]

                    def fsg(s_, dst=dst, gel=gel, b1c=b1c):
                        s_.activation(gel, dst, AF.Gelu, bias=b1c, scale=1.0)
                    p.s(fsg)
                    p.di("scalar", evG)
                    p.wait("vector", evG)

                    def fvgl(v, hglt=hglt, gel=gel):
                        v.tensor_tensor(hglt, hglt, gel, ALU.mult)
                        v.drain()
                    p.v(fvgl)
                    p.di("vector", evG)
            p.wait("tensor", evG)
            for half in range(2):
                for kt in range(22):
                    qeng, qsem = "gpsimd", ldG
                    p.lag_wait(qeng, mmG, 3)
                    p.dma(qeng,
                          cx.wB[:, (kt % 4) * 1024:(kt % 4) * 1024 + 512],
                          cx.io["w2p"][l][kt * 128:(kt + 1) * 128,
                                          half * 512:(half + 1) * 512], qsem)
                    p.wait("tensor", qsem)

                    def ftw2(t, kt=kt, half=half, th=th):
                        for mtl in range(4):
                            i = t.matmul(
                                cx.pA[:, mtl * 512:(mtl + 1) * 512],
                                cx.wB[:, (kt % 4) * 1024 + mtl * 128:
                                      (kt % 4) * 1024 + mtl * 128 + 128],
                                wk[:, W_HGL + kt * 512:W_HGL + (kt + 1) * 512],
                                start=(kt == 0), stop=(kt == 21))
                        i.then_inc(mmG.h, 1)
                    p.t(ftw2)
                    mmG.n += 1
                p.wait("vector", mmG)
                for mtl in range(4):
                    mtg = half * 4 + mtl
                    tmp = cx.scr[:, mtl * 512: mtl * 512 + 512]
                    b2c = cx.stc[:, 216 + mtg:217 + mtg]
                    xm = cx.xres[:, mtg * XMW + 129 + th * 512:
                                 mtg * XMW + 129 + th * 512 + 512]

                    def fvy(v, tmp=tmp, mtl=mtl, b2c=b2c, xm=xm):
                        v.scalar_tensor_tensor(
                            tmp, cx.pA[:, mtl * 512:(mtl + 1) * 512], b2c,
                            xm, ALU.add, ALU.add)
                        v.drain()
                    p.v(fvy)
                p.di("vector", evG)
                p.wait("tensor", evG)
                p.wait("sync", evG)
                p.wait("gpsimd", evG)
                for mtl in range(4):
                    mtg = half * 4 + mtl
                    tmp = cx.scr[:, mtl * 512: mtl * 512 + 512]
                    if l == 0:
                        p.dma("gpsimd",
                              yout[mtg * 128:(mtg + 1) * 128,
                                   coff + th * 512: coff + th * 512 + 512],
                              tmp, yst)
                        if ylout is not None and th == 1:
                            p.dma("gpsimd",
                                  AP(ylout.tensor, mtg * 256 + ci,
                                     [[2, 128], [1, 1]]),
                                  tmp[:, 511:512], yst, slow_ok=True)
                    else:
                        for piece in range(2):
                            p.dma("sync", cx.scr[:, 2048:2304],
                                  x2a[mtg * 128:(mtg + 1) * 128,
                                      coff + th * 512 + piece * 256:
                                      coff + th * 512 + piece * 256 + 256],
                                  ldG)
                            p.wait("vector", ldG)

                            def fvf(v, tmp=tmp, piece=piece):
                                v.tensor_add(
                                    tmp[:, piece * 256:(piece + 1) * 256],
                                    tmp[:, piece * 256:(piece + 1) * 256],
                                    cx.scr[:, 2048:2304])
                                v.drain()
                            p.v(fvf)
                            p.di("vector", evG)
                            p.wait("sync", evG)
                            p.dma("sync",
                                  fout[mtg * 128:(mtg + 1) * 128,
                                       coff + th * 512 + piece * 256:
                                       coff + th * 512 + piece * 256 + 256],
                                  tmp[:, piece * 256:(piece + 1) * 256], yst)
                if l == 0:
                    p.wait("gpsimd", yst)
                else:
                    p.wait("sync", yst)
                p.barrier(("vector", "sync", "gpsimd", "tensor"))
    p.wait("sync", yst)
    p.wait("gpsimd", yst)
    p.barrier()


def build_launch3(core_in):
    p = Prog()
    nc = p.nc
    outs = [("X2A", [1024, TPC], F32), ("Y1", [1024, TPC], BF16)]
    io = _declare_io(nc, core_in, outs)
    cx = setup_common(p, io, big16_cols=NCT * ZW, xn_cols=1,
                      rows_cols=1024, scr_cols=3072, stc_cols=384,
                      itab_cols=8)
    cx.work16 = nc.alloc_sbuf_tensor("work16", [128, 27648], BF16)
    cx.qksb = nc.alloc_sbuf_tensor("qksb", [128, 1024], BF16)
    cx.gtab = nc.alloc_sbuf_tensor("gtab", [128, 70], mybir.dt.int32)
    vscr = nc.dram_tensor("vscr", [2048, 512], BF16).ap()
    p.open()
    ph_init(p, cx)
    ld = p.sem("ld_x")
    p.dma("sync", cx.gtab[:, :], io["l1g"], ld)
    for ct in range(NCT):
        p.dma("sync", cx.xres[:, ct * TPC:(ct + 1) * TPC],
              cx.io["X1"][ct * 128:(ct + 1) * 128, :], ld)
    for e in ("vector", "scalar", "tensor", "gpsimd"):
        p.wait(e, ld)
    p.barrier()
    ph_l1_apply(p, cx, io["LBUF"], io["PBUF"], io["SUMGX"], cx.gtab)
    sts = p.sem("st3")
    for ct in range(NCT):
        p.dma("sync", io["X2A"][ct * 128:(ct + 1) * 128, :],
              cx.xres[:, ct * TPC:(ct + 1) * TPC], sts)
    p.wait("sync", sts)
    p.barrier()
    ph_l1ffn_layer(p, cx, 0, io["XN1G2"], io["XN1L"], cx.gtab, vscr,
                   io["maskx"], yout=io["Y1"])
    p.barrier()
    return p.close()


def build_launch4(core_in):
    p = Prog()
    nc = p.nc
    outs = [("OUT", [1024, TPC], F32)]
    io = _declare_io(nc, core_in, outs)
    cx = setup_common(p, io, big16_cols=NCT * ZW, xn_cols=1,
                      rows_cols=1024, scr_cols=3072, stc_cols=384,
                      itab_cols=8)
    cx.work16 = nc.alloc_sbuf_tensor("work16", [128, 27648], BF16)
    cx.qksb = nc.alloc_sbuf_tensor("qksb", [128, 1024], BF16)
    cx.gtab = nc.alloc_sbuf_tensor("gtab", [128, 70], mybir.dt.int32)
    vscr = nc.dram_tensor("vscr", [2048, 512], BF16).ap()
    p.open()
    ph_init(p, cx)
    ld = p.sem("ld_g4")
    p.dma("sync", cx.gtab[:, :], io["l1g"], ld)
    for e in ("vector", "scalar", "tensor", "gpsimd"):
        p.wait(e, ld)
    p.barrier()
    ph_l1ffn_layer(p, cx, 1, io["Y1G2"], io["Y1L"], cx.gtab, vscr,
                   io["maskx"], x2a=io["X2A"], fout=io["OUT"])
    p.barrier()
    return p.close()



def _double_rows(zg):
    import ml_dtypes
    d = zg.reshape(16384, 1024)
    return np.ascontiguousarray(
        np.concatenate([d, np.zeros((128, 1024), d.dtype)], 0))


def _sum_gather(sums):
    ident = np.tile(np.array([1.0, 0.0], np.float32), 16)[None, :]
    return np.ascontiguousarray(np.concatenate(
        sums + [np.broadcast_to(ident, (128, 32))], 0).astype(np.float32))



def kernel(**inputs):
    return kernel_fused(**inputs)


def kernel_multilaunch(**inputs):
    import ml_dtypes
    per_core = prep_inputs(inputs)
    maskx = _l1ffn_maskx().astype(ml_dtypes.bfloat16)
    for c in range(NC):
        per_core[c]["l0g"] = l0_gtab(c)
        per_core[c]["l1g"] = l1_gtab(c)
        per_core[c]["maskx"] = maskx
        per_core[c].update(l1_rots(c))
    nc1 = build_launch1(per_core[0])
    r1 = run_bass_kernel_spmd(nc1, per_core, core_ids=list(range(NC))).results
    KG = np.concatenate([r1[c]["K_OWN"] for c in range(NC)], 0)
    VG = np.concatenate([r1[c]["V_OWN"] for c in range(NC)], 0)
    KGX = np.concatenate([KG, np.zeros((128, 2048), ml_dtypes.bfloat16)], 0)
    KGX2 = np.ascontiguousarray(KGX.reshape(2304, 1024))
    VGX = np.ascontiguousarray(np.concatenate(
        [VG, np.zeros((128, 1024), ml_dtypes.bfloat16)], 0))

    m2 = []
    for c in range(NC):
        m = dict(per_core[c])
        m["KGX2"] = KGX2
        m["VGX"] = VGX
        m["Q_OWN"] = r1[c]["Q_OWN"]
        m["X1A"] = r1[c]["X1A"]
        m2.append(m)
    nc2 = build_launch2(m2[0])
    r2 = run_bass_kernel_spmd(nc2, m2, core_ids=list(range(NC))).results
    XN1G2 = _double_rows(np.concatenate([r2[c]["XN1"] for c in range(NC)], 0))
    SUMGX = _sum_gather([r2[c]["SUM"] for c in range(NC)])

    m3 = []
    for c in range(NC):
        m = dict(per_core[c])
        m["X1"] = r2[c]["X1"]
        m["LBUF"] = r2[c]["LBUF"]
        m["PBUF"] = r2[c]["PBUF"]
        m["SUMGX"] = SUMGX
        m["XN1G2"] = XN1G2
        m["XN1L"] = np.ascontiguousarray(XN1G2[:, 1023:1024])
        m3.append(m)
    nc3 = build_launch3(m3[0])
    r3 = run_bass_kernel_spmd(nc3, m3, core_ids=list(range(NC))).results
    Y1G2 = _double_rows(np.concatenate([r3[c]["Y1"] for c in range(NC)], 0))

    m4 = []
    for c in range(NC):
        m = dict(per_core[c])
        m["Y1G2"] = Y1G2
        m["Y1L"] = np.ascontiguousarray(Y1G2[:, 1023:1024])
        m["X2A"] = r3[c]["X2A"]
        m4.append(m)
    nc4 = build_launch4(m4[0])
    r4 = run_bass_kernel_spmd(nc4, m4, core_ids=list(range(NC))).results
    full = np.zeros((B, S, D), np.float32)
    for c in range(NC):
        out = r4[c]["OUT"]
        for (b, k, off) in chunks_of_core(c):
            full[b, k * CHUNK:(k + 1) * CHUNK] = out[:, off:off + CHUNK].T
    return full



def setup_fused(p: Prog, io):
    nc = p.nc
    cx = Ctx()
    A = nc.alloc_sbuf_tensor
    p.constpool = A("constpool", [128, len(CONSTS)], F32)
    for i, val in enumerate(CONSTS):
        nc.const_aps.register(F32, float(val), p.constpool[:, i:i + 1])
    cx.xres = A("xres", [128, NCT * TPC], F32)
    cx.wA = A("wA", [128, 12288], BF16)
    cx.wB = A("wB", [128, 4096], BF16)
    cx.scr = A("scr", [128, 3072], F32)
    cx.stc = A("stc", [128, 512], F32)
    rows = A("rows", [1, 2560], F32)
    cx.rows = rows
    cx.rrow = rows[:, 1024:2048]
    cx.ones_col = A("ones_col", [128, 1], F32)
    cx.ones_row = A("ones_row", [1, 128], F32)
    cx.ones_col_b = A("ones_col_b", [128, 1], BF16)
    cx.ones_row_b = A("ones_row_b", [1, 128], BF16)
    cx.identb = A("identb", [128, 128], BF16)
    cx.gtab_l1 = A("gtab_l1", [128, 70], mybir.dt.int32)
    ov = A("overlay", [128, 22080], F32)
    cx.identf = ov[:, 20188:20316]
    ovb = ov.bitcast(BF16)
    ovi = ov.bitcast(mybir.dt.int32)
    cx.xn = ovb[:, 0:16384]
    cx.big16_l0 = ovb[:, 16384:32768]
    cx.l0q = ovb[:, 32768:34816]
    cx.l0k = ovb[:, 34816:36864]
    cx.gcm = ovb[:, 36864:38912]
    cx.gtab_l0 = ovi[:, 19456:20188]
    cx.abuf = ov[:, 8192:10240]
    cx.attn_bring = [ov[:, 12288 + i * ICH: 12288 + (i + 1) * ICH]
                     for i in range(4)]
    cx.gbuf = ov[:, 10240:12288]
    cx.work16 = ovb[:, 0:27648]
    cx.big16_z = ovb[:, 27648:44160]
    cx.qksb = cx.xres.bitcast(BF16)[:, 24576:25600]
    cx.pA = nc.alloc_psum_tensor("pA", [128, 2048], F32)
    cx.pB = nc.alloc_psum_tensor("pB", [128, 1536], F32)
    cx.tp = nc.alloc_psum_tensor("tp", [128, 1024], BF16)
    cx.io = io
    return cx


def build_fused(core_in):
    p = Prog()
    nc = p.nc
    outs = [("OUT", [1024, TPC], F32)]
    io = _declare_io(nc, core_in, outs)
    cx = setup_fused(p, io)
    D = nc.dram_tensor
    kin = D("kin", [256, 1024], BF16).ap()
    vin = D("vin", [16 * 128, 1024], BF16).ap()
    kgx2 = D("KGX2i", [2304, 1024], BF16).ap()
    vgx = D("VGXi", [16512, 1024], BF16).ap()
    btiles = D("btiles", [4 * 128, ICH], F32).ap()
    xn1_in = D("xn1_in", [2048, 1024], BF16).ap()
    xn1l_in = D("xn1l_in", [2048, 1], BF16).ap()
    xn1g2 = D("XN1G2i", [16512, 1024], BF16).ap()
    xn1lw = D("XN1Li", [16512, 1], BF16).ap()
    sum_in = D("sum_in", [128, 32], F32).ap()
    sumgx = D("SUMGXi", [1152, 32], F32).ap()
    lbuf = D("lbufi", [1024, TPC], F32).ap()
    pbuf = D("pbufi", [1024, TPC], F32).ap()
    x2a_d = D("x2a_d", [1024, TPC], F32).ap()
    y1_in = D("y1_in", [2048, 1024], BF16).ap()
    y1l_in = D("y1l_in", [2048, 1], BF16).ap()
    y1g2 = D("Y1G2i", [16512, 1024], BF16).ap()
    y1lw = D("Y1Li", [16512, 1], BF16).ap()
    vscr = D("vscr", [2048, 512], BF16).ap()
    p.open()
    cx.big16 = cx.big16_l0
    cx.gtab = cx.gtab_l0
    ph_init(p, cx)
    zp = p.sem("zp")

    def fz(v):
        v.memset(cx.xn[:, 0:1024], 0.0)
        sv = cx.scr[:, 0:32].rearrange("p (a b) -> p a b", b=2)
        v.memset(sv[:, :, 0:1], 1.0)
        v.memset(sv[:, :, 1:2], 0.0)
    p.v(fz)
    p.di("vector", zp)
    p.wait("sync", zp)
    p.dma("sync", kgx2[2048:2176, :], cx.xn[:, 0:1024], zp)
    p.dma("sync", kgx2[2176:2304, :], cx.xn[:, 0:1024], zp)
    p.dma("sync", vgx[16384:16512, :], cx.xn[:, 0:1024], zp)
    p.dma("sync", xn1g2[16384:16512, :], cx.xn[:, 0:1024], zp)
    p.dma("sync", y1g2[16384:16512, :], cx.xn[:, 0:1024], zp)
    p.dma("sync", xn1lw[16384:16512, :], cx.xn[:, 0:1], zp, slow_ok=True)
    p.dma("sync", y1lw[16384:16512, :], cx.xn[:, 0:1], zp, slow_ok=True)
    p.dma("sync", sumgx[1024:1152, :], cx.scr[:, 0:32], zp)
    p.wait("sync", zp)
    p.barrier()
    ph_load_x(p, cx)
    ph_layernorm(p, cx, "ln0")
    ph_l0_qkv(p, cx)
    sts = p.sem("stf")
    kv = AP(kin.tensor, 0, [[2048, 128], [1, 2048]])
    p.dma("sync", kv, cx.l0k[:, :], sts)
    for tt in range(16):
        p.dma("sync", vin[tt * 128:(tt + 1) * 128, :],
              cx.vtm[:, tt * 1024:(tt + 1) * 1024], sts)
    p.wait("sync", sts)
    p.barrier()
    cc1 = p.sem("cc1")

    def fc1(g):
        g.collective_compute("AllGather", ALU.bypass,
                             replica_groups=[list(range(NC))],
                             ins=[kin.opt()], outs=[kgx2[0:2048, :].opt()]
                             ).then_inc(cc1.h, 1)
        g.collective_compute("AllGather", ALU.bypass,
                             replica_groups=[list(range(NC))],
                             ins=[vin.opt()], outs=[vgx[0:16384, :].opt()]
                             ).then_inc(cc1.h, 1)
    p.g(fc1)
    cc1.n += 2
    ph_l0_gates(p, cx)
    ph_l0_moe(p, cx)
    p.wait("gpsimd", cc1)
    p.barrier()
    ph_l0_attn_u(p, cx, kgx2, vgx, btiles, io["l0g"])
    ph_layernorm(p, cx, "ln1")
    xv = AP(xn1_in.tensor, 0, [[2048, 1024], [1, 2048]])
    for ct in range(NCT):
        p.dma("sync", xv[ct * 128:(ct + 1) * 128, :],
              cx.xn[:, ct * TPC:(ct + 1) * TPC], sts)
    for ct in range(NCT):
        for sl in range(2):
            p.dma("sync",
                  AP(xn1l_in.tensor, ct * 256 + sl, [[2, 128], [1, 1]]),
                  cx.xn[:, ct * TPC + sl * 1024 + 1023:
                        ct * TPC + sl * 1024 + 1024], sts, slow_ok=True)
    p.wait("sync", sts)
    p.wait("gpsimd", sts)
    cc2 = p.sem("cc2")

    def fc2a(g):
        g.collective_compute("AllGather", ALU.bypass,
                             replica_groups=[list(range(NC))],
                             ins=[xn1_in.opt()],
                             outs=[xn1g2[0:16384, :].opt()]).then_inc(cc2.h, 1)
        g.collective_compute("AllGather", ALU.bypass,
                             replica_groups=[list(range(NC))],
                             ins=[xn1l_in.opt()],
                             outs=[xn1lw[0:16384, :].opt()]).then_inc(cc2.h, 1)
    p.g(fc2a)
    cc2.n += 2
    ph_l1_proj_scan(p, cx, lbuf, pbuf, sum_in)

    def fc2b(g):
        g.collective_compute("AllGather", ALU.bypass,
                             replica_groups=[list(range(NC))],
                             ins=[sum_in.opt()],
                             outs=[sumgx[0:1024, :].opt()]).then_inc(cc2.h, 1)
    p.g(fc2b)
    cc2.n += 1
    p.wait("gpsimd", cc2)
    p.barrier()
    cx.big16 = cx.big16_z
    cx.gtab = cx.gtab_l1
    gl = p.sem("gl1")
    p.dma("sync", cx.gtab[:, :], io["l1g"], gl)
    for e in ("vector", "scalar", "tensor", "gpsimd"):
        p.wait(e, gl)
    ph_l1_apply(p, cx, lbuf, pbuf, sumgx, cx.gtab)
    for ct in range(NCT):
        p.dma("sync", x2a_d[ct * 128:(ct + 1) * 128, :],
              cx.xres[:, ct * TPC:(ct + 1) * TPC], sts)
    for e in ("vector", "scalar", "tensor"):
        p.wait(e, sts)
    yv = AP(y1_in.tensor, 0, [[2048, 1024], [1, 2048]])
    ph_l1ffn_layer(p, cx, 0, xn1g2, xn1lw, cx.gtab, vscr, io["maskx"],
                   yout=yv, ylout=y1l_in)
    cc3 = p.sem("cc3")

    def fc3(g):
        g.collective_compute("AllGather", ALU.bypass,
                             replica_groups=[list(range(NC))],
                             ins=[y1_in.opt()],
                             outs=[y1g2[0:16384, :].opt()]).then_inc(cc3.h, 1)
        g.collective_compute("AllGather", ALU.bypass,
                             replica_groups=[list(range(NC))],
                             ins=[y1l_in.opt()],
                             outs=[y1lw[0:16384, :].opt()]).then_inc(cc3.h, 1)
    p.g(fc3)
    cc3.n += 2
    p.wait("gpsimd", cc3)
    p.barrier()
    ph_l1ffn_layer(p, cx, 1, y1g2, y1lw, cx.gtab, vscr, io["maskx"],
                   x2a=x2a_d, fout=io["OUT"])
    p.barrier()
    return p.close()


def kernel_fused(**inputs):
    import ml_dtypes
    per_core = prep_inputs(inputs)
    maskx = _l1ffn_maskx().astype(ml_dtypes.bfloat16)
    for c in range(NC):
        per_core[c]["l0g"] = l0_gtab(c)
        per_core[c]["l1g"] = l1_gtab(c)
        per_core[c]["maskx"] = maskx
        per_core[c].update(l1_rots(c))
    ncf = build_fused(per_core[0])
    rr = run_bass_kernel_spmd(ncf, per_core, core_ids=list(range(NC))).results
    full = np.zeros((B, S, D), np.float32)
    for c in range(NC):
        out = rr[c]["OUT"]
        for (b, k, off) in chunks_of_core(c):
            full[b, k * CHUNK:(k + 1) * CHUNK] = out[:, off:off + CHUNK].T
    return full
